# version 06
# baseline (speedup 1.0000x reference)
import numpy as np
from contextlib import ExitStack
import concourse.bass as bass
import concourse.mybir as mybir
from concourse.bass_utils import run_bass_kernel_spmd

F32 = mybir.dt.float32
BF16 = mybir.dt.bfloat16
I32 = mybir.dt.int32
AF = mybir.ActivationFunctionType
ALU = mybir.AluOpType
AX = mybir.AxisListType

HID = 4096
NH = 32
S = 4096
BATCH = 4
IM = 11008
ATTN_MULT = 0.08838834764831845
CAP = 30.0
QZ = 7.5
QSTEP = 0.005
LRS = 0.02
EPS = 1e-5
TM = 2048
NTM = TM // 128
NTF = S // 128
OC = IM // 128
HC = HID // 128

_PHASES = ("ln", "kv", "flash", "gateup", "down", "wo")
_DEBUG_TAPS = ()


NLANES = 8


class Prog:

    def __init__(self):
        nc = bass.Bass("TRN2", target_bir_lowering=False, debug=False)
        self.nc = nc
        self.es = ExitStack()
        self.sem = {}
        self.cnt = {}
        for name in ("pe", "dve", "act"):
            self.sem[name] = self.es.enter_context(nc.semaphore(name))
            self.cnt[name] = 0
        self.lanes = {}
        for q in ("sy", "gp"):
            for i in range(NLANES):
                nm = f"{q}{i}"
                self.sem[nm] = self.es.enter_context(nc.semaphore(nm))
                self.cnt[nm] = 0
        self.rr = {"sy": 0, "gp": 0}
        self.lastw = {}

    def inc(self, instr, s, k=1):
        self.cnt[s] += k
        instr.then_inc(self.sem[s], k)
        return (s, self.cnt[s])

    def dma(self, eng, q, **kw):
        lane = f"{q}{self.rr[q] % NLANES}"
        self.rr[q] += 1
        if self.cnt[lane] > 0:
            eng.wait_ge(self.sem[lane], self.cnt[lane])
        self.cnt[lane] += 16
        eng.dma_start(**kw).then_inc(self.sem[lane], 16)
        return (lane, self.cnt[lane])

    def dmat(self, eng, q, out, in_):
        lane = f"{q}{self.rr[q] % NLANES}"
        self.rr[q] += 1
        if self.cnt[lane] > 0:
            eng.wait_ge(self.sem[lane], self.cnt[lane])
        self.cnt[lane] += 16
        eng.dma_start_transpose(out, in_).then_inc(self.sem[lane], 16)
        return (lane, self.cnt[lane])

    def w(self, eng, tok):
        if tok is not None:
            s, v = tok
            if v > 0 and self.lastw.get((id(eng), s), 0) < v:
                eng.wait_ge(self.sem[s], v)
                self.lastw[(id(eng), s)] = v

    def snap(self):
        return dict(self.cnt)

    def wall(self, eng, snap):
        for s, v in snap.items():
            self.w(eng, (s, v))

    def wall_all(self, snap=None):
        snap = snap or self.snap()
        nc = self.nc
        for eng in (nc.sync, nc.gpsimd, nc.tensor, nc.vector, nc.scalar):
            self.wall(eng, snap)

    def sb(self, name, shape, dtype, stack=None):
        return (stack or self.es).enter_context(self.nc.sbuf_tensor(name, shape, dtype))

    def ps(self, name, shape, dtype=F32, stack=None):
        return (stack or self.es).enter_context(self.nc.psum_tensor(name, shape, dtype))


def build_program(debug_taps=(), phases=_PHASES):
    p = Prog()
    nc = p.nc

    def dram(name, shape, dtype, kind="ExternalInput"):
        if name in debug_taps:
            kind = "ExternalOutput"
        return nc.dram_tensor(name, shape, dtype, kind=kind).ap()

    xf = dram("xf", [S, HID], F32)
    xm = dram("xm", [TM, HID], F32)
    wq = dram("wq", [HID, HID], F32)
    wk = dram("wk", [HID, 1024], F32)
    wv = dram("wv", [HID, 1024], F32)
    wo = dram("wo", [HID, HID], F32)
    gq = dram("gq", [IM, HID], I32)
    gsc = dram("gsc", [IM], F32)
    gscT = dram("gscT", [128, OC], F32)
    gA = dram("gA", [HID, 32], F32)
    gB = dram("gB", [32, IM], F32)
    uq = dram("uq", [IM, HID], I32)
    usc = dram("usc", [IM], F32)
    uscT = dram("uscT", [128, OC], F32)
    uA = dram("uA", [HID, 32], F32)
    uB = dram("uB", [32, IM], F32)
    dq = dram("dq", [HID, IM], I32)
    dsc = dram("dsc", [HID], F32)
    dscT = dram("dscT", [128, HC], F32)
    dA = dram("dA", [IM, 32], F32)
    dB = dram("dB", [32, HID], F32)
    identb = dram("identb", [128, 128], BF16)
    identf = dram("identf", [128, 128], F32)
    fmask = dram("fmask", [4, 128, 1024], BF16)
    out = dram("out", [TM, HID], F32, kind="ExternalOutput")

    def scratch(name, shape, dtype):
        kind = "ExternalOutput" if name in debug_taps else "Internal"
        return nc.dram_tensor(name, shape, dtype, kind=kind).ap()

    g_dbg = scratch("g_dbg", [IM, TM], BF16) if "g_dbg" in debug_taps else None
    u_dbg = scratch("u_dbg", [IM, TM], BF16) if "u_dbg" in debug_taps else None
    xnf_s = scratch("xnf_s", [HID, S], BF16)
    xnm_s = scratch("xnm_s", [HID, TM], BF16)
    k_s = scratch("k_s", [1024, S], BF16)
    v_s = scratch("v_s", [S, 1024], BF16)
    o_s = scratch("o_s", [HID, TM], BF16)
    h_s = scratch("h_s", [IM, TM], BF16)
    ffn_s = scratch("ffn_s", [HID, TM], F32)
    gqb_s = scratch("gqb_s", [IM, HID], BF16)
    uqb_s = scratch("uqb_s", [IM, HID], BF16)
    dqb_s = scratch("dqb_s", [HID, IM], BF16)
    bsd_s = scratch("bsd_s", [33, HID], BF16)

    identb_sb = p.sb("identb_sb", [128, 128], BF16)
    ones_col = p.sb("ones_col", [128, 1], BF16)
    ones_r1f = p.sb("ones_r1f", [1, 128], F32)
    ones_r32 = p.sb("ones_r32", [1, 32], F32)
    epsb = p.sb("epsb", [128, 1], F32)
    b30 = p.sb("b30", [128, 1], F32)
    vec33 = p.sb("vec33", [33, 1], F32)
    rhs33_g = p.sb("rhs33_g", [33, TM], BF16)
    rhs33_u = p.sb("rhs33_u", [33, TM], BF16)
    rhs33_d = p.sb("rhs33_d", [33, TM], BF16)

    p.dma(nc.sync, "sy", out=identb_sb[:], in_=identb[:])
    nc.vector.memset(ones_col[:], 1.0)
    nc.vector.memset(ones_r1f[:], 1.0)
    nc.vector.memset(ones_r32[:], 1.0)
    nc.vector.memset(epsb[:], EPS)
    nc.vector.memset(b30[:], -CAP)
    nc.vector.memset(vec33[0:32, :], 1.0)
    p.inc(nc.vector.memset(vec33[32:33, :], -QZ), "dve")
    consts_ready = p.snap()

    qcast_jobs = []
    for (srcq, dstb, nrow) in ((gq, gqb_s, IM), (uq, uqb_s, IM), (dq, dqb_s, HID)):
        step = nrow // 16
        for r0 in range(0, nrow, step):
            qcast_jobs.append((srcq, dstb, r0, r0 + step))

    def qcast_drip(n):
        for _ in range(n):
            if qcast_jobs:
                srcq, dstb, r0, r1 = qcast_jobs.pop(0)
                p.dma(nc.gpsimd, "gp", out=dstb[r0:r1, :], in_=srcq[r0:r1, :])


    def phase_lnkv(bar):
        st = ExitStack()
        ND = 2
        xbuf = [p.sb(f"p0x{i}", [128, HID], F32, st) for i in range(ND)]
        bst = [p.sb(f"p0b{i}", [128, 8, 6], F32, st) for i in range(ND)]
        bat = [p.sb(f"p0a{i}", [128, 2], F32, st) for i in range(ND)]
        lnv = [p.sb(f"p0l{i}", [128, 1], F32, st) for i in range(ND)]
        rstd = [p.sb(f"p0r{i}", [128, 1], F32, st) for i in range(ND)]
        xn_tm = [p.sb(f"p0n{i}", [128, HID], BF16, st) for i in range(ND)]
        stage = [p.sb(f"p0s{i}", [128, HC, 128], BF16, st) for i in range(ND)]
        pst = [p.ps(f"p0p{i}", [128, 512], F32, st) for i in range(2)]
        wbuf = p.sb("p1w", [128, HC, 1024], BF16, st)
        xnb = [p.sb(f"p1x{i}", [128, HC, 512], BF16, st) for i in range(2)]
        kst = [p.sb(f"p1k{i}", [128, 512], BF16, st) for i in range(2)]
        pk = [p.ps(f"p1p{i}", [128, 512], F32, st) for i in range(2)]

        p.wall_all(bar)
        wtok = p.dma(nc.gpsimd, "gp", out=wbuf[:],
                     in_=wk.rearrange("(c p) d -> p c d", p=128))

        ln_jobs = [("f", i) for i in range(NTF)] + [("m", i) for i in range(NTM)]
        store_tok = {}
        lnstate = {"jj": 0, "free_x": [None] * ND, "free_stage": [None] * ND,
                   "free_pst": [None, None], "free_stats": [None] * ND,
                   "free_xn": [None] * ND}

        def ln_emit_one():
            if not ln_jobs:
                return
            kind, i = ln_jobs.pop(0)
            src_, dst = (xf, xnf_s) if kind == "f" else (xm, xnm_s)
            dst3 = dst.rearrange("(c p) t -> p c t", p=128)
            s = lnstate
            bi = s["jj"] % ND
            p.w(nc.sync, s["free_x"][bi])
            xa = xbuf[bi]
            xrdy = p.dma(nc.sync, "sy", out=xa[:], in_=src_[i * 128:(i + 1) * 128, :])
            p.w(nc.vector, xrdy)
            p.w(nc.vector, s["free_stats"][bi])
            for ch in range(8):
                t1 = p.inc(nc.vector.bn_stats(out=bst[bi][:, ch, :],
                                              in_=xa[:, ch * 512:(ch + 1) * 512]), "dve")
            p.w(nc.vector, t1)
            t2 = p.inc(nc.vector.bn_aggr(out=bat[bi][:],
                                         in_=bst[bi][:].rearrange("p c f -> p (c f)")), "dve")
            p.w(nc.scalar, t2)
            a1 = p.inc(nc.scalar.activation(out=lnv[bi][:], in_=bat[bi][:, 1:2], func=AF.Ln,
                                            bias=epsb[:, 0:1]), "act")
            p.w(nc.scalar, a1)
            a2 = p.inc(nc.scalar.activation(out=rstd[bi][:], in_=lnv[bi][:], func=AF.Exp,
                                            scale=-0.5), "act")
            p.w(nc.vector, a2)
            p.w(nc.vector, s["free_xn"][bi])
            t3 = p.inc(nc.vector.tensor_scalar(out=xn_tm[bi][:], in0=xa[:],
                                               scalar1=bat[bi][:, 0:1],
                                               scalar2=rstd[bi][:, 0:1],
                                               op0=ALU.subtract, op1=ALU.mult), "dve")
            s["free_x"][bi] = ("dve", t3[1])
            s["free_stats"][bi] = ("dve", t3[1])
            p.w(nc.tensor, t3)
            first_ev = True
            for qb in range(8):
                pb = pst[qb % 2]
                p.w(nc.tensor, s["free_pst"][qb % 2])
                for k in range(4):
                    c = qb * 4 + k
                    mm = nc.tensor.matmul(out=pb[:, k * 128:(k + 1) * 128],
                                          lhsT=xn_tm[bi][:, c * 128:(c + 1) * 128],
                                          rhs=identb_sb[:], start=True, stop=True)
                mtok = p.inc(mm, "pe")
                p.w(nc.vector, mtok)
                if first_ev:
                    p.w(nc.vector, s["free_stage"][bi])
                    first_ev = False
                ev = p.inc(nc.vector.tensor_copy(
                    out=stage[bi][:, qb * 4:(qb + 1) * 4, :].rearrange("p c f -> p (c f)"),
                    in_=pb[:]), "dve")
                s["free_pst"][qb % 2] = ("dve", ev[1])
            s["free_xn"][bi] = ("pe", mtok[1])
            p.w(nc.sync, ev)
            sdone = p.dma(nc.sync, "sy", out=dst3[:, :, i * 128:(i + 1) * 128],
                          in_=stage[bi][:])
            s["free_stage"][bi] = sdone
            store_tok[(kind, i)] = sdone
            s["jj"] += 1

        xnf3 = xnf_s.rearrange("(c p) t -> p c t", p=128)
        kvstate = {"free_xn": [None, None], "free_kst": [None] * 2,
                   "free_pk": [None, None], "ks": 0}

        def kv_emit(kind, tb):
            s = kvstate
            bi = tb % 2
            p.w(nc.sync, s["free_xn"][bi])
            for i in range(4 * tb, 4 * tb + 4):
                p.w(nc.sync, store_tok[("f", i)])
            xtok = p.dma(nc.sync, "sy", out=xnb[bi][:],
                         in_=xnf3[:, :, tb * 512:(tb + 1) * 512])
            for dch in range(8):
                pbi = dch % 2
                pb = pk[pbi]
                p.w(nc.tensor, xtok)
                if tb == 0 and dch == 0:
                    p.w(nc.tensor, wtok)
                p.w(nc.tensor, s["free_pk"][pbi])
                if kind == 0:
                    for c in range(HC):
                        mm = nc.tensor.matmul(out=pb[:],
                                              lhsT=wbuf[:, c, dch * 128:(dch + 1) * 128],
                                              rhs=xnb[bi][:, c, :],
                                              start=(c == 0), stop=(c == HC - 1))
                else:
                    tksub, dvb = dch // 2, dch % 2
                    for c in range(HC):
                        mm = nc.tensor.matmul(out=pb[:],
                                              lhsT=xnb[bi][:, c, tksub * 128:(tksub + 1) * 128],
                                              rhs=wbuf[:, c, dvb * 512:(dvb + 1) * 512],
                                              start=(c == 0), stop=(c == HC - 1))
                mtok = p.inc(mm, "pe")
                if dch == 7:
                    s["free_xn"][bi] = ("pe", mtok[1])
                sbi = s["ks"] % 2
                p.w(nc.vector, mtok)
                p.w(nc.vector, s["free_kst"][sbi])
                ev = p.inc(nc.vector.tensor_copy(out=kst[sbi][:], in_=pb[:]), "dve")
                s["free_pk"][pbi] = ("dve", ev[1])
                p.w(nc.sync, ev)
                if kind == 0:
                    dst = k_s[dch * 128:(dch + 1) * 128, tb * 512:(tb + 1) * 512]
                else:
                    row0 = tb * 512 + tksub * 128
                    dst = v_s[row0:row0 + 128, dvb * 512:(dvb + 1) * 512]
                sd = p.dma(nc.sync, "sy", out=dst, in_=kst[sbi][:])
                s["free_kst"][sbi] = sd
                s["ks"] += 1

        for _ in range(8):
            ln_emit_one()
        for tb in range(8):
            for _ in range(5):
                ln_emit_one()
            kv_emit(0, tb)
        p.w(nc.gpsimd, ("pe", p.cnt["pe"]))
        wtok = p.dma(nc.gpsimd, "gp", out=wbuf[:],
                     in_=wv.rearrange("(c p) d -> p c d", p=128))
        kvstate["free_xn"] = [("pe", p.cnt["pe"])] * 2
        for tb in range(8):
            while ln_jobs:
                ln_emit_one()
            p.w(nc.tensor, wtok)
            kv_emit(1, tb)
        bar2 = p.snap()
        st.close()
        return bar2

    def phase_flash(bar):
        st = ExitStack()
        xnm_sb = p.sb("p3xn", [128, HC, TM], BF16, st)
        k_sb = p.sb("p3k", [128, S], BF16, st)
        v_sb = p.sb("p3v", [128, NTF, 128], BF16, st)
        q_sb = p.sb("p3q", [128, TM], BF16, st)
        wqb = [p.sb(f"p3wq{i}", [128, HC, 128], BF16, st) for i in range(2)]
        mask_sb = p.sb("p3m", [128, 4, 1024], BF16, st)
        probs = [p.sb(f"p3pr{i}", [128, 1024], BF16, st) for i in range(2)]
        tmp_t = p.sb("p3tt", [128, 1024], F32, st)
        binv = p.sb("p3bi", [128, 512], F32, st)
        o_tile = [p.sb(f"p3o{i}", [128, 512], BF16, st) for i in range(2)]
        agb = p.sb("p3ag", [128, HC, 33], BF16, st)

        psc = [p.ps(f"p3c{i}", [128, 1024], F32, st) for i in range(2)]
        pO = [p.ps(f"p3O{i}", [128, 512], F32, st) for i in range(2)]
        pr = p.ps("p3r", [1, 512], F32, st)
        pb_ = p.ps("p3b", [128, 512], F32, st)

        p.wall_all(bar)
        xnm3 = xnm_s.rearrange("(c p) t -> p c t", p=128)
        xntok = p.dma(nc.sync, "sy", out=xnm_sb[:], in_=xnm3)
        msk = p.dma(nc.sync, "sy", out=mask_sb[:],
                                      in_=fmask.rearrange("g p f -> p g f"))

        last_ev = None
        for mi, (Asrc, r33) in enumerate(((gA, rhs33_g), (uA, rhs33_u))):
            if mi > 0:
                p.w(nc.gpsimd, ("pe", p.cnt["pe"]))
            atok = p.dma(nc.gpsimd, "gp", out=agb[:, :, 0:32],
                                             in_=Asrc.rearrange("(c p) r -> p c r", p=128))
            p.w(nc.vector, atok)
            otok = p.inc(nc.vector.memset(agb[:, :, 32:33], 1.0), "dve")
            for tb in range(4):
                pp = psc[tb % 2]
                p.w(nc.tensor, xntok)
                p.w(nc.tensor, otok)
                if last_ev is not None:
                    p.w(nc.tensor, last_ev)
                for c in range(HC):
                    mm = nc.tensor.matmul(out=pp[0:33, 0:512],
                                          lhsT=agb[:, c, :],
                                          rhs=xnm_sb[:, c, tb * 512:(tb + 1) * 512],
                                          start=(c == 0), stop=(c == HC - 1))
                mtok = p.inc(mm, "pe")
                p.w(nc.scalar, mtok)
                last_ev = p.inc(nc.scalar.mul(r33[:, tb * 512:(tb + 1) * 512], pp[0:33, 0:512],
                                              vec33[:, 0:1]), "act")
        pgu_done = last_ev

        free_wq = [None, None]
        free_q = None
        free_k = None
        free_v = None
        free_probs = [None, None]
        free_psc = [("act", pgu_done[1]), ("act", pgu_done[1])]
        free_pO = [None, None]
        free_otile = [None, None]
        free_pb = None
        last_exp = None
        wq4 = wq.rearrange("(c p) o -> p c o", p=128)
        v4 = v_s.rearrange("(c p) d -> p c d", p=128)
        ktok = vtok = None
        for h in range(NH):
            kvh = h // 4
            if h % 4 == 0:
                p.w(nc.sync, free_k)
                ktok = p.dma(nc.sync, "sy", out=k_sb[:],
                                               in_=k_s[kvh * 128:(kvh + 1) * 128, :])
                p.w(nc.sync, free_v)
                vtok = p.dma(nc.sync, "sy", out=v_sb[:],
                                               in_=v4[:, :, kvh * 128:(kvh + 1) * 128])
            wbi = h % 2
            p.w(nc.gpsimd, free_wq[wbi])
            wtok = p.dma(nc.gpsimd, "gp", out=wqb[wbi][:],
                                             in_=wq4[:, :, h * 128:(h + 1) * 128])
            qcast_drip(2)
            p.w(nc.tensor, wtok)
            if h == 0:
                p.w(nc.tensor, xntok)
            for tb in range(4):
                pp = pO[tb % 2]
                p.w(nc.tensor, free_pO[tb % 2])
                if tb == 0:
                    p.w(nc.tensor, free_q)
                for c in range(HC):
                    mm = nc.tensor.matmul(out=pp[:], lhsT=wqb[wbi][:, c, :],
                                          rhs=xnm_sb[:, c, tb * 512:(tb + 1) * 512],
                                          start=(c == 0), stop=(c == HC - 1))
                mtok = p.inc(mm, "pe")
                p.w(nc.vector, mtok)
                ev = p.inc(nc.vector.tensor_copy(out=q_sb[:, tb * 512:(tb + 1) * 512], in_=pp[:]),
                           "dve")
                free_pO[tb % 2] = ("dve", ev[1])
            qready = ("dve", p.cnt["dve"])
            free_wq[wbi] = ("pe", p.cnt["pe"])

            units = [(g, u) for g in range(4) for u in range(4 * g + 4)]
            pend_pv = None
            pend_norm = None
            rtoks = {}

            def emit_pv(g, u, pbi, ptok):
                npair = 4 * g + 4
                ob = g % 2
                p.w(nc.tensor, ptok)
                if u == 0:
                    p.w(nc.tensor, free_pO[ob])
                    if h % 4 == 0 and g == 0:
                        p.w(nc.tensor, vtok)
                    if g > 0:
                        p.w(nc.tensor, rtoks.get(g - 1))
                nc.tensor.matmul(out=pO[ob][:], lhsT=v_sb[:, 2 * u, :],
                                 rhs=probs[pbi][:, 0:512], start=(u == 0), stop=False)
                nc.tensor.matmul(out=pO[ob][:], lhsT=v_sb[:, 2 * u + 1, :],
                                 rhs=probs[pbi][:, 512:1024],
                                 start=False, stop=(u == npair - 1))
                nc.tensor.matmul(out=pr[:], lhsT=ones_col[:], rhs=probs[pbi][:, 0:512],
                                 start=(u == 0), stop=False)
                rmm = nc.tensor.matmul(out=pr[:], lhsT=ones_col[:],
                                       rhs=probs[pbi][:, 512:1024],
                                       start=False, stop=(u == npair - 1))
                mtok2 = p.inc(rmm, "pe")
                free_probs[pbi] = ("pe", mtok2[1])
                if u == npair - 1:
                    p.w(nc.vector, ("pe", mtok2[1]))
                    rtoks[g] = p.inc(nc.vector.reciprocal(out=binv[0:1, :], in_=pr[0:1, :]),
                                     "dve")
                    return True
                return False

            def flush_norm(g):
                ob = g % 2
                rtok = rtoks[g]
                p.w(nc.tensor, rtok)
                p.w(nc.tensor, free_pb)
                bmm = nc.tensor.matmul(out=pb_[:], lhsT=ones_r1f[:], rhs=binv[0:1, :],
                                       start=True, stop=True)
                btok = p.inc(bmm, "pe")
                p.w(nc.scalar, btok)
                btok2 = p.inc(nc.scalar.copy(out=binv[:], in_=pb_[:]), "act")
                obi = g % 2
                p.w(nc.vector, btok2)
                p.w(nc.vector, free_otile[obi])
                otok = p.inc(nc.vector.tensor_tensor(out=o_tile[obi][:], in0=pO[ob][:],
                                                     in1=binv[:], op=ALU.mult), "dve")
                p.w(nc.sync, otok)
                sd = p.dma(nc.sync, "sy",
                           out=o_s[h * 128:(h + 1) * 128, g * 512:(g + 1) * 512],
                           in_=o_tile[obi][:])
                return ("act", btok2[1]), ("dve", otok[1]), sd

            for g, u in units:
                sbi = u % 2
                pp = psc[sbi]
                p.w(nc.tensor, free_psc[sbi])
                if u == 0 and g == 0:
                    p.w(nc.tensor, qready)
                    if h % 4 == 0:
                        p.w(nc.tensor, ktok)
                mm0 = nc.tensor.matmul(out=pp[:, 0:512],
                                       lhsT=k_sb[:, (2 * u) * 128:(2 * u + 1) * 128],
                                       rhs=q_sb[:, g * 512:(g + 1) * 512],
                                       start=True, stop=True)
                mm1 = nc.tensor.matmul(out=pp[:, 512:1024],
                                       lhsT=k_sb[:, (2 * u + 1) * 128:(2 * u + 2) * 128],
                                       rhs=q_sb[:, g * 512:(g + 1) * 512],
                                       start=True, stop=True)
                sctok = p.inc(mm1, "pe")
                if pend_pv is not None:
                    done = emit_pv(*pend_pv)
                    if done:
                        pend_norm = pend_pv[0]
                    pend_pv = None
                if pend_norm is not None and u >= 1:
                    fb, fo, fs = flush_norm(pend_norm)
                    free_pb = fb
                    free_pO[pend_norm % 2] = fo
                    free_otile[pend_norm % 2] = fs
                    pend_norm = None
                p.w(nc.scalar, sctok)
                if last_exp is not None:
                    p.w(nc.scalar, last_exp)
                t1 = p.inc(nc.scalar.activation(out=tmp_t[:], in_=pp[:], func=AF.Tanh,
                                                scale=ATTN_MULT / CAP), "act")
                free_psc[sbi] = ("act", t1[1])
                p.w(nc.scalar, t1)
                pbi = u % 2
                p.w(nc.scalar, free_probs[pbi])
                t2 = p.inc(nc.scalar.activation(out=probs[pbi][:], in_=tmp_t[:], func=AF.Exp,
                                                scale=CAP, bias=b30[:, 0:1]), "act")
                last_exp = t2
                ptok = t2
                if u >= 4 * g:
                    mp = u - 4 * g
                    p.w(nc.vector, t2)
                    if u == 0 and g == 0 and h == 0:
                        p.w(nc.vector, msk)
                    ptok = p.inc(nc.vector.tensor_tensor(out=probs[pbi][:], in0=probs[pbi][:],
                                                         in1=mask_sb[:, mp, :],
                                                         op=ALU.mult), "dve")
                pend_pv = (g, u, pbi, ptok)
            done = emit_pv(*pend_pv)
            pend_pv = None
            if pend_norm is not None:
                fb, fo, fs = flush_norm(pend_norm)
                free_pb = fb
                free_pO[pend_norm % 2] = fo
                free_otile[pend_norm % 2] = fs
            fb, fo, fs = flush_norm(3)
            free_pb = fb
            free_pO[3 % 2] = fo
            free_otile[3 % 2] = fs
            free_q = ("pe", p.cnt["pe"])
            if h % 4 == 3:
                free_k = ("pe", p.cnt["pe"])
                free_v = ("pe", p.cnt["pe"])
        qcast_drip(len(qcast_jobs))
        bar2 = p.snap()
        st.close()
        return bar2

    def phase_gateup(bar):
        st = ExitStack()
        scg = p.sb("p4scg", [128, OC], F32, st)
        scu = p.sb("p4scu", [128, OC], F32, st)
        Bs_g = p.sb("p4bsg", [33, IM], BF16, st)
        Bs_u = p.sb("p4bsu", [33, IM], BF16, st)
        srow = [p.sb(f"p4srow{i}", [1, 512], F32, st) for i in range(2)]

        p.wall_all(bar)
        t = p.dma(nc.sync, "sy", out=scg[:], in_=gscT[:])
        t2_ = p.dma(nc.sync, "sy", out=scu[:], in_=uscT[:])
        p.w(nc.vector, t2_)
        t1 = p.inc(nc.vector.tensor_scalar(out=scg[:], in0=scg[:], scalar1=QSTEP, scalar2=1e-30,
                                           op0=ALU.mult, op1=ALU.max), "dve")
        p.w(nc.vector, t1)
        t1 = p.inc(nc.vector.tensor_scalar(out=scu[:], in0=scu[:], scalar1=QSTEP, scalar2=1e-30,
                                           op0=ALU.mult, op1=ALU.max), "dve")
        scready = t1
        if "sc_dbg" in debug_taps:
            sc_dbg = scratch("sc_dbg", [128, 2 * OC], F32)
            p.w(nc.sync, scready)
            p.dma(nc.sync, "sy", out=sc_dbg[:, 0:OC], in_=scg[:])
            p.dma(nc.sync, "sy", out=sc_dbg[:, OC:], in_=scu[:])

        with ExitStack() as st2:
            gb_tmp = p.sb("p4gbt", [32, IM], F32, st2)
            pbc = p.ps("p4bc", [32, 512], F32, st2)
            pdum = p.ps("p4dum", [32, 16], F32, st2)
            for mi, (scsrc, Bsrc, Bdst) in enumerate(((gsc, gB, Bs_g), (usc, uB, Bs_u))):
                if mi > 0:
                    p.w(nc.sync, ("dve", p.cnt["dve"]))
                tb_ = p.dma(nc.sync, "sy", out=gb_tmp[:], in_=Bsrc[:])
                p.inc(nc.vector.memset(Bdst[32:33, :], 1.0), "dve")
                free_bc = None
                free_sr = [None, None]
                for ch in range((IM + 511) // 512):
                    c0, c1 = ch * 512, min((ch + 1) * 512, IM)
                    n = c1 - c0
                    cbi = ch % 2
                    p.w(nc.sync, free_sr[cbi])
                    t = p.dma(nc.sync, "sy", out=srow[cbi][:, 0:n],
                                                in_=scsrc[None, c0:c1])
                    p.w(nc.vector, t)
                    t1 = p.inc(nc.vector.tensor_scalar(out=srow[cbi][:, 0:n],
                                                       in0=srow[cbi][:, 0:n],
                                                       scalar1=QSTEP, scalar2=1e-30,
                                                       op0=ALU.mult, op1=ALU.max), "dve")
                    p.w(nc.vector, t1)
                    t2 = p.inc(nc.vector.reciprocal(out=srow[cbi][:, 0:n],
                                                    in_=srow[cbi][:, 0:n]), "dve")
                    p.w(nc.tensor, t2)
                    p.w(nc.tensor, free_bc)
                    nc.tensor.matmul(out=pbc[:, 0:n], lhsT=ones_r32[:],
                                     rhs=srow[cbi][:, 0:n], start=True, stop=True)
                    dmm = nc.tensor.matmul(out=pdum[:], lhsT=ones_r32[:],
                                           rhs=srow[cbi][:, 0:16], start=True, stop=True)
                    btok = p.inc(dmm, "pe")
                    free_sr[cbi] = ("pe", btok[1])
                    p.w(nc.vector, btok)
                    p.w(nc.vector, tb_)
                    ev = p.inc(nc.vector.scalar_tensor_tensor(
                        out=Bdst[0:32, c0:c1], in0=pbc[:, 0:n], scalar=LRS,
                        in1=gb_tmp[:, c0:c1], op0=ALU.mult, op1=ALU.mult),
                        "dve")
                    free_bc = ("dve", ev[1])
                p.wall_all()
            bsready = ("dve", p.cnt["dve"])

        if "bs_dbg" in debug_taps:
            bs_dbg = scratch("bs_dbg", [33, 2 * IM + 2 * TM], BF16)
            p.wall_all()
            p.dma(nc.sync, "sy", out=bs_dbg[:, 0:IM], in_=Bs_g[:])
            p.dma(nc.sync, "sy", out=bs_dbg[:, IM:2 * IM], in_=Bs_u[:])
            p.dma(nc.sync, "sy", out=bs_dbg[:, 2 * IM:2 * IM + TM], in_=rhs33_g[:])
            p.dma(nc.sync, "sy", out=bs_dbg[:, 2 * IM + TM:], in_=rhs33_u[:])

        bs3_dbg = scratch("bs3_dbg", [33, 4 * IM], BF16) if "bs3_dbg" in debug_taps else None

        def bsdump(idx):
            if bs3_dbg is not None:
                p.wall_all()
                p.dma(nc.sync, "sy", out=bs3_dbg[:, idx * IM:(idx + 1) * IM], in_=Bs_g[:])

        bsdump(0)
        dnA_sb = p.sb("p4dnA", [128, OC, 33], BF16, st)
        t = p.dma(nc.gpsimd, "gp", out=dnA_sb[:, :, 0:32],
                                      in_=dA.rearrange("(c p) r -> p c r", p=128))
        p.w(nc.vector, t)
        dnatok = p.inc(nc.vector.memset(dnA_sb[:, :, 32:33], 1.0), "dve")
        bsdump(1)

        xnm_h = p.sb("p4xn", [128, HC, 1024], BF16, st)
        wgT = [p.sb(f"p4wgT{i}", [128, HC, 128], BF16, st) for i in range(3)]
        wuT = [p.sb(f"p4wuT{i}", [128, HC, 128], BF16, st) for i in range(3)]
        hstage = [p.sb(f"p4h{i}", [128, 2, 512], BF16, st) for i in range(3)]
        gs_t = [p.sb(f"p4g{i}", [128, 512], BF16, st) for i in range(2)]
        us_t = [p.sb(f"p4u{i}", [128, 512], BF16, st) for i in range(2)]

        pgu = [p.ps(f"p4gu{i}", [128, 512], F32, st) for i in range(2)]
        pP33 = [p.ps(f"p4P{i}", [33, 512], F32, st) for i in range(2)]

        xnm3 = xnm_s.rearrange("(c p) t -> p c t", p=128)
        r33evac = None
        for half in range(2):
            p.wall_all()
            xtok = p.dma(nc.sync, "sy", out=xnm_h[:],
                                           in_=xnm3[:, :, half * 1024:(half + 1) * 1024])
            if half == 0:
                p.w(nc.sync, xtok)
                bsdump(2)
            free_wT = [[None] * 3, [None] * 3]
            free_h_dma = [None] * 3
            free_h_pe = [None] * 3
            free_gu = [None, None]
            free_gs = [None, None]
            free_us = [None, None]
            p33_pend = []
            wTrdy = [[None] * 3, [None] * 3]

            def tload(oc):
                obi = oc % 3
                for (srcb, wTs, mi) in ((gqb_s, wgT, 0), (uqb_s, wuT, 1)):
                    p.w(nc.sync, free_wT[mi][obi])
                    wTrdy[mi][obi] = p.dmat(
                        nc.sync, "sy", wTs[obi][:],
                        srcb[oc * 128:(oc + 1) * 128, :].rearrange("o (c h) -> o c h", h=128))

            tload(0)
            tload(1)
            for oc in range(OC):
                obi = oc % 3
                if oc + 2 < OC:
                    tload(oc + 2)
                if bs3_dbg is not None and half == 0 and oc == 1:
                    p.w(nc.sync, ("pe", p.cnt["pe"]))
                    p.dma(nc.sync, "sy", out=bs3_dbg[:, 3 * IM:4 * IM], in_=Bs_g[:])
                if "bs2_dbg" in debug_taps and half == 0 and oc == 2:
                    bs2_dbg = scratch("bs2_dbg", [33, IM + TM], BF16)
                    p.w(nc.sync, ("pe", p.cnt["pe"]))
                    p.dma(nc.sync, "sy", out=bs2_dbg[:, 0:IM], in_=Bs_g[:])
                    p.dma(nc.sync, "sy", out=bs2_dbg[:, IM:], in_=rhs33_g[:])
                if p33_pend and p33_pend[0][0] <= oc - 2:
                    ptok_ = None
                    hb2 = None
                    while p33_pend and p33_pend[0][0] <= oc - 2:
                        poc, ptb, htok2, hb2 = p33_pend.pop(0)
                        p.w(nc.tensor, htok2)
                        if poc == 0 and ptb == 0:
                            p.w(nc.tensor, dnatok)
                            if r33evac is not None:
                                p.w(nc.tensor, r33evac)
                        mmp = nc.tensor.matmul(out=pP33[ptb][0:33, :], lhsT=dnA_sb[:, poc, :],
                                               rhs=hstage[hb2][:, ptb, :],
                                               start=(poc == 0), stop=(poc == OC - 1))
                        ptok_ = p.inc(mmp, "pe")
                        free_h_pe[hb2] = ("pe", ptok_[1])
                for tb in range(2):
                    gb_ = tb % 2
                    p.w(nc.tensor, wTrdy[0][obi])
                    p.w(nc.tensor, free_gu[gb_])
                    if oc == 0:
                        p.w(nc.tensor, xtok)
                        if half == 0:
                            p.w(nc.tensor, bsready)
                    nc.tensor.matmul(out=pgu[gb_][:],
                                     lhsT=Bs_g[:, oc * 128:(oc + 1) * 128],
                                     rhs=rhs33_g[:, half * 1024 + tb * 512:
                                                 half * 1024 + (tb + 1) * 512],
                                     start=True, stop=False)
                    for c in range(HC):
                        mmg = nc.tensor.matmul(out=pgu[gb_][:], lhsT=wgT[obi][:, c, :],
                                               rhs=xnm_h[:, c, tb * 512:(tb + 1) * 512],
                                               start=False, stop=(c == HC - 1))
                    gtok = p.inc(mmg, "pe")
                    p.w(nc.scalar, gtok)
                    if oc == 0 and half == 0:
                        p.w(nc.scalar, scready)
                    p.w(nc.scalar, free_gs[gb_])
                    gevac = p.inc(nc.scalar.activation(out=gs_t[gb_][:], in_=pgu[gb_][:],
                                                       func=AF.Silu, scale=scg[:, oc:oc + 1]),
                                  "act")
                    free_gu[gb_] = ("act", gevac[1])
                    ub_ = 1 - gb_
                    p.w(nc.tensor, wTrdy[1][obi])
                    p.w(nc.tensor, free_gu[ub_])
                    for c in range(HC):
                        nc.tensor.matmul(out=pgu[ub_][:], lhsT=wuT[obi][:, c, :],
                                         rhs=xnm_h[:, c, tb * 512:(tb + 1) * 512],
                                         start=(c == 0), stop=False)
                    mmu = nc.tensor.matmul(out=pgu[ub_][:],
                                           lhsT=Bs_u[:, oc * 128:(oc + 1) * 128],
                                           rhs=rhs33_u[:, half * 1024 + tb * 512:
                                                       half * 1024 + (tb + 1) * 512],
                                           start=False, stop=True)
                    utok = p.inc(mmu, "pe")
                    p.w(nc.scalar, utok)
                    p.w(nc.scalar, gevac)
                    p.w(nc.scalar, free_us[ub_])
                    uevac = p.inc(nc.scalar.mul(us_t[ub_][:], pgu[ub_][:], scu[:, oc:oc + 1]),
                                  "act")
                    free_gu[ub_] = ("act", uevac[1])
                    if g_dbg is not None:
                        p.w(nc.sync, gevac)
                        p.dma(nc.sync, "sy", 
                            out=g_dbg[oc * 128:(oc + 1) * 128,
                                      half * 1024 + tb * 512:half * 1024 + (tb + 1) * 512],
                            in_=gs_t[gb_][:])
                    if u_dbg is not None:
                        p.w(nc.sync, uevac)
                        p.dma(nc.sync, "sy", 
                            out=u_dbg[oc * 128:(oc + 1) * 128,
                                      half * 1024 + tb * 512:half * 1024 + (tb + 1) * 512],
                            in_=us_t[ub_][:])
                    hb = oc % 3
                    p.w(nc.vector, uevac)
                    if tb == 0:
                        p.w(nc.vector, free_h_dma[hb])
                        p.w(nc.vector, free_h_pe[hb])
                    htok = p.inc(nc.vector.tensor_tensor(out=hstage[hb][:, tb, :],
                                                         in0=gs_t[gb_][:], in1=us_t[ub_][:],
                                                         op=ALU.mult), "dve")
                    free_gs[gb_] = ("dve", htok[1])
                    free_us[ub_] = ("dve", htok[1])
                    p.w(nc.sync, htok)
                    sd = p.dma(nc.sync, "sy", 
                        out=h_s[oc * 128:(oc + 1) * 128,
                                half * 1024 + tb * 512:half * 1024 + (tb + 1) * 512],
                        in_=hstage[hb][:, tb, :])
                    p33_pend.append((oc, tb, htok, hb))
                free_h_dma[oc % 3] = sd
                free_wT[0][obi] = ("pe", p.cnt["pe"])
                free_wT[1][obi] = ("pe", p.cnt["pe"])
            ptok_ = None
            for (poc, ptb, htok2, hb2) in p33_pend:
                p.w(nc.tensor, htok2)
                mmp = nc.tensor.matmul(out=pP33[ptb][0:33, :], lhsT=dnA_sb[:, poc, :],
                                       rhs=hstage[hb2][:, ptb, :],
                                       start=(poc == 0), stop=(poc == OC - 1))
                ptok_ = p.inc(mmp, "pe")
            p33_pend = []
            for tb in range(2):
                p.w(nc.scalar, ("pe", p.cnt["pe"]))
                r33evac = p.inc(nc.scalar.mul(rhs33_d[:, half * 1024 + tb * 512:
                                                      half * 1024 + (tb + 1) * 512],
                                              pP33[tb][0:33, :], vec33[:, 0:1]), "act")
        bar2 = p.snap()
        st.close()
        return bar2

    def phase_down(bar):
        st = ExitStack()
        scd = p.sb("p5scd", [128, HC], F32, st)

        p.wall_all(bar)
        t = p.dma(nc.sync, "sy", out=scd[:], in_=dscT[:])
        p.w(nc.vector, t)
        sctok = p.inc(nc.vector.tensor_scalar(out=scd[:], in0=scd[:], scalar1=QSTEP,
                                              scalar2=1e-30, op0=ALU.mult, op1=ALU.max), "dve")
        with ExitStack() as st2:
            db_tmp = p.sb("p5dbt", [32, HID], F32, st2)
            srow = [p.sb(f"p5srow{i}", [1, 512], F32, st2) for i in range(2)]
            bs_tmp = [p.sb(f"p5bst{i}", [33, 512], BF16, st2) for i in range(2)]
            pbc = p.ps("p5bc", [32, 512], F32, st2)
            pdum = p.ps("p5dum", [32, 16], F32, st2)
            tb_ = p.dma(nc.sync, "sy", out=db_tmp[:], in_=dB[:])
            p.inc(nc.vector.memset(bs_tmp[0][32:33, :], 1.0), "dve")
            p.inc(nc.vector.memset(bs_tmp[1][32:33, :], 1.0), "dve")
            free_bc = None
            free_sr = [None, None]
            free_bt = [None, None]
            for ch in range(HID // 512):
                cbi = ch % 2
                p.w(nc.sync, free_sr[cbi])
                t = p.dma(nc.sync, "sy", out=srow[cbi][:],
                                            in_=dsc[None, ch * 512:(ch + 1) * 512])
                p.w(nc.vector, t)
                t1 = p.inc(nc.vector.tensor_scalar(out=srow[cbi][:], in0=srow[cbi][:],
                                                   scalar1=QSTEP, scalar2=1e-30,
                                                   op0=ALU.mult, op1=ALU.max), "dve")
                p.w(nc.vector, t1)
                t2 = p.inc(nc.vector.reciprocal(out=srow[cbi][:], in_=srow[cbi][:]), "dve")
                p.w(nc.tensor, t2)
                p.w(nc.tensor, free_bc)
                nc.tensor.matmul(out=pbc[:], lhsT=ones_r32[:],
                                 rhs=srow[cbi][:], start=True, stop=True)
                dmm = nc.tensor.matmul(out=pdum[:], lhsT=ones_r32[:],
                                       rhs=srow[cbi][:, 0:16], start=True, stop=True)
                btok = p.inc(dmm, "pe")
                free_sr[cbi] = ("pe", btok[1])
                p.w(nc.vector, btok)
                p.w(nc.vector, tb_)
                p.w(nc.vector, free_bt[cbi])
                ev = p.inc(nc.vector.scalar_tensor_tensor(
                    out=bs_tmp[cbi][0:32, :], in0=pbc[:], scalar=LRS,
                    in1=db_tmp[:, ch * 512:(ch + 1) * 512], op0=ALU.mult, op1=ALU.mult), "dve")
                free_bc = ("dve", ev[1])
                p.w(nc.sync, ev)
                bt = p.dma(nc.sync, "sy", out=bsd_s[:, ch * 512:(ch + 1) * 512],
                           in_=bs_tmp[cbi][:])
                free_bt[cbi] = bt
            p.wall_all()
        bsready = ("pe", p.cnt["pe"])

        h_sb = p.sb("p5h", [128, OC, 1024], BF16, st)
        wdT = [p.sb(f"p5wdT{i}", [128, 22, 128], BF16, st) for i in range(2)]
        bs_t = [p.sb(f"p5bt{i}", [33, 128], BF16, st) for i in range(2)]
        fsum = [p.sb("p5f0", [128, 512], F32, st) for i in range(1)]
        pd = [p.ps(f"p5d{i}", [128, 1024], F32, st) for i in range(2)]

        qlist = [(0, 22), (22, 22), (44, 22), (66, 20)]
        dqb4 = dqb_s
        h3 = h_s.rearrange("(c p) t -> p c t", p=128)
        for half in range(2):
            p.wall_all()
            htoks = []
            for cq in range(4):
                c0, c1 = cq * 22, min((cq + 1) * 22, OC)
                t = p.dma(nc.sync, "sy", 
                    out=h_sb[:, c0:c1, :],
                    in_=h3[:, c0:c1, half * 1024:(half + 1) * 1024])
                htoks.append(t)
            free_wdT = [None, None]
            free_bs = [None, None]
            free_pd = [None, None]
            free_fs = [None, None]
            wq_toks = [None, None]
            bs_toks = [None, None]

            def wload(ho, qi, buf):
                so, sn = qlist[qi]
                p.w(nc.sync, free_wdT[buf])
                wq_toks[buf] = p.dmat(
                    nc.sync, "sy", wdT[buf][:, 0:sn, :],
                    dqb4[ho * 128:(ho + 1) * 128,
                         so * 128:(so + sn) * 128].rearrange("p (c f) -> p c f", f=128))

            def bsload(ho, buf):
                p.w(nc.sync, free_bs[buf])
                bs_toks[buf] = p.dma(nc.sync, "sy", out=bs_t[buf][:],
                                     in_=bsd_s[:, ho * 128:(ho + 1) * 128])

            wload(0, 0, 0)
            bsload(0, 0)
            for ho in range(HC):
                pbi = ho % 2
                p.w(nc.tensor, free_pd[pbi])
                if ho == 0:
                    for tt in htoks:
                        p.w(nc.tensor, tt)
                    p.w(nc.tensor, bsready)
                for qi in range(4):
                    buf = qi % 2
                    so, sn = qlist[qi]
                    if qi < 3:
                        wload(ho, qi + 1, 1 - buf)
                    elif ho + 1 < HC:
                        wload(ho + 1, 0, 1 - buf)
                        bsload(ho + 1, 1 - (ho % 2))
                    p.w(nc.tensor, wq_toks[buf])
                    for k in range(sn):
                        oc = so + k
                        for nb in range(2):
                            mmd = nc.tensor.matmul(
                                out=pd[pbi][:, nb * 512:(nb + 1) * 512],
                                lhsT=wdT[buf][:, k, :],
                                rhs=h_sb[:, oc, nb * 512:(nb + 1) * 512],
                                start=(oc == 0), stop=False)
                    dtok = p.inc(mmd, "pe")
                    free_wdT[buf] = ("pe", dtok[1])
                bbuf = ho % 2
                p.w(nc.tensor, bs_toks[bbuf])
                for nb in range(2):
                    mmd = nc.tensor.matmul(out=pd[pbi][:, nb * 512:(nb + 1) * 512],
                                           lhsT=bs_t[bbuf][:],
                                           rhs=rhs33_d[:, half * 1024 + nb * 512:
                                                       half * 1024 + (nb + 1) * 512],
                                           start=False, stop=True)
                dtok = p.inc(mmd, "pe")
                free_bs[bbuf] = ("pe", dtok[1])
                p.w(nc.scalar, dtok)
                if ho == 0 and half == 0:
                    p.w(nc.scalar, sctok)
                for nb in range(2):
                    p.w(nc.scalar, free_fs[0])
                    ev = p.inc(nc.scalar.mul(fsum[0][:], pd[pbi][:, nb * 512:(nb + 1) * 512],
                                             scd[:, ho:ho + 1]), "act")
                    p.w(nc.sync, ev)
                    sd = p.dma(nc.sync, "sy",
                        out=ffn_s[ho * 128:(ho + 1) * 128,
                                  half * 1024 + nb * 512:half * 1024 + (nb + 1) * 512],
                        in_=fsum[0][:])
                    free_fs[0] = sd
                free_pd[pbi] = ("act", p.cnt["act"])
        bar2 = p.snap()
        st.close()
        return bar2

    def phase_wo(bar):
        st = ExitStack()
        identf_sb = p.sb("identf_sb", [128, 128], F32, st)
        o_sb = p.sb("p6o", [128, HC, TM], BF16, st)
        wob = [p.sb(f"p6w{i}", [128, HC, 128], BF16, st) for i in range(2)]
        ffn_t = [p.sb(f"p6f{i}", [128, 512], F32, st) for i in range(2)]
        x_t = [p.sb(f"p6x{i}", [128, 4, 128], F32, st) for i in range(2)]
        s1 = [p.sb(f"p6s{i}", [128, 512], F32, st) for i in range(2)]
        outb = [p.sb(f"p6ob{i}", [128, 4, 128], F32, st) for i in range(2)]
        pa = [p.ps(f"p6a{i}", [128, 512], F32, st) for i in range(2)]
        pt = [p.ps(f"p6t{i}", [128, 512], F32, st) for i in range(2)]

        p.wall_all(bar)
        iftok = p.dma(nc.sync, "sy", out=identf_sb[:], in_=identf[:])
        o4 = o_s.rearrange("(c p) t -> p c t", p=128)
        wo4 = wo.rearrange("(c p) o -> p c o", p=128)
        xm4 = xm.rearrange("(tt p) (c f) -> p tt c f", p=128, f=128)
        out4 = out.rearrange("(tt p) (c f) -> p tt c f", p=128, f=128)
        otoks = []
        for cq in range(4):
            t = p.dma(nc.sync, "sy", out=o_sb[:, cq * 8:(cq + 1) * 8, :],
                                        in_=o4[:, cq * 8:(cq + 1) * 8, :])
            otoks.append(t)
        free_wo = [None, None]
        free_ffn = [None, None]
        free_x = [None, None]
        free_s1 = [None, None]
        free_out = [None, None]
        free_pa = [None, None]
        free_pt = [None, None]
        for oc in range(HC):
            wbi = oc % 2
            p.w(nc.gpsimd, free_wo[wbi])
            wt = p.dma(nc.gpsimd, "gp", out=wob[wbi][:],
                                           in_=wo4[:, :, oc * 128:(oc + 1) * 128])
            for tb in range(4):
                pbi = tb % 2
                fbi = (oc * 4 + tb) % 2
                p.w(nc.sync, free_ffn[fbi])
                ftok = p.dma(nc.sync, "sy", 
                    out=ffn_t[fbi][:],
                    in_=ffn_s[oc * 128:(oc + 1) * 128, tb * 512:(tb + 1) * 512])
                p.w(nc.sync, free_x[fbi])
                xtok = p.dma(nc.sync, "sy", out=x_t[fbi][:],
                                               in_=xm4[:, tb * 4:(tb + 1) * 4, oc, :])
                p.w(nc.tensor, wt)
                if oc == 0 and tb == 0:
                    for tt in otoks:
                        p.w(nc.tensor, tt)
                p.w(nc.tensor, free_pa[pbi])
                for c in range(HC):
                    mm = nc.tensor.matmul(out=pa[pbi][:], lhsT=wob[wbi][:, c, :],
                                          rhs=o_sb[:, c, tb * 512:(tb + 1) * 512],
                                          start=(c == 0), stop=(c == HC - 1))
                atok = p.inc(mm, "pe")
                sbi = (oc * 4 + tb) % 2
                p.w(nc.vector, atok)
                p.w(nc.vector, ftok)
                p.w(nc.vector, free_s1[sbi])
                stok = p.inc(nc.vector.tensor_tensor(out=s1[sbi][:], in0=pa[pbi][:],
                                                     in1=ffn_t[fbi][:], op=ALU.add), "dve")
                free_pa[pbi] = ("dve", stok[1])
                free_ffn[fbi] = ("dve", stok[1])
                p.w(nc.tensor, stok)
                p.w(nc.tensor, iftok)
                p.w(nc.tensor, free_pt[pbi])
                for k in range(4):
                    mm = nc.tensor.matmul(out=pt[pbi][:, k * 128:(k + 1) * 128],
                                          lhsT=s1[sbi][:, k * 128:(k + 1) * 128],
                                          rhs=identf_sb[:], start=True, stop=True)
                ttok = p.inc(mm, "pe")
                free_s1[sbi] = ("pe", ttok[1])
                p.w(nc.vector, ttok)
                p.w(nc.vector, xtok)
                p.w(nc.vector, free_out[fbi])
                ot = p.inc(nc.vector.tensor_tensor(
                    out=outb[fbi][:].rearrange("p c f -> p (c f)"), in0=pt[pbi][:],
                    in1=x_t[fbi][:].rearrange("p c f -> p (c f)"), op=ALU.add), "dve")
                free_pt[pbi] = ("dve", ot[1])
                free_x[fbi] = ("dve", ot[1])
                p.w(nc.sync, ot)
                sd = p.dma(nc.sync, "sy", out=out4[:, tb * 4:(tb + 1) * 4, oc, :],
                                             in_=outb[fbi][:])
                free_out[fbi] = sd
            free_wo[wbi] = ("pe", p.cnt["pe"])
        p.wall(nc.sync, {s: v for s, v in p.snap().items() if s.startswith("sy")})
        st.close()

    bar = consts_ready
    if "ln" in phases or "kv" in phases:
        bar = phase_lnkv(bar)
    if "flash" in phases:
        bar = phase_flash(bar)
    if "gateup" in phases:
        bar = phase_gateup(bar)
    if "down" in phases:
        bar = phase_down(bar)
    if "wo" in phases:
        phase_wo(bar)
    else:
        p.wall(nc.sync, {s: v for s, v in p.snap().items() if s.startswith("sy")})
        p.wall(nc.gpsimd, {s: v for s, v in p.snap().items() if s.startswith("gp")})

    p.es.close()
    return nc


def _host_inputs(inputs):
    import ml_dtypes
    x = np.asarray(inputs["L0_attn_hidden_states"], dtype=np.float32)
    f32 = lambda k: np.ascontiguousarray(np.asarray(inputs[k], dtype=np.float32))
    i32 = lambda k: np.ascontiguousarray(np.asarray(inputs[k], dtype=np.int32))
    wq, wk, wv, wo = f32("L0_attn_Wq"), f32("L0_attn_Wk"), f32("L0_attn_Wv"), f32("L0_attn_Wo")
    gq, gsc = i32("L0_ffn_gate_q"), f32("L0_ffn_gate_scale")
    gA, gB = f32("L0_ffn_gate_A"), f32("L0_ffn_gate_B")
    uq, usc = i32("L0_ffn_up_q"), f32("L0_ffn_up_scale")
    uA, uB = f32("L0_ffn_up_A"), f32("L0_ffn_up_B")
    dq, dsc = i32("L0_ffn_down_q"), f32("L0_ffn_down_scale")
    dA, dB = f32("L0_ffn_down_A"), f32("L0_ffn_down_B")
    gscT = np.ascontiguousarray(gsc.reshape(OC, 128).T)
    uscT = np.ascontiguousarray(usc.reshape(OC, 128).T)
    dscT = np.ascontiguousarray(dsc.reshape(HC, 128).T)
    bf = ml_dtypes.bfloat16
    identb = np.eye(128, dtype=np.float32).astype(bf)
    identf = np.eye(128, dtype=np.float32)
    tri = (np.arange(128)[:, None] <= np.arange(128)[None, :])
    in_maps = []
    for c in range(8):
        b, j = c // 2, c % 2
        perm = [2 * i + j for i in range(NTM)]
        xb = np.ascontiguousarray(x[b])
        xmv = np.ascontiguousarray(xb.reshape(NTF, 128, HID)[perm].reshape(TM, HID))
        masks = []
        for m in range(8):
            blocks = []
            for k in range(4):
                rel = 2 * k + j
                if rel > m:
                    blocks.append(np.ones((128, 128), dtype=np.float32))
                elif rel == m:
                    blocks.append(tri.astype(np.float32))
                else:
                    blocks.append(np.zeros((128, 128), dtype=np.float32))
            masks.append(np.concatenate(blocks, axis=1))
        fm = np.ascontiguousarray(
            np.stack([np.concatenate([masks[2 * mp], masks[2 * mp + 1]], axis=1)
                      for mp in range(4)]).astype(bf))
        in_maps.append({
            "xf": xb, "xm": xmv,
            "wq": wq, "wk": wk, "wv": wv, "wo": wo,
            "gq": gq, "gsc": gsc, "gscT": gscT, "gA": gA, "gB": gB,
            "uq": uq, "usc": usc, "uscT": uscT, "uA": uA, "uB": uB,
            "dq": dq, "dsc": dsc, "dscT": dscT, "dA": dA, "dB": dB,
            "identb": identb, "identf": identf, "fmask": fm,
        })
    return in_maps


def kernel(**inputs):
    in_maps = _host_inputs(inputs)
    nc = build_program(debug_taps=_DEBUG_TAPS, phases=_PHASES)
    res = run_bass_kernel_spmd(nc, in_maps, core_ids=list(range(8)))
    out = np.empty((BATCH, S, HID), dtype=np.float32)
    for c in range(8):
        b, j = c // 2, c % 2
        perm = [2 * i + j for i in range(NTM)]
        out[b].reshape(NTF, 128, HID)[perm] = res.results[c]["out"].reshape(NTM, 128, HID)
    if _DEBUG_TAPS:
        kernel._taps = res.results
    return out


# version 07
# speedup vs baseline: 1.0557x; 1.0557x over previous
import numpy as np
from contextlib import ExitStack
import concourse.bass as bass
import concourse.mybir as mybir
from concourse.bass_utils import run_bass_kernel_spmd

F32 = mybir.dt.float32
BF16 = mybir.dt.bfloat16
I32 = mybir.dt.int32
AF = mybir.ActivationFunctionType
ALU = mybir.AluOpType
AX = mybir.AxisListType

HID = 4096
NH = 32
S = 4096
BATCH = 4
IM = 11008
ATTN_MULT = 0.08838834764831845
CAP = 30.0
QZ = 7.5
QSTEP = 0.005
LRS = 0.02
EPS = 1e-5
TM = 2048
NTM = TM // 128
NTF = S // 128
OC = IM // 128
HC = HID // 128

_PHASES = ("ln", "kv", "flash", "gateup", "down", "wo")
_DEBUG_TAPS = ()


NLANES = 8


class Prog:

    def __init__(self):
        nc = bass.Bass("TRN2", target_bir_lowering=False, debug=False)
        self.nc = nc
        self.es = ExitStack()
        self.sem = {}
        self.cnt = {}
        for name in ("pe", "dve", "act"):
            self.sem[name] = self.es.enter_context(nc.semaphore(name))
            self.cnt[name] = 0
        self.lanes = {}
        for q in ("sy", "gp"):
            for i in range(NLANES):
                nm = f"{q}{i}"
                self.sem[nm] = self.es.enter_context(nc.semaphore(nm))
                self.cnt[nm] = 0
        self.rr = {"sy": 0, "gp": 0}
        self.lastw = {}

    def inc(self, instr, s, k=1):
        self.cnt[s] += k
        instr.then_inc(self.sem[s], k)
        return (s, self.cnt[s])

    def dma(self, eng, q, **kw):
        lane = f"{q}{self.rr[q] % NLANES}"
        self.rr[q] += 1
        if self.cnt[lane] > 0:
            eng.wait_ge(self.sem[lane], self.cnt[lane])
        self.cnt[lane] += 16
        eng.dma_start(**kw).then_inc(self.sem[lane], 16)
        return (lane, self.cnt[lane])

    def dmat(self, eng, q, out, in_):
        lane = f"{q}{self.rr[q] % NLANES}"
        self.rr[q] += 1
        if self.cnt[lane] > 0:
            eng.wait_ge(self.sem[lane], self.cnt[lane])
        self.cnt[lane] += 16
        eng.dma_start_transpose(out, in_).then_inc(self.sem[lane], 16)
        return (lane, self.cnt[lane])

    def w(self, eng, tok):
        if tok is not None:
            s, v = tok
            if v > 0 and self.lastw.get((id(eng), s), 0) < v:
                eng.wait_ge(self.sem[s], v)
                self.lastw[(id(eng), s)] = v

    def snap(self):
        return dict(self.cnt)

    def wall(self, eng, snap):
        for s, v in snap.items():
            self.w(eng, (s, v))

    def wall_all(self, snap=None):
        snap = snap or self.snap()
        nc = self.nc
        for eng in (nc.sync, nc.gpsimd, nc.tensor, nc.vector, nc.scalar):
            self.wall(eng, snap)

    def sb(self, name, shape, dtype, stack=None):
        return (stack or self.es).enter_context(self.nc.sbuf_tensor(name, shape, dtype))

    def ps(self, name, shape, dtype=F32, stack=None):
        return (stack or self.es).enter_context(self.nc.psum_tensor(name, shape, dtype))


def build_program(debug_taps=(), phases=_PHASES):
    p = Prog()
    nc = p.nc

    def dram(name, shape, dtype, kind="ExternalInput"):
        if name in debug_taps:
            kind = "ExternalOutput"
        return nc.dram_tensor(name, shape, dtype, kind=kind).ap()

    xf = dram("xf", [S, HID], F32)
    xm = dram("xm", [TM, HID], F32)
    wq = dram("wq", [HID, HID], F32)
    wk = dram("wk", [HID, 1024], F32)
    wv = dram("wv", [HID, 1024], F32)
    wo = dram("wo", [HID, HID], F32)
    gqL = dram("gqL", [128, OC, HC, 128], I32)
    gsc = dram("gsc", [IM], F32)
    gscT = dram("gscT", [128, OC], F32)
    gA = dram("gA", [HID, 32], F32)
    gB = dram("gB", [32, IM], F32)
    uqL = dram("uqL", [128, OC, HC, 128], I32)
    usc = dram("usc", [IM], F32)
    uscT = dram("uscT", [128, OC], F32)
    uA = dram("uA", [HID, 32], F32)
    uB = dram("uB", [32, IM], F32)
    dqL = dram("dqL", [128, HC, OC, 128], I32)
    dsc = dram("dsc", [HID], F32)
    dscT = dram("dscT", [128, HC], F32)
    dA = dram("dA", [IM, 32], F32)
    dB = dram("dB", [32, HID], F32)
    identb = dram("identb", [128, 128], BF16)
    identf = dram("identf", [128, 128], F32)
    fmask = dram("fmask", [4, 128, 1024], BF16)
    out = dram("out", [TM, HID], F32, kind="ExternalOutput")

    def scratch(name, shape, dtype):
        kind = "ExternalOutput" if name in debug_taps else "Internal"
        return nc.dram_tensor(name, shape, dtype, kind=kind).ap()

    g_dbg = scratch("g_dbg", [IM, TM], BF16) if "g_dbg" in debug_taps else None
    u_dbg = scratch("u_dbg", [IM, TM], BF16) if "u_dbg" in debug_taps else None
    xnf_s = scratch("xnf_s", [HID, S], BF16)
    xnm_s = scratch("xnm_s", [HID, TM], BF16)
    k_s = scratch("k_s", [1024, S], BF16)
    v_s = scratch("v_s", [S, 1024], BF16)
    o_s = scratch("o_s", [HID, TM], BF16)
    h_s = scratch("h_s", [IM, TM], BF16)
    ffn_s = scratch("ffn_s", [HID, TM], F32)
    bsg_s = scratch("bsg_s", [33, IM], BF16)
    bsu_s = scratch("bsu_s", [33, IM], BF16)
    bsd_s = scratch("bsd_s", [33, HID], BF16)

    identb_sb = p.sb("identb_sb", [128, 128], BF16)
    ones_col = p.sb("ones_col", [128, 1], BF16)
    ones_r1f = p.sb("ones_r1f", [1, 128], F32)
    ones_r32 = p.sb("ones_r32", [1, 32], F32)
    epsb = p.sb("epsb", [128, 1], F32)
    b30 = p.sb("b30", [128, 1], F32)
    vec33 = p.sb("vec33", [33, 1], F32)
    rhs33_g = p.sb("rhs33_g", [33, TM], BF16)
    rhs33_u = p.sb("rhs33_u", [33, TM], BF16)
    rhs33_d = p.sb("rhs33_d", [33, TM], BF16)

    p.dma(nc.sync, "sy", out=identb_sb[:], in_=identb[:])
    nc.vector.memset(ones_col[:], 1.0)
    nc.vector.memset(ones_r1f[:], 1.0)
    nc.vector.memset(ones_r32[:], 1.0)
    nc.vector.memset(epsb[:], EPS)
    nc.vector.memset(b30[:], -CAP)
    nc.vector.memset(vec33[0:32, :], 1.0)
    p.inc(nc.vector.memset(vec33[32:33, :], -QZ), "dve")
    consts_ready = p.snap()

    def phase_lnkv(bar):
        st = ExitStack()
        ND = 2
        xbuf = [p.sb(f"p0x{i}", [128, HID], F32, st) for i in range(ND)]
        bst = [p.sb(f"p0b{i}", [128, 8, 6], F32, st) for i in range(ND)]
        bat = [p.sb(f"p0a{i}", [128, 2], F32, st) for i in range(ND)]
        lnv = [p.sb(f"p0l{i}", [128, 1], F32, st) for i in range(ND)]
        rstd = [p.sb(f"p0r{i}", [128, 1], F32, st) for i in range(ND)]
        xn_tm = [p.sb(f"p0n{i}", [128, HID], BF16, st) for i in range(ND)]
        stage = [p.sb(f"p0s{i}", [128, HC, 128], BF16, st) for i in range(ND)]
        pst = [p.ps(f"p0p{i}", [128, 512], F32, st) for i in range(2)]
        wbuf = p.sb("p1w", [128, HC, 1024], BF16, st)
        xnb = [p.sb(f"p1x{i}", [128, HC, 512], BF16, st) for i in range(2)]
        kst = [p.sb(f"p1k{i}", [128, 512], BF16, st) for i in range(2)]
        pk = [p.ps(f"p1p{i}", [128, 512], F32, st) for i in range(2)]

        p.wall_all(bar)
        wtok = p.dma(nc.gpsimd, "gp", out=wbuf[:],
                     in_=wk.rearrange("(c p) d -> p c d", p=128))

        ln_jobs = [("f", i) for i in range(NTF)] + [("m", i) for i in range(NTM)]
        store_tok = {}
        lnstate = {"jj": 0, "free_x": [None] * ND, "free_stage": [None] * ND,
                   "free_pst": [None, None], "free_stats": [None] * ND,
                   "free_xn": [None] * ND}

        def ln_emit_one():
            if not ln_jobs:
                return
            kind, i = ln_jobs.pop(0)
            src_, dst = (xf, xnf_s) if kind == "f" else (xm, xnm_s)
            dst3 = dst.rearrange("(c p) t -> p c t", p=128)
            s = lnstate
            bi = s["jj"] % ND
            p.w(nc.sync, s["free_x"][bi])
            xa = xbuf[bi]
            xrdy = p.dma(nc.sync, "sy", out=xa[:], in_=src_[i * 128:(i + 1) * 128, :])
            p.w(nc.vector, xrdy)
            p.w(nc.vector, s["free_stats"][bi])
            for ch in range(8):
                t1 = p.inc(nc.vector.bn_stats(out=bst[bi][:, ch, :],
                                              in_=xa[:, ch * 512:(ch + 1) * 512]), "dve")
            p.w(nc.vector, t1)
            t2 = p.inc(nc.vector.bn_aggr(out=bat[bi][:],
                                         in_=bst[bi][:].rearrange("p c f -> p (c f)")), "dve")
            p.w(nc.scalar, t2)
            a1 = p.inc(nc.scalar.activation(out=lnv[bi][:], in_=bat[bi][:, 1:2], func=AF.Ln,
                                            bias=epsb[:, 0:1]), "act")
            p.w(nc.scalar, a1)
            a2 = p.inc(nc.scalar.activation(out=rstd[bi][:], in_=lnv[bi][:], func=AF.Exp,
                                            scale=-0.5), "act")
            p.w(nc.vector, a2)
            p.w(nc.vector, s["free_xn"][bi])
            t3 = p.inc(nc.vector.tensor_scalar(out=xn_tm[bi][:], in0=xa[:],
                                               scalar1=bat[bi][:, 0:1],
                                               scalar2=rstd[bi][:, 0:1],
                                               op0=ALU.subtract, op1=ALU.mult), "dve")
            s["free_x"][bi] = ("dve", t3[1])
            s["free_stats"][bi] = ("dve", t3[1])
            p.w(nc.tensor, t3)
            first_ev = True
            for qb in range(8):
                pb = pst[qb % 2]
                p.w(nc.tensor, s["free_pst"][qb % 2])
                for k in range(4):
                    c = qb * 4 + k
                    mm = nc.tensor.matmul(out=pb[:, k * 128:(k + 1) * 128],
                                          lhsT=xn_tm[bi][:, c * 128:(c + 1) * 128],
                                          rhs=identb_sb[:], start=True, stop=True)
                mtok = p.inc(mm, "pe")
                p.w(nc.vector, mtok)
                if first_ev:
                    p.w(nc.vector, s["free_stage"][bi])
                    first_ev = False
                ev = p.inc(nc.vector.tensor_copy(
                    out=stage[bi][:, qb * 4:(qb + 1) * 4, :].rearrange("p c f -> p (c f)"),
                    in_=pb[:]), "dve")
                s["free_pst"][qb % 2] = ("dve", ev[1])
            s["free_xn"][bi] = ("pe", mtok[1])
            p.w(nc.sync, ev)
            sdone = p.dma(nc.sync, "sy", out=dst3[:, :, i * 128:(i + 1) * 128],
                          in_=stage[bi][:])
            s["free_stage"][bi] = sdone
            store_tok[(kind, i)] = sdone
            s["jj"] += 1

        xnf3 = xnf_s.rearrange("(c p) t -> p c t", p=128)
        kvstate = {"free_xn": [None, None], "free_kst": [None] * 2,
                   "free_pk": [None, None], "ks": 0}

        def kv_emit(kind, tb):
            s = kvstate
            bi = tb % 2
            p.w(nc.sync, s["free_xn"][bi])
            for i in range(4 * tb, 4 * tb + 4):
                p.w(nc.sync, store_tok[("f", i)])
            xtok = p.dma(nc.sync, "sy", out=xnb[bi][:],
                         in_=xnf3[:, :, tb * 512:(tb + 1) * 512])
            for dch in range(8):
                pbi = dch % 2
                pb = pk[pbi]
                p.w(nc.tensor, xtok)
                if tb == 0 and dch == 0:
                    p.w(nc.tensor, wtok)
                p.w(nc.tensor, s["free_pk"][pbi])
                if kind == 0:
                    for c in range(HC):
                        mm = nc.tensor.matmul(out=pb[:],
                                              lhsT=wbuf[:, c, dch * 128:(dch + 1) * 128],
                                              rhs=xnb[bi][:, c, :],
                                              start=(c == 0), stop=(c == HC - 1))
                else:
                    tksub, dvb = dch // 2, dch % 2
                    for c in range(HC):
                        mm = nc.tensor.matmul(out=pb[:],
                                              lhsT=xnb[bi][:, c, tksub * 128:(tksub + 1) * 128],
                                              rhs=wbuf[:, c, dvb * 512:(dvb + 1) * 512],
                                              start=(c == 0), stop=(c == HC - 1))
                mtok = p.inc(mm, "pe")
                if dch == 7:
                    s["free_xn"][bi] = ("pe", mtok[1])
                sbi = s["ks"] % 2
                p.w(nc.vector, mtok)
                p.w(nc.vector, s["free_kst"][sbi])
                ev = p.inc(nc.vector.tensor_copy(out=kst[sbi][:], in_=pb[:]), "dve")
                s["free_pk"][pbi] = ("dve", ev[1])
                p.w(nc.sync, ev)
                if kind == 0:
                    dst = k_s[dch * 128:(dch + 1) * 128, tb * 512:(tb + 1) * 512]
                else:
                    row0 = tb * 512 + tksub * 128
                    dst = v_s[row0:row0 + 128, dvb * 512:(dvb + 1) * 512]
                sd = p.dma(nc.sync, "sy", out=dst, in_=kst[sbi][:])
                s["free_kst"][sbi] = sd
                s["ks"] += 1

        for _ in range(8):
            ln_emit_one()
        for tb in range(8):
            for _ in range(5):
                ln_emit_one()
            kv_emit(0, tb)
        p.w(nc.gpsimd, ("pe", p.cnt["pe"]))
        wtok = p.dma(nc.gpsimd, "gp", out=wbuf[:],
                     in_=wv.rearrange("(c p) d -> p c d", p=128))
        kvstate["free_xn"] = [("pe", p.cnt["pe"])] * 2
        for tb in range(8):
            while ln_jobs:
                ln_emit_one()
            p.w(nc.tensor, wtok)
            kv_emit(1, tb)
        bar2 = p.snap()
        st.close()
        return bar2

    def phase_flash(bar):
        st = ExitStack()
        xnm_sb = p.sb("p3xn", [128, HC, TM], BF16, st)
        k_sb = p.sb("p3k", [128, S], BF16, st)
        v_sb = p.sb("p3v", [128, NTF, 128], BF16, st)
        q_sb = p.sb("p3q", [128, TM], BF16, st)
        wqb = [p.sb(f"p3wq{i}", [128, HC, 128], BF16, st) for i in range(2)]
        mask_sb = p.sb("p3m", [128, 4, 1024], BF16, st)
        probs = [p.sb(f"p3pr{i}", [128, 1024], BF16, st) for i in range(2)]
        tmp_t = p.sb("p3tt", [128, 1024], F32, st)
        binv = p.sb("p3bi", [128, 512], F32, st)
        o_tile = [p.sb(f"p3o{i}", [128, 512], BF16, st) for i in range(2)]
        agb = p.sb("p3ag", [128, HC, 33], BF16, st)

        psc = [p.ps(f"p3c{i}", [128, 1024], F32, st) for i in range(2)]
        pO = [p.ps(f"p3O{i}", [128, 512], F32, st) for i in range(2)]
        pr = p.ps("p3r", [1, 512], F32, st)
        pb_ = p.ps("p3b", [128, 512], F32, st)

        p.wall_all(bar)
        xnm3 = xnm_s.rearrange("(c p) t -> p c t", p=128)
        xntok = p.dma(nc.sync, "sy", out=xnm_sb[:], in_=xnm3)
        msk = p.dma(nc.sync, "sy", out=mask_sb[:],
                                      in_=fmask.rearrange("g p f -> p g f"))

        last_ev = None
        for mi, (Asrc, r33) in enumerate(((gA, rhs33_g), (uA, rhs33_u))):
            if mi > 0:
                p.w(nc.gpsimd, ("pe", p.cnt["pe"]))
            atok = p.dma(nc.gpsimd, "gp", out=agb[:, :, 0:32],
                                             in_=Asrc.rearrange("(c p) r -> p c r", p=128))
            p.w(nc.vector, atok)
            otok = p.inc(nc.vector.memset(agb[:, :, 32:33], 1.0), "dve")
            for tb in range(4):
                pp = psc[tb % 2]
                p.w(nc.tensor, xntok)
                p.w(nc.tensor, otok)
                if last_ev is not None:
                    p.w(nc.tensor, last_ev)
                for c in range(HC):
                    mm = nc.tensor.matmul(out=pp[0:33, 0:512],
                                          lhsT=agb[:, c, :],
                                          rhs=xnm_sb[:, c, tb * 512:(tb + 1) * 512],
                                          start=(c == 0), stop=(c == HC - 1))
                mtok = p.inc(mm, "pe")
                p.w(nc.scalar, mtok)
                last_ev = p.inc(nc.scalar.mul(r33[:, tb * 512:(tb + 1) * 512], pp[0:33, 0:512],
                                              vec33[:, 0:1]), "act")
        pgu_done = last_ev

        free_wq = [None, None]
        free_q = None
        free_k = None
        free_v = None
        free_probs = [None, None]
        free_psc = [("act", pgu_done[1]), ("act", pgu_done[1])]
        free_pO = [None, None]
        free_otile = [None, None]
        free_pb = None
        last_exp = None
        wq4 = wq.rearrange("(c p) o -> p c o", p=128)
        v4 = v_s.rearrange("(c p) d -> p c d", p=128)
        ktok = vtok = None
        for h in range(NH):
            kvh = h // 4
            if h % 4 == 0:
                p.w(nc.sync, free_k)
                ktok = p.dma(nc.sync, "sy", out=k_sb[:],
                                               in_=k_s[kvh * 128:(kvh + 1) * 128, :])
                p.w(nc.sync, free_v)
                vtok = p.dma(nc.sync, "sy", out=v_sb[:],
                                               in_=v4[:, :, kvh * 128:(kvh + 1) * 128])
            wbi = h % 2
            p.w(nc.gpsimd, free_wq[wbi])
            wtok = p.dma(nc.gpsimd, "gp", out=wqb[wbi][:],
                                             in_=wq4[:, :, h * 128:(h + 1) * 128])
            p.w(nc.tensor, wtok)
            if h == 0:
                p.w(nc.tensor, xntok)
            for tb in range(4):
                pp = pO[tb % 2]
                p.w(nc.tensor, free_pO[tb % 2])
                if tb == 0:
                    p.w(nc.tensor, free_q)
                for c in range(HC):
                    mm = nc.tensor.matmul(out=pp[:], lhsT=wqb[wbi][:, c, :],
                                          rhs=xnm_sb[:, c, tb * 512:(tb + 1) * 512],
                                          start=(c == 0), stop=(c == HC - 1))
                mtok = p.inc(mm, "pe")
                p.w(nc.vector, mtok)
                ev = p.inc(nc.vector.tensor_copy(out=q_sb[:, tb * 512:(tb + 1) * 512], in_=pp[:]),
                           "dve")
                free_pO[tb % 2] = ("dve", ev[1])
            qready = ("dve", p.cnt["dve"])
            free_wq[wbi] = ("pe", p.cnt["pe"])

            units = [(g, u) for g in range(4) for u in range(4 * g + 4)]
            pend_pv = None
            pend_norm = None
            rtoks = {}

            def emit_pv(g, u, pbi, ptok):
                npair = 4 * g + 4
                ob = g % 2
                p.w(nc.tensor, ptok)
                if u == 0:
                    p.w(nc.tensor, free_pO[ob])
                    if h % 4 == 0 and g == 0:
                        p.w(nc.tensor, vtok)
                    if g > 0:
                        p.w(nc.tensor, rtoks.get(g - 1))
                nc.tensor.matmul(out=pO[ob][:], lhsT=v_sb[:, 2 * u, :],
                                 rhs=probs[pbi][:, 0:512], start=(u == 0), stop=False)
                nc.tensor.matmul(out=pO[ob][:], lhsT=v_sb[:, 2 * u + 1, :],
                                 rhs=probs[pbi][:, 512:1024],
                                 start=False, stop=(u == npair - 1))
                nc.tensor.matmul(out=pr[:], lhsT=ones_col[:], rhs=probs[pbi][:, 0:512],
                                 start=(u == 0), stop=False)
                rmm = nc.tensor.matmul(out=pr[:], lhsT=ones_col[:],
                                       rhs=probs[pbi][:, 512:1024],
                                       start=False, stop=(u == npair - 1))
                mtok2 = p.inc(rmm, "pe")
                free_probs[pbi] = ("pe", mtok2[1])
                if u == npair - 1:
                    p.w(nc.vector, ("pe", mtok2[1]))
                    rtoks[g] = p.inc(nc.vector.reciprocal(out=binv[0:1, :], in_=pr[0:1, :]),
                                     "dve")
                    return True
                return False

            def flush_norm(g):
                ob = g % 2
                rtok = rtoks[g]
                p.w(nc.tensor, rtok)
                p.w(nc.tensor, free_pb)
                bmm = nc.tensor.matmul(out=pb_[:], lhsT=ones_r1f[:], rhs=binv[0:1, :],
                                       start=True, stop=True)
                btok = p.inc(bmm, "pe")
                p.w(nc.scalar, btok)
                btok2 = p.inc(nc.scalar.copy(out=binv[:], in_=pb_[:]), "act")
                obi = g % 2
                p.w(nc.vector, btok2)
                p.w(nc.vector, free_otile[obi])
                otok = p.inc(nc.vector.tensor_tensor(out=o_tile[obi][:], in0=pO[ob][:],
                                                     in1=binv[:], op=ALU.mult), "dve")
                p.w(nc.sync, otok)
                sd = p.dma(nc.sync, "sy",
                           out=o_s[h * 128:(h + 1) * 128, g * 512:(g + 1) * 512],
                           in_=o_tile[obi][:])
                return ("act", btok2[1]), ("dve", otok[1]), sd

            for g, u in units:
                sbi = u % 2
                pp = psc[sbi]
                p.w(nc.tensor, free_psc[sbi])
                if u == 0 and g == 0:
                    p.w(nc.tensor, qready)
                    if h % 4 == 0:
                        p.w(nc.tensor, ktok)
                mm0 = nc.tensor.matmul(out=pp[:, 0:512],
                                       lhsT=k_sb[:, (2 * u) * 128:(2 * u + 1) * 128],
                                       rhs=q_sb[:, g * 512:(g + 1) * 512],
                                       start=True, stop=True)
                mm1 = nc.tensor.matmul(out=pp[:, 512:1024],
                                       lhsT=k_sb[:, (2 * u + 1) * 128:(2 * u + 2) * 128],
                                       rhs=q_sb[:, g * 512:(g + 1) * 512],
                                       start=True, stop=True)
                sctok = p.inc(mm1, "pe")
                if pend_pv is not None:
                    done = emit_pv(*pend_pv)
                    if done:
                        pend_norm = pend_pv[0]
                    pend_pv = None
                if pend_norm is not None and u >= 1:
                    fb, fo, fs = flush_norm(pend_norm)
                    free_pb = fb
                    free_pO[pend_norm % 2] = fo
                    free_otile[pend_norm % 2] = fs
                    pend_norm = None
                p.w(nc.scalar, sctok)
                if last_exp is not None:
                    p.w(nc.scalar, last_exp)
                t1 = p.inc(nc.scalar.activation(out=tmp_t[:], in_=pp[:], func=AF.Tanh,
                                                scale=ATTN_MULT / CAP), "act")
                free_psc[sbi] = ("act", t1[1])
                p.w(nc.scalar, t1)
                pbi = u % 2
                p.w(nc.scalar, free_probs[pbi])
                t2 = p.inc(nc.scalar.activation(out=probs[pbi][:], in_=tmp_t[:], func=AF.Exp,
                                                scale=CAP, bias=b30[:, 0:1]), "act")
                last_exp = t2
                ptok = t2
                if u >= 4 * g:
                    mp = u - 4 * g
                    p.w(nc.vector, t2)
                    if u == 0 and g == 0 and h == 0:
                        p.w(nc.vector, msk)
                    ptok = p.inc(nc.vector.tensor_tensor(out=probs[pbi][:], in0=probs[pbi][:],
                                                         in1=mask_sb[:, mp, :],
                                                         op=ALU.mult), "dve")
                pend_pv = (g, u, pbi, ptok)
            done = emit_pv(*pend_pv)
            pend_pv = None
            if pend_norm is not None:
                fb, fo, fs = flush_norm(pend_norm)
                free_pb = fb
                free_pO[pend_norm % 2] = fo
                free_otile[pend_norm % 2] = fs
            fb, fo, fs = flush_norm(3)
            free_pb = fb
            free_pO[3 % 2] = fo
            free_otile[3 % 2] = fs
            free_q = ("pe", p.cnt["pe"])
            if h % 4 == 3:
                free_k = ("pe", p.cnt["pe"])
                free_v = ("pe", p.cnt["pe"])
        bar2 = p.snap()
        st.close()
        return bar2

    def phase_gateup(bar):
        st = ExitStack()
        scg = p.sb("p4scg", [128, OC], F32, st)
        scu = p.sb("p4scu", [128, OC], F32, st)
        srow = [p.sb(f"p4srow{i}", [1, 512], F32, st) for i in range(2)]
        xnm_h = p.sb("p4xn", [128, HC, TM], BF16, st)

        p.wall_all(bar)
        xnm3 = xnm_s.rearrange("(c p) t -> p c t", p=128)
        xtok = p.dma(nc.gpsimd, "gp", out=xnm_h[:], in_=xnm3)
        t = p.dma(nc.sync, "sy", out=scg[:], in_=gscT[:])
        t2_ = p.dma(nc.sync, "sy", out=scu[:], in_=uscT[:])
        p.w(nc.vector, t2_)
        t1 = p.inc(nc.vector.tensor_scalar(out=scg[:], in0=scg[:], scalar1=QSTEP, scalar2=1e-30,
                                           op0=ALU.mult, op1=ALU.max), "dve")
        p.w(nc.vector, t1)
        t1 = p.inc(nc.vector.tensor_scalar(out=scu[:], in0=scu[:], scalar1=QSTEP, scalar2=1e-30,
                                           op0=ALU.mult, op1=ALU.max), "dve")
        scready = t1

        with ExitStack() as st2:
            gb_tmp = p.sb("p4gbt", [32, IM], F32, st2)
            bs_tmp = [p.sb(f"p4bst{i}", [33, 512], BF16, st2) for i in range(2)]
            pbc = p.ps("p4bc", [32, 512], F32, st2)
            pdum = p.ps("p4dum", [32, 16], F32, st2)
            for mi, (scsrc, Bsrc, Bdst_s) in enumerate(((gsc, gB, bsg_s), (usc, uB, bsu_s))):
                if mi > 0:
                    p.wall_all()
                tb_ = p.dma(nc.sync, "sy", out=gb_tmp[:], in_=Bsrc[:])
                p.inc(nc.vector.memset(bs_tmp[0][32:33, :], 1.0), "dve")
                p.inc(nc.vector.memset(bs_tmp[1][32:33, :], 1.0), "dve")
                free_bc = None
                free_sr = [None, None]
                free_bt = [None, None]
                for ch in range((IM + 511) // 512):
                    c0, c1 = ch * 512, min((ch + 1) * 512, IM)
                    n = c1 - c0
                    cbi = ch % 2
                    p.w(nc.sync, free_sr[cbi])
                    t = p.dma(nc.sync, "sy", out=srow[cbi][:, 0:n],
                                                in_=scsrc[None, c0:c1])
                    p.w(nc.vector, t)
                    t1 = p.inc(nc.vector.tensor_scalar(out=srow[cbi][:, 0:n],
                                                       in0=srow[cbi][:, 0:n],
                                                       scalar1=QSTEP, scalar2=1e-30,
                                                       op0=ALU.mult, op1=ALU.max), "dve")
                    p.w(nc.vector, t1)
                    t2 = p.inc(nc.vector.reciprocal(out=srow[cbi][:, 0:n],
                                                    in_=srow[cbi][:, 0:n]), "dve")
                    p.w(nc.tensor, t2)
                    p.w(nc.tensor, free_bc)
                    nc.tensor.matmul(out=pbc[:, 0:n], lhsT=ones_r32[:],
                                     rhs=srow[cbi][:, 0:n], start=True, stop=True)
                    dmm = nc.tensor.matmul(out=pdum[:], lhsT=ones_r32[:],
                                           rhs=srow[cbi][:, 0:16], start=True, stop=True)
                    btok = p.inc(dmm, "pe")
                    free_sr[cbi] = ("pe", btok[1])
                    p.w(nc.vector, btok)
                    p.w(nc.vector, tb_)
                    p.w(nc.vector, free_bt[cbi])
                    ev = p.inc(nc.vector.scalar_tensor_tensor(
                        out=bs_tmp[cbi][0:32, 0:n], in0=pbc[:, 0:n], scalar=LRS,
                        in1=gb_tmp[:, c0:c1], op0=ALU.mult, op1=ALU.mult), "dve")
                    free_bc = ("dve", ev[1])
                    p.w(nc.sync, ev)
                    bt = p.dma(nc.sync, "sy", out=Bdst_s[:, c0:c1], in_=bs_tmp[cbi][:, 0:n])
                    free_bt[cbi] = bt
            p.wall_all()

        wgT = [p.sb(f"p4wgT{i}", [128, HC, 128], BF16, st) for i in range(2)]
        wuT = [p.sb(f"p4wuT{i}", [128, HC, 128], BF16, st) for i in range(2)]
        bsg_t = [p.sb(f"p4bsgt{i}", [33, 128], BF16, st) for i in range(2)]
        bsu_t = [p.sb(f"p4bsut{i}", [33, 128], BF16, st) for i in range(2)]
        gs_t = [p.sb(f"p4g{i}", [128, 512], BF16, st) for i in range(4)]
        us_t = [p.sb(f"p4u{i}", [128, 512], BF16, st) for i in range(4)]
        hstage = [p.sb(f"p4h{i}", [128, 4, 512], BF16, st) for i in range(2)]
        pg = p.ps("p4pg", [128, 2048], F32, st)
        pu = p.ps("p4pu", [128, 2048], F32, st)

        wg_toks = [None, None]
        wu_toks = [None, None]
        bg_toks = [None, None]
        bu_toks = [None, None]
        free_wg = [None, None]
        free_wu = [None, None]
        free_bg = [None, None]
        free_bu = [None, None]
        free_pg = None
        free_pu = None
        free_gs = [None] * 4
        free_us = [None] * 4
        free_h = [None, None]

        def tload(oc):
            b = oc % 2
            p.w(nc.gpsimd, free_wg[b])
            wg_toks[b] = p.dma(nc.gpsimd, "gp", out=wgT[b][:], in_=gqL[:, oc, :, :])
            p.w(nc.gpsimd, free_wu[b])
            wu_toks[b] = p.dma(nc.gpsimd, "gp", out=wuT[b][:], in_=uqL[:, oc, :, :])
            p.w(nc.sync, free_bg[b])
            bg_toks[b] = p.dma(nc.sync, "sy", out=bsg_t[b][:],
                               in_=bsg_s[:, oc * 128:(oc + 1) * 128])
            p.w(nc.sync, free_bu[b])
            bu_toks[b] = p.dma(nc.sync, "sy", out=bsu_t[b][:],
                               in_=bsu_s[:, oc * 128:(oc + 1) * 128])

        tload(0)
        tload(1)
        for oc in range(OC):
            b = oc % 2
            p.w(nc.tensor, wg_toks[b])
            p.w(nc.tensor, bg_toks[b])
            p.w(nc.tensor, free_pg)
            if oc == 0:
                p.w(nc.tensor, xtok)
            for tb in range(4):
                nc.tensor.matmul(out=pg[:, tb * 512:(tb + 1) * 512], lhsT=bsg_t[b][:],
                                 rhs=rhs33_g[:, tb * 512:(tb + 1) * 512],
                                 start=True, stop=False)
            for c in range(HC):
                for tb in range(4):
                    mmg = nc.tensor.matmul(out=pg[:, tb * 512:(tb + 1) * 512],
                                           lhsT=wgT[b][:, c, :],
                                           rhs=xnm_h[:, c, tb * 512:(tb + 1) * 512],
                                           start=False, stop=(c == HC - 1))
            gtok = p.inc(mmg, "pe")
            free_wg[b] = ("pe", gtok[1])
            free_bg[b] = ("pe", gtok[1])
            p.w(nc.scalar, gtok)
            if oc == 0:
                p.w(nc.scalar, scready)
            for tb in range(4):
                p.w(nc.scalar, free_gs[tb])
                ge = p.inc(nc.scalar.activation(out=gs_t[tb][:],
                                                in_=pg[:, tb * 512:(tb + 1) * 512],
                                                func=AF.Silu, scale=scg[:, oc:oc + 1]), "act")
            free_pg = ("act", ge[1])
            p.w(nc.tensor, wu_toks[b])
            p.w(nc.tensor, bu_toks[b])
            p.w(nc.tensor, free_pu)
            for tb in range(4):
                nc.tensor.matmul(out=pu[:, tb * 512:(tb + 1) * 512], lhsT=bsu_t[b][:],
                                 rhs=rhs33_u[:, tb * 512:(tb + 1) * 512],
                                 start=True, stop=False)
            for c in range(HC):
                for tb in range(4):
                    mmu = nc.tensor.matmul(out=pu[:, tb * 512:(tb + 1) * 512],
                                           lhsT=wuT[b][:, c, :],
                                           rhs=xnm_h[:, c, tb * 512:(tb + 1) * 512],
                                           start=False, stop=(c == HC - 1))
            utok = p.inc(mmu, "pe")
            free_wu[b] = ("pe", utok[1])
            free_bu[b] = ("pe", utok[1])
            if oc + 2 < OC:
                tload(oc + 2)
            p.w(nc.scalar, utok)
            hb = oc % 2
            htok = None
            for tb in range(4):
                p.w(nc.scalar, free_us[tb])
                ue = p.inc(nc.scalar.mul(us_t[tb][:], pu[:, tb * 512:(tb + 1) * 512],
                                         scu[:, oc:oc + 1]), "act")
                p.w(nc.vector, ue)
                p.w(nc.vector, ("act", ge[1]))
                if tb == 0:
                    p.w(nc.vector, free_h[hb])
                htok = p.inc(nc.vector.tensor_tensor(out=hstage[hb][:, tb, :],
                                                     in0=gs_t[tb][:], in1=us_t[tb][:],
                                                     op=ALU.mult), "dve")
                free_gs[tb] = ("dve", htok[1])
                free_us[tb] = ("dve", htok[1])
            free_pu = ("act", p.cnt["act"])
            p.w(nc.sync, htok)
            sd = p.dma(nc.sync, "sy", out=h_s[oc * 128:(oc + 1) * 128, :],
                       in_=hstage[hb][:].rearrange("p c f -> p (c f)"))
            free_h[hb] = sd
        bar2 = p.snap()
        st.close()
        return bar2

    def phase_down(bar):
        st = ExitStack()
        scd = p.sb("p5scd", [128, HC], F32, st)

        p.wall_all(bar)
        t = p.dma(nc.sync, "sy", out=scd[:], in_=dscT[:])
        p.w(nc.vector, t)
        sctok = p.inc(nc.vector.tensor_scalar(out=scd[:], in0=scd[:], scalar1=QSTEP,
                                              scalar2=1e-30, op0=ALU.mult, op1=ALU.max), "dve")
        with ExitStack() as st2:
            db_tmp = p.sb("p5dbt", [32, HID], F32, st2)
            srow = [p.sb(f"p5srow{i}", [1, 512], F32, st2) for i in range(2)]
            bs_tmp = [p.sb(f"p5bst{i}", [33, 512], BF16, st2) for i in range(2)]
            pbc = p.ps("p5bc", [32, 512], F32, st2)
            pdum = p.ps("p5dum", [32, 16], F32, st2)
            tb_ = p.dma(nc.sync, "sy", out=db_tmp[:], in_=dB[:])
            p.inc(nc.vector.memset(bs_tmp[0][32:33, :], 1.0), "dve")
            p.inc(nc.vector.memset(bs_tmp[1][32:33, :], 1.0), "dve")
            free_bc = None
            free_sr = [None, None]
            free_bt = [None, None]
            for ch in range(HID // 512):
                cbi = ch % 2
                p.w(nc.sync, free_sr[cbi])
                t = p.dma(nc.sync, "sy", out=srow[cbi][:],
                                            in_=dsc[None, ch * 512:(ch + 1) * 512])
                p.w(nc.vector, t)
                t1 = p.inc(nc.vector.tensor_scalar(out=srow[cbi][:], in0=srow[cbi][:],
                                                   scalar1=QSTEP, scalar2=1e-30,
                                                   op0=ALU.mult, op1=ALU.max), "dve")
                p.w(nc.vector, t1)
                t2 = p.inc(nc.vector.reciprocal(out=srow[cbi][:], in_=srow[cbi][:]), "dve")
                p.w(nc.tensor, t2)
                p.w(nc.tensor, free_bc)
                nc.tensor.matmul(out=pbc[:], lhsT=ones_r32[:],
                                 rhs=srow[cbi][:], start=True, stop=True)
                dmm = nc.tensor.matmul(out=pdum[:], lhsT=ones_r32[:],
                                       rhs=srow[cbi][:, 0:16], start=True, stop=True)
                btok = p.inc(dmm, "pe")
                free_sr[cbi] = ("pe", btok[1])
                p.w(nc.vector, btok)
                p.w(nc.vector, tb_)
                p.w(nc.vector, free_bt[cbi])
                ev = p.inc(nc.vector.scalar_tensor_tensor(
                    out=bs_tmp[cbi][0:32, :], in0=pbc[:], scalar=LRS,
                    in1=db_tmp[:, ch * 512:(ch + 1) * 512], op0=ALU.mult, op1=ALU.mult), "dve")
                free_bc = ("dve", ev[1])
                p.w(nc.sync, ev)
                bt = p.dma(nc.sync, "sy", out=bsd_s[:, ch * 512:(ch + 1) * 512],
                           in_=bs_tmp[cbi][:])
                free_bt[cbi] = bt
            p.wall_all()
        bsready = ("pe", p.cnt["pe"])

        h_sb = p.sb("p5h", [128, OC, 1024], BF16, st)
        wdT = [p.sb(f"p5wdT{i}", [128, 22, 128], BF16, st) for i in range(2)]
        bs_t = [p.sb(f"p5bt{i}", [33, 128], BF16, st) for i in range(2)]
        fsum = [p.sb("p5f0", [128, 512], F32, st) for i in range(1)]
        dnA_sb = p.sb("p5dnA", [128, OC, 33], BF16, st)
        pd = [p.ps(f"p5d{i}", [128, 1024], F32, st) for i in range(2)]
        pP33 = p.ps("p5P33", [33, 1024], F32, st)

        t = p.dma(nc.gpsimd, "gp", out=dnA_sb[:, :, 0:32],
                                      in_=dA.rearrange("(c p) r -> p c r", p=128))
        p.w(nc.vector, t)
        dnatok = p.inc(nc.vector.memset(dnA_sb[:, :, 32:33], 1.0), "dve")

        qlist = [(0, 22), (22, 22), (44, 22), (66, 20)]
        h3 = h_s.rearrange("(c p) t -> p c t", p=128)
        r33evac = None
        for half in range(2):
            p.wall_all()
            htoks = []
            for cq in range(4):
                c0, c1 = cq * 22, min((cq + 1) * 22, OC)
                t = p.dma(nc.sync, "sy",
                    out=h_sb[:, c0:c1, :],
                    in_=h3[:, c0:c1, half * 1024:(half + 1) * 1024])
                htoks.append(t)
            for tt in htoks:
                p.w(nc.tensor, tt)
            p.w(nc.tensor, dnatok)
            if r33evac is not None:
                p.w(nc.tensor, r33evac)
            for poc in range(OC):
                for nb in range(2):
                    mmp = nc.tensor.matmul(out=pP33[0:33, nb * 512:(nb + 1) * 512],
                                           lhsT=dnA_sb[:, poc, :],
                                           rhs=h_sb[:, poc, nb * 512:(nb + 1) * 512],
                                           start=(poc == 0), stop=(poc == OC - 1))
            ptok = p.inc(mmp, "pe")
            p.w(nc.scalar, ptok)
            r33evac = p.inc(nc.scalar.mul(rhs33_d[:, half * 1024:(half + 1) * 1024],
                                          pP33[0:33, :], vec33[:, 0:1]), "act")
            free_wdT = [None, None]
            free_bs = [None, None]
            free_pd = [None, None]
            free_fs = [None, None]
            wq_toks = [None, None]
            bs_toks = [None, None]

            def wload(ho, qi, buf):
                so, sn = qlist[qi]
                p.w(nc.gpsimd, free_wdT[buf])
                wq_toks[buf] = p.dma(nc.gpsimd, "gp", out=wdT[buf][:, 0:sn, :],
                                     in_=dqL[:, ho, so:so + sn, :])

            def bsload(ho, buf):
                p.w(nc.sync, free_bs[buf])
                bs_toks[buf] = p.dma(nc.sync, "sy", out=bs_t[buf][:],
                                     in_=bsd_s[:, ho * 128:(ho + 1) * 128])

            wload(0, 0, 0)
            bsload(0, 0)
            for ho in range(HC):
                pbi = ho % 2
                p.w(nc.tensor, free_pd[pbi])
                if ho == 0:
                    for tt in htoks:
                        p.w(nc.tensor, tt)
                    p.w(nc.tensor, bsready)
                for qi in range(4):
                    buf = qi % 2
                    so, sn = qlist[qi]
                    if qi < 3:
                        wload(ho, qi + 1, 1 - buf)
                    elif ho + 1 < HC:
                        wload(ho + 1, 0, 1 - buf)
                        bsload(ho + 1, 1 - (ho % 2))
                    p.w(nc.tensor, wq_toks[buf])
                    for k in range(sn):
                        oc = so + k
                        for nb in range(2):
                            mmd = nc.tensor.matmul(
                                out=pd[pbi][:, nb * 512:(nb + 1) * 512],
                                lhsT=wdT[buf][:, k, :],
                                rhs=h_sb[:, oc, nb * 512:(nb + 1) * 512],
                                start=(oc == 0), stop=False)
                    dtok = p.inc(mmd, "pe")
                    free_wdT[buf] = ("pe", dtok[1])
                bbuf = ho % 2
                p.w(nc.tensor, bs_toks[bbuf])
                p.w(nc.tensor, r33evac)
                for nb in range(2):
                    mmd = nc.tensor.matmul(out=pd[pbi][:, nb * 512:(nb + 1) * 512],
                                           lhsT=bs_t[bbuf][:],
                                           rhs=rhs33_d[:, half * 1024 + nb * 512:
                                                       half * 1024 + (nb + 1) * 512],
                                           start=False, stop=True)
                dtok = p.inc(mmd, "pe")
                free_bs[bbuf] = ("pe", dtok[1])
                p.w(nc.scalar, dtok)
                if ho == 0 and half == 0:
                    p.w(nc.scalar, sctok)
                for nb in range(2):
                    p.w(nc.scalar, free_fs[0])
                    ev = p.inc(nc.scalar.mul(fsum[0][:], pd[pbi][:, nb * 512:(nb + 1) * 512],
                                             scd[:, ho:ho + 1]), "act")
                    p.w(nc.sync, ev)
                    sd = p.dma(nc.sync, "sy",
                        out=ffn_s[ho * 128:(ho + 1) * 128,
                                  half * 1024 + nb * 512:half * 1024 + (nb + 1) * 512],
                        in_=fsum[0][:])
                    free_fs[0] = sd
                free_pd[pbi] = ("act", p.cnt["act"])
        bar2 = p.snap()
        st.close()
        return bar2

    def phase_wo(bar):
        st = ExitStack()
        identf_sb = p.sb("identf_sb", [128, 128], F32, st)
        o_sb = p.sb("p6o", [128, HC, TM], BF16, st)
        wob = [p.sb(f"p6w{i}", [128, HC, 128], BF16, st) for i in range(2)]
        ffn_t = [p.sb(f"p6f{i}", [128, 512], F32, st) for i in range(2)]
        x_t = [p.sb(f"p6x{i}", [128, 4, 128], F32, st) for i in range(2)]
        s1 = [p.sb(f"p6s{i}", [128, 512], F32, st) for i in range(2)]
        outb = [p.sb(f"p6ob{i}", [128, 4, 128], F32, st) for i in range(2)]
        pa = [p.ps(f"p6a{i}", [128, 512], F32, st) for i in range(2)]
        pt = [p.ps(f"p6t{i}", [128, 512], F32, st) for i in range(2)]

        p.wall_all(bar)
        iftok = p.dma(nc.sync, "sy", out=identf_sb[:], in_=identf[:])
        o4 = o_s.rearrange("(c p) t -> p c t", p=128)
        wo4 = wo.rearrange("(c p) o -> p c o", p=128)
        xm4 = xm.rearrange("(tt p) (c f) -> p tt c f", p=128, f=128)
        out4 = out.rearrange("(tt p) (c f) -> p tt c f", p=128, f=128)
        otoks = []
        for cq in range(4):
            t = p.dma(nc.sync, "sy", out=o_sb[:, cq * 8:(cq + 1) * 8, :],
                                        in_=o4[:, cq * 8:(cq + 1) * 8, :])
            otoks.append(t)
        free_wo = [None, None]
        free_ffn = [None, None]
        free_x = [None, None]
        free_s1 = [None, None]
        free_out = [None, None]
        free_pa = [None, None]
        free_pt = [None, None]
        for oc in range(HC):
            wbi = oc % 2
            p.w(nc.gpsimd, free_wo[wbi])
            wt = p.dma(nc.gpsimd, "gp", out=wob[wbi][:],
                                           in_=wo4[:, :, oc * 128:(oc + 1) * 128])
            for tb in range(4):
                pbi = tb % 2
                fbi = (oc * 4 + tb) % 2
                p.w(nc.sync, free_ffn[fbi])
                ftok = p.dma(nc.sync, "sy", 
                    out=ffn_t[fbi][:],
                    in_=ffn_s[oc * 128:(oc + 1) * 128, tb * 512:(tb + 1) * 512])
                p.w(nc.sync, free_x[fbi])
                xtok = p.dma(nc.sync, "sy", out=x_t[fbi][:],
                                               in_=xm4[:, tb * 4:(tb + 1) * 4, oc, :])
                p.w(nc.tensor, wt)
                if oc == 0 and tb == 0:
                    for tt in otoks:
                        p.w(nc.tensor, tt)
                p.w(nc.tensor, free_pa[pbi])
                for c in range(HC):
                    mm = nc.tensor.matmul(out=pa[pbi][:], lhsT=wob[wbi][:, c, :],
                                          rhs=o_sb[:, c, tb * 512:(tb + 1) * 512],
                                          start=(c == 0), stop=(c == HC - 1))
                atok = p.inc(mm, "pe")
                sbi = (oc * 4 + tb) % 2
                p.w(nc.vector, atok)
                p.w(nc.vector, ftok)
                p.w(nc.vector, free_s1[sbi])
                stok = p.inc(nc.vector.tensor_tensor(out=s1[sbi][:], in0=pa[pbi][:],
                                                     in1=ffn_t[fbi][:], op=ALU.add), "dve")
                free_pa[pbi] = ("dve", stok[1])
                free_ffn[fbi] = ("dve", stok[1])
                p.w(nc.tensor, stok)
                p.w(nc.tensor, iftok)
                p.w(nc.tensor, free_pt[pbi])
                for k in range(4):
                    mm = nc.tensor.matmul(out=pt[pbi][:, k * 128:(k + 1) * 128],
                                          lhsT=s1[sbi][:, k * 128:(k + 1) * 128],
                                          rhs=identf_sb[:], start=True, stop=True)
                ttok = p.inc(mm, "pe")
                free_s1[sbi] = ("pe", ttok[1])
                p.w(nc.vector, ttok)
                p.w(nc.vector, xtok)
                p.w(nc.vector, free_out[fbi])
                ot = p.inc(nc.vector.tensor_tensor(
                    out=outb[fbi][:].rearrange("p c f -> p (c f)"), in0=pt[pbi][:],
                    in1=x_t[fbi][:].rearrange("p c f -> p (c f)"), op=ALU.add), "dve")
                free_pt[pbi] = ("dve", ot[1])
                free_x[fbi] = ("dve", ot[1])
                p.w(nc.sync, ot)
                sd = p.dma(nc.sync, "sy", out=out4[:, tb * 4:(tb + 1) * 4, oc, :],
                                             in_=outb[fbi][:])
                free_out[fbi] = sd
            free_wo[wbi] = ("pe", p.cnt["pe"])
        p.wall(nc.sync, {s: v for s, v in p.snap().items() if s.startswith("sy")})
        st.close()

    bar = consts_ready
    if "ln" in phases or "kv" in phases:
        bar = phase_lnkv(bar)
    if "flash" in phases:
        bar = phase_flash(bar)
    if "gateup" in phases:
        bar = phase_gateup(bar)
    if "down" in phases:
        bar = phase_down(bar)
    if "wo" in phases:
        phase_wo(bar)
    else:
        p.wall(nc.sync, {s: v for s, v in p.snap().items() if s.startswith("sy")})
        p.wall(nc.gpsimd, {s: v for s, v in p.snap().items() if s.startswith("gp")})

    p.es.close()
    return nc


def _host_inputs(inputs):
    import ml_dtypes
    x = np.asarray(inputs["L0_attn_hidden_states"], dtype=np.float32)
    f32 = lambda k: np.ascontiguousarray(np.asarray(inputs[k], dtype=np.float32))
    i32 = lambda k: np.ascontiguousarray(np.asarray(inputs[k], dtype=np.int32))
    wq, wk, wv, wo = f32("L0_attn_Wq"), f32("L0_attn_Wk"), f32("L0_attn_Wv"), f32("L0_attn_Wo")
    gq, gsc = i32("L0_ffn_gate_q"), f32("L0_ffn_gate_scale")
    gA, gB = f32("L0_ffn_gate_A"), f32("L0_ffn_gate_B")
    uq, usc = i32("L0_ffn_up_q"), f32("L0_ffn_up_scale")
    uA, uB = f32("L0_ffn_up_A"), f32("L0_ffn_up_B")
    dq, dsc = i32("L0_ffn_down_q"), f32("L0_ffn_down_scale")
    dA, dB = f32("L0_ffn_down_A"), f32("L0_ffn_down_B")
    gscT = np.ascontiguousarray(gsc.reshape(OC, 128).T)
    uscT = np.ascontiguousarray(usc.reshape(OC, 128).T)
    dscT = np.ascontiguousarray(dsc.reshape(HC, 128).T)
    gqL = np.ascontiguousarray(gq.reshape(OC, 128, HC, 128).transpose(3, 0, 2, 1))
    uqL = np.ascontiguousarray(uq.reshape(OC, 128, HC, 128).transpose(3, 0, 2, 1))
    dqL = np.ascontiguousarray(dq.reshape(HC, 128, OC, 128).transpose(3, 0, 2, 1))
    bf = ml_dtypes.bfloat16
    identb = np.eye(128, dtype=np.float32).astype(bf)
    identf = np.eye(128, dtype=np.float32)
    tri = (np.arange(128)[:, None] <= np.arange(128)[None, :])
    in_maps = []
    for c in range(8):
        b, j = c // 2, c % 2
        perm = [2 * i + j for i in range(NTM)]
        xb = np.ascontiguousarray(x[b])
        xmv = np.ascontiguousarray(xb.reshape(NTF, 128, HID)[perm].reshape(TM, HID))
        masks = []
        for m in range(8):
            blocks = []
            for k in range(4):
                rel = 2 * k + j
                if rel > m:
                    blocks.append(np.ones((128, 128), dtype=np.float32))
                elif rel == m:
                    blocks.append(tri.astype(np.float32))
                else:
                    blocks.append(np.zeros((128, 128), dtype=np.float32))
            masks.append(np.concatenate(blocks, axis=1))
        fm = np.ascontiguousarray(
            np.stack([np.concatenate([masks[2 * mp], masks[2 * mp + 1]], axis=1)
                      for mp in range(4)]).astype(bf))
        in_maps.append({
            "xf": xb, "xm": xmv,
            "wq": wq, "wk": wk, "wv": wv, "wo": wo,
            "gqL": gqL, "gsc": gsc, "gscT": gscT, "gA": gA, "gB": gB,
            "uqL": uqL, "usc": usc, "uscT": uscT, "uA": uA, "uB": uB,
            "dqL": dqL, "dsc": dsc, "dscT": dscT, "dA": dA, "dB": dB,
            "identb": identb, "identf": identf, "fmask": fm,
        })
    return in_maps


def kernel(**inputs):
    in_maps = _host_inputs(inputs)
    nc = build_program(debug_taps=_DEBUG_TAPS, phases=_PHASES)
    res = run_bass_kernel_spmd(nc, in_maps, core_ids=list(range(8)))
    out = np.empty((BATCH, S, HID), dtype=np.float32)
    for c in range(8):
        b, j = c // 2, c % 2
        perm = [2 * i + j for i in range(NTM)]
        out[b].reshape(NTF, 128, HID)[perm] = res.results[c]["out"].reshape(NTM, 128, HID)
    if _DEBUG_TAPS:
        kernel._taps = res.results
    return out


# version 08
# speedup vs baseline: 1.0910x; 1.0335x over previous
import numpy as np
from contextlib import ExitStack
import concourse.bass as bass
import concourse.mybir as mybir
from concourse.bass_utils import run_bass_kernel_spmd

F32 = mybir.dt.float32
BF16 = mybir.dt.bfloat16
I32 = mybir.dt.int32
AF = mybir.ActivationFunctionType
ALU = mybir.AluOpType
AX = mybir.AxisListType

HID = 4096
NH = 32
S = 4096
BATCH = 4
IM = 11008
ATTN_MULT = 0.08838834764831845
CAP = 30.0
QZ = 7.5
QSTEP = 0.005
LRS = 0.02
EPS = 1e-5
TM = 2048
NTM = TM // 128
NTF = S // 128
OC = IM // 128
HC = HID // 128

_PHASES = ("ln", "kv", "flash", "gateup", "down", "wo")
_DEBUG_TAPS = ()


NLANES = 8


class Prog:

    def __init__(self):
        nc = bass.Bass("TRN2", target_bir_lowering=False, debug=False)
        self.nc = nc
        self.es = ExitStack()
        self.sem = {}
        self.cnt = {}
        for name in ("pe", "dve", "act"):
            self.sem[name] = self.es.enter_context(nc.semaphore(name))
            self.cnt[name] = 0
        self.lanes = {}
        for q in ("sy", "gp"):
            for i in range(NLANES):
                nm = f"{q}{i}"
                self.sem[nm] = self.es.enter_context(nc.semaphore(nm))
                self.cnt[nm] = 0
        self.rr = {"sy": 0, "gp": 0}
        self.lastw = {}

    def inc(self, instr, s, k=1):
        self.cnt[s] += k
        instr.then_inc(self.sem[s], k)
        return (s, self.cnt[s])

    def dma(self, eng, q, **kw):
        lane = f"{q}{self.rr[q] % NLANES}"
        self.rr[q] += 1
        if self.cnt[lane] > 0:
            eng.wait_ge(self.sem[lane], self.cnt[lane])
        self.cnt[lane] += 16
        eng.dma_start(**kw).then_inc(self.sem[lane], 16)
        return (lane, self.cnt[lane])

    def dmat(self, eng, q, out, in_):
        lane = f"{q}{self.rr[q] % NLANES}"
        self.rr[q] += 1
        if self.cnt[lane] > 0:
            eng.wait_ge(self.sem[lane], self.cnt[lane])
        self.cnt[lane] += 16
        eng.dma_start_transpose(out, in_).then_inc(self.sem[lane], 16)
        return (lane, self.cnt[lane])

    def w(self, eng, tok):
        if tok is not None:
            s, v = tok
            if v > 0 and self.lastw.get((id(eng), s), 0) < v:
                eng.wait_ge(self.sem[s], v)
                self.lastw[(id(eng), s)] = v

    def snap(self):
        return dict(self.cnt)

    def wall(self, eng, snap):
        for s, v in snap.items():
            self.w(eng, (s, v))

    def wall_all(self, snap=None):
        snap = snap or self.snap()
        nc = self.nc
        for eng in (nc.sync, nc.gpsimd, nc.tensor, nc.vector, nc.scalar):
            self.wall(eng, snap)

    def sb(self, name, shape, dtype, stack=None):
        return (stack or self.es).enter_context(self.nc.sbuf_tensor(name, shape, dtype))

    def ps(self, name, shape, dtype=F32, stack=None):
        return (stack or self.es).enter_context(self.nc.psum_tensor(name, shape, dtype))


def build_program(debug_taps=(), phases=_PHASES):
    p = Prog()
    nc = p.nc

    def dram(name, shape, dtype, kind="ExternalInput"):
        if name in debug_taps:
            kind = "ExternalOutput"
        return nc.dram_tensor(name, shape, dtype, kind=kind).ap()

    xf = dram("xf", [S, HID], F32)
    xm = dram("xm", [TM, HID], F32)
    wq = dram("wq", [HID, HID], F32)
    wk = dram("wk", [HID, 1024], F32)
    wv = dram("wv", [HID, 1024], F32)
    wo = dram("wo", [HID, HID], F32)
    gqL = dram("gqL", [128, OC, HC, 128], I32)
    gsc = dram("gsc", [IM], F32)
    gscT = dram("gscT", [128, OC], F32)
    gA = dram("gA", [HID, 32], F32)
    gB = dram("gB", [32, IM], F32)
    uqL = dram("uqL", [128, OC, HC, 128], I32)
    usc = dram("usc", [IM], F32)
    uscT = dram("uscT", [128, OC], F32)
    uA = dram("uA", [HID, 32], F32)
    uB = dram("uB", [32, IM], F32)
    dqL = dram("dqL", [128, HC, OC, 128], I32)
    dsc = dram("dsc", [HID], F32)
    dscT = dram("dscT", [128, HC], F32)
    dA = dram("dA", [IM, 32], F32)
    dB = dram("dB", [32, HID], F32)
    identb = dram("identb", [128, 128], BF16)
    identf = dram("identf", [128, 128], F32)
    fmask = dram("fmask", [4, 128, 1024], BF16)
    out = dram("out", [TM, HID], F32, kind="ExternalOutput")

    def scratch(name, shape, dtype):
        kind = "ExternalOutput" if name in debug_taps else "Internal"
        return nc.dram_tensor(name, shape, dtype, kind=kind).ap()

    g_dbg = scratch("g_dbg", [IM, TM], BF16) if "g_dbg" in debug_taps else None
    u_dbg = scratch("u_dbg", [IM, TM], BF16) if "u_dbg" in debug_taps else None
    xnf_s = scratch("xnf_s", [HID, S], BF16)
    xnm_s = scratch("xnm_s", [HID, TM], BF16)
    k_s = scratch("k_s", [1024, S], BF16)
    v_s = scratch("v_s", [S, 1024], BF16)
    o_s = scratch("o_s", [HID, TM], BF16)
    h_s = scratch("h_s", [IM, TM], BF16)
    ffn_s = scratch("ffn_s", [HID, TM], F32)
    bsg_s = scratch("bsg_s", [33, IM], BF16)
    bsu_s = scratch("bsu_s", [33, IM], BF16)
    bsd_s = scratch("bsd_s", [33, HID], BF16)

    identb_sb = p.sb("identb_sb", [128, 128], BF16)
    ones_col = p.sb("ones_col", [128, 1], BF16)
    ones_r1f = p.sb("ones_r1f", [1, 128], F32)
    ones_r32 = p.sb("ones_r32", [1, 32], F32)
    epsb = p.sb("epsb", [128, 1], F32)
    b30 = p.sb("b30", [128, 1], F32)
    vec33 = p.sb("vec33", [33, 1], F32)
    rhs33_g = p.sb("rhs33_g", [33, TM], BF16)
    rhs33_u = p.sb("rhs33_u", [33, TM], BF16)
    rhs33_d = p.sb("rhs33_d", [33, TM], BF16)

    p.dma(nc.sync, "sy", out=identb_sb[:], in_=identb[:])
    nc.vector.memset(ones_col[:], 1.0)
    nc.vector.memset(ones_r1f[:], 1.0)
    nc.vector.memset(ones_r32[:], 1.0)
    nc.vector.memset(epsb[:], EPS)
    nc.vector.memset(b30[:], -CAP)
    nc.vector.memset(vec33[0:32, :], 1.0)
    p.inc(nc.vector.memset(vec33[32:33, :], -QZ), "dve")
    consts_ready = p.snap()

    def phase_lnkv(bar):
        st = ExitStack()
        ND = 2
        xbuf = [p.sb(f"p0x{i}", [128, HID], F32, st) for i in range(ND)]
        bst = [p.sb(f"p0b{i}", [128, 8, 6], F32, st) for i in range(ND)]
        bat = [p.sb(f"p0a{i}", [128, 2], F32, st) for i in range(ND)]
        lnv = [p.sb(f"p0l{i}", [128, 1], F32, st) for i in range(ND)]
        rstd = [p.sb(f"p0r{i}", [128, 1], F32, st) for i in range(ND)]
        xn_tm = [p.sb(f"p0n{i}", [128, HID], BF16, st) for i in range(ND)]
        stage = [p.sb(f"p0s{i}", [128, HC, 128], BF16, st) for i in range(ND)]
        pst = [p.ps(f"p0p{i}", [128, 512], F32, st) for i in range(2)]
        wbuf = p.sb("p1w", [128, HC, 1024], BF16, st)
        xnb = [p.sb(f"p1x{i}", [128, HC, 512], BF16, st) for i in range(2)]
        kst = [p.sb(f"p1k{i}", [128, 512], BF16, st) for i in range(2)]
        pk = [p.ps(f"p1p{i}", [128, 512], F32, st) for i in range(2)]

        p.wall_all(bar)
        wtok = p.dma(nc.gpsimd, "gp", out=wbuf[:],
                     in_=wk.rearrange("(c p) d -> p c d", p=128))

        ln_jobs = [("f", i) for i in range(NTF)] + [("m", i) for i in range(NTM)]
        store_tok = {}
        lnstate = {"jj": 0, "free_x": [None] * ND, "free_stage": [None] * ND,
                   "free_pst": [None, None], "free_stats": [None] * ND,
                   "free_xn": [None] * ND}

        def ln_emit_one():
            if not ln_jobs:
                return
            kind, i = ln_jobs.pop(0)
            src_, dst = (xf, xnf_s) if kind == "f" else (xm, xnm_s)
            dst3 = dst.rearrange("(c p) t -> p c t", p=128)
            s = lnstate
            bi = s["jj"] % ND
            p.w(nc.sync, s["free_x"][bi])
            xa = xbuf[bi]
            xrdy = p.dma(nc.sync, "sy", out=xa[:], in_=src_[i * 128:(i + 1) * 128, :])
            p.w(nc.vector, xrdy)
            p.w(nc.vector, s["free_stats"][bi])
            for ch in range(8):
                t1 = p.inc(nc.vector.bn_stats(out=bst[bi][:, ch, :],
                                              in_=xa[:, ch * 512:(ch + 1) * 512]), "dve")
            p.w(nc.vector, t1)
            t2 = p.inc(nc.vector.bn_aggr(out=bat[bi][:],
                                         in_=bst[bi][:].rearrange("p c f -> p (c f)")), "dve")
            p.w(nc.scalar, t2)
            a1 = p.inc(nc.scalar.activation(out=lnv[bi][:], in_=bat[bi][:, 1:2], func=AF.Ln,
                                            bias=epsb[:, 0:1]), "act")
            p.w(nc.scalar, a1)
            a2 = p.inc(nc.scalar.activation(out=rstd[bi][:], in_=lnv[bi][:], func=AF.Exp,
                                            scale=-0.5), "act")
            p.w(nc.vector, a2)
            p.w(nc.vector, s["free_xn"][bi])
            t3 = p.inc(nc.vector.tensor_scalar(out=xn_tm[bi][:], in0=xa[:],
                                               scalar1=bat[bi][:, 0:1],
                                               scalar2=rstd[bi][:, 0:1],
                                               op0=ALU.subtract, op1=ALU.mult), "dve")
            s["free_x"][bi] = ("dve", t3[1])
            s["free_stats"][bi] = ("dve", t3[1])
            p.w(nc.tensor, t3)
            first_ev = True
            for qb in range(8):
                pb = pst[qb % 2]
                p.w(nc.tensor, s["free_pst"][qb % 2])
                for k in range(4):
                    c = qb * 4 + k
                    mm = nc.tensor.matmul(out=pb[:, k * 128:(k + 1) * 128],
                                          lhsT=xn_tm[bi][:, c * 128:(c + 1) * 128],
                                          rhs=identb_sb[:], start=True, stop=True)
                mtok = p.inc(mm, "pe")
                p.w(nc.vector, mtok)
                if first_ev:
                    p.w(nc.vector, s["free_stage"][bi])
                    first_ev = False
                ev = p.inc(nc.vector.tensor_copy(
                    out=stage[bi][:, qb * 4:(qb + 1) * 4, :].rearrange("p c f -> p (c f)"),
                    in_=pb[:]), "dve")
                s["free_pst"][qb % 2] = ("dve", ev[1])
            s["free_xn"][bi] = ("pe", mtok[1])
            p.w(nc.sync, ev)
            sdone = p.dma(nc.sync, "sy", out=dst3[:, :, i * 128:(i + 1) * 128],
                          in_=stage[bi][:])
            s["free_stage"][bi] = sdone
            store_tok[(kind, i)] = sdone
            s["jj"] += 1

        xnf3 = xnf_s.rearrange("(c p) t -> p c t", p=128)
        kvstate = {"free_xn": [None, None], "free_kst": [None] * 2,
                   "free_pk": [None, None], "ks": 0}

        def kv_emit(kind, tb):
            s = kvstate
            bi = tb % 2
            p.w(nc.sync, s["free_xn"][bi])
            for i in range(4 * tb, 4 * tb + 4):
                p.w(nc.sync, store_tok[("f", i)])
            xtok = p.dma(nc.sync, "sy", out=xnb[bi][:],
                         in_=xnf3[:, :, tb * 512:(tb + 1) * 512])
            for dch in range(8):
                pbi = dch % 2
                pb = pk[pbi]
                p.w(nc.tensor, xtok)
                if tb == 0 and dch == 0:
                    p.w(nc.tensor, wtok)
                p.w(nc.tensor, s["free_pk"][pbi])
                if kind == 0:
                    for c in range(HC):
                        mm = nc.tensor.matmul(out=pb[:],
                                              lhsT=wbuf[:, c, dch * 128:(dch + 1) * 128],
                                              rhs=xnb[bi][:, c, :],
                                              start=(c == 0), stop=(c == HC - 1))
                else:
                    tksub, dvb = dch // 2, dch % 2
                    for c in range(HC):
                        mm = nc.tensor.matmul(out=pb[:],
                                              lhsT=xnb[bi][:, c, tksub * 128:(tksub + 1) * 128],
                                              rhs=wbuf[:, c, dvb * 512:(dvb + 1) * 512],
                                              start=(c == 0), stop=(c == HC - 1))
                mtok = p.inc(mm, "pe")
                if dch == 7:
                    s["free_xn"][bi] = ("pe", mtok[1])
                sbi = s["ks"] % 2
                p.w(nc.vector, mtok)
                p.w(nc.vector, s["free_kst"][sbi])
                ev = p.inc(nc.vector.tensor_copy(out=kst[sbi][:], in_=pb[:]), "dve")
                s["free_pk"][pbi] = ("dve", ev[1])
                p.w(nc.sync, ev)
                if kind == 0:
                    dst = k_s[dch * 128:(dch + 1) * 128, tb * 512:(tb + 1) * 512]
                else:
                    row0 = tb * 512 + tksub * 128
                    dst = v_s[row0:row0 + 128, dvb * 512:(dvb + 1) * 512]
                sd = p.dma(nc.sync, "sy", out=dst, in_=kst[sbi][:])
                s["free_kst"][sbi] = sd
                s["ks"] += 1

        for _ in range(8):
            ln_emit_one()
        for tb in range(8):
            for _ in range(5):
                ln_emit_one()
            kv_emit(0, tb)
        p.w(nc.gpsimd, ("pe", p.cnt["pe"]))
        wtok = p.dma(nc.gpsimd, "gp", out=wbuf[:],
                     in_=wv.rearrange("(c p) d -> p c d", p=128))
        kvstate["free_xn"] = [("pe", p.cnt["pe"])] * 2
        for tb in range(8):
            while ln_jobs:
                ln_emit_one()
            p.w(nc.tensor, wtok)
            kv_emit(1, tb)
        bar2 = p.snap()
        st.close()
        return bar2

    def phase_flash(bar):
        st = ExitStack()
        xnm_sb = p.sb("p3xn", [128, HC, TM], BF16, st)
        k_sb = p.sb("p3k", [128, S], BF16, st)
        v_sb = p.sb("p3v", [128, NTF, 128], BF16, st)
        q_sb = p.sb("p3q", [128, TM], BF16, st)
        wqb = [p.sb(f"p3wq{i}", [128, HC, 128], BF16, st) for i in range(2)]
        mask_sb = p.sb("p3m", [128, 4, 1024], BF16, st)
        probs = [p.sb(f"p3pr{i}", [128, 1024], BF16, st) for i in range(2)]
        tmp_t = p.sb("p3tt", [128, 1024], F32, st)
        binv = p.sb("p3bi", [128, 512], F32, st)
        acc = p.sb("p3acc", [128, 1024], F32, st)
        ones_mat = p.sb("ones_mat", [128, 128], F32, st)
        o_tile = [p.sb(f"p3o{i}", [128, 512], BF16, st) for i in range(2)]
        agb = p.sb("p3ag", [128, HC, 33], BF16, st)

        psc = [p.ps(f"p3c{i}", [128, 1024], F32, st) for i in range(2)]
        pO = [p.ps(f"p3O{i}", [128, 512], F32, st) for i in range(2)]
        pD = p.ps("p3pd", [128, 512], F32, st)

        p.wall_all(bar)
        p.inc(nc.vector.memset(ones_mat[:], 1.0), "dve")
        xnm3 = xnm_s.rearrange("(c p) t -> p c t", p=128)
        xntok = p.dma(nc.sync, "sy", out=xnm_sb[:], in_=xnm3)
        msk = p.dma(nc.sync, "sy", out=mask_sb[:],
                                      in_=fmask.rearrange("g p f -> p g f"))

        last_ev = None
        for mi, (Asrc, r33) in enumerate(((gA, rhs33_g), (uA, rhs33_u))):
            if mi > 0:
                p.w(nc.gpsimd, ("pe", p.cnt["pe"]))
            atok = p.dma(nc.gpsimd, "gp", out=agb[:, :, 0:32],
                                             in_=Asrc.rearrange("(c p) r -> p c r", p=128))
            p.w(nc.vector, atok)
            otok = p.inc(nc.vector.memset(agb[:, :, 32:33], 1.0), "dve")
            for tb in range(4):
                pp = psc[tb % 2]
                p.w(nc.tensor, xntok)
                p.w(nc.tensor, otok)
                if last_ev is not None:
                    p.w(nc.tensor, last_ev)
                for c in range(HC):
                    mm = nc.tensor.matmul(out=pp[0:33, 0:512],
                                          lhsT=agb[:, c, :],
                                          rhs=xnm_sb[:, c, tb * 512:(tb + 1) * 512],
                                          start=(c == 0), stop=(c == HC - 1))
                mtok = p.inc(mm, "pe")
                p.w(nc.scalar, mtok)
                last_ev = p.inc(nc.scalar.mul(r33[:, tb * 512:(tb + 1) * 512], pp[0:33, 0:512],
                                              vec33[:, 0:1]), "act")
        pgu_done = last_ev

        free_wq = [None, None]
        free_q = None
        free_k = None
        free_v = None
        free_probs = [None, None]
        free_probs2 = [None, None]
        free_psc = [("act", pgu_done[1]), ("act", pgu_done[1])]
        free_pO = [None, None]
        free_otile = [None, None]
        free_pD = [None]
        acc_free = [None]
        last_exp = None
        wq4 = wq.rearrange("(c p) o -> p c o", p=128)
        v4 = v_s.rearrange("(c p) d -> p c d", p=128)
        ktok = vtok = None
        for h in range(NH):
            kvh = h // 4
            if h % 4 == 0:
                p.w(nc.sync, free_k)
                ktok = p.dma(nc.sync, "sy", out=k_sb[:],
                                               in_=k_s[kvh * 128:(kvh + 1) * 128, :])
                p.w(nc.sync, free_v)
                vtok = p.dma(nc.sync, "sy", out=v_sb[:],
                                               in_=v4[:, :, kvh * 128:(kvh + 1) * 128])
            wbi = h % 2
            p.w(nc.gpsimd, free_wq[wbi])
            wtok = p.dma(nc.gpsimd, "gp", out=wqb[wbi][:],
                                             in_=wq4[:, :, h * 128:(h + 1) * 128])
            p.w(nc.tensor, wtok)
            if h == 0:
                p.w(nc.tensor, xntok)
            for tb in range(4):
                pp = pO[tb % 2]
                p.w(nc.tensor, free_pO[tb % 2])
                if tb == 0:
                    p.w(nc.tensor, free_q)
                for c in range(HC):
                    mm = nc.tensor.matmul(out=pp[:], lhsT=wqb[wbi][:, c, :],
                                          rhs=xnm_sb[:, c, tb * 512:(tb + 1) * 512],
                                          start=(c == 0), stop=(c == HC - 1))
                mtok = p.inc(mm, "pe")
                p.w(nc.vector, mtok)
                ev = p.inc(nc.vector.tensor_copy(out=q_sb[:, tb * 512:(tb + 1) * 512], in_=pp[:]),
                           "dve")
                free_pO[tb % 2] = ("dve", ev[1])
            qready = ("dve", p.cnt["dve"])
            free_wq[wbi] = ("pe", p.cnt["pe"])

            units = [(g, u) for g in range(4) for u in range(4 * g + 4)]
            pend_pv = None
            pend_norm = None
            rtoks = {}
            acc_done = {}

            def emit_pv(g, u, pbi, ptok):
                npair = 4 * g + 4
                ob = g % 2
                p.w(nc.tensor, ptok)
                if u == 0:
                    p.w(nc.tensor, free_pO[ob])
                    if h % 4 == 0 and g == 0:
                        p.w(nc.tensor, vtok)
                nc.tensor.matmul(out=pO[ob][:], lhsT=v_sb[:, 2 * u, :],
                                 rhs=probs[pbi][:, 0:512], start=(u == 0), stop=False)
                pvm = nc.tensor.matmul(out=pO[ob][:], lhsT=v_sb[:, 2 * u + 1, :],
                                       rhs=probs[pbi][:, 512:1024],
                                       start=False, stop=(u == npair - 1))
                mtok2 = p.inc(pvm, "pe")
                free_probs[pbi] = ("pe", mtok2[1])
                if u == npair - 1:
                    p.w(nc.tensor, acc_done[g])
                    p.w(nc.tensor, free_pD[0])
                    nc.tensor.matmul(out=pD[:], lhsT=ones_mat[:], rhs=acc[:, 0:512],
                                     start=True, stop=False)
                    rmm = nc.tensor.matmul(out=pD[:], lhsT=ones_mat[:],
                                           rhs=acc[:, 512:1024], start=False, stop=True)
                    rtok_pe = p.inc(rmm, "pe")
                    acc_free[0] = ("pe", rtok_pe[1])
                    p.w(nc.vector, rtok_pe)
                    rtoks[g] = p.inc(nc.vector.reciprocal(out=binv[:], in_=pD[:]), "dve")
                    free_pD[0] = ("dve", rtoks[g][1])
                    return True
                return False

            def flush_norm(g):
                ob = g % 2
                rtok = rtoks[g]
                obi = g % 2
                p.w(nc.vector, rtok)
                p.w(nc.vector, free_otile[obi])
                otok = p.inc(nc.vector.tensor_tensor(out=o_tile[obi][:], in0=pO[ob][:],
                                                     in1=binv[:], op=ALU.mult), "dve")
                p.w(nc.sync, otok)
                sd = p.dma(nc.sync, "sy",
                           out=o_s[h * 128:(h + 1) * 128, g * 512:(g + 1) * 512],
                           in_=o_tile[obi][:])
                return ("dve", otok[1]), sd

            for g, u in units:
                sbi = u % 2
                pp = psc[sbi]
                p.w(nc.tensor, free_psc[sbi])
                if u == 0 and g == 0:
                    p.w(nc.tensor, qready)
                    if h % 4 == 0:
                        p.w(nc.tensor, ktok)
                mm0 = nc.tensor.matmul(out=pp[:, 0:512],
                                       lhsT=k_sb[:, (2 * u) * 128:(2 * u + 1) * 128],
                                       rhs=q_sb[:, g * 512:(g + 1) * 512],
                                       start=True, stop=True)
                mm1 = nc.tensor.matmul(out=pp[:, 512:1024],
                                       lhsT=k_sb[:, (2 * u + 1) * 128:(2 * u + 2) * 128],
                                       rhs=q_sb[:, g * 512:(g + 1) * 512],
                                       start=True, stop=True)
                sctok = p.inc(mm1, "pe")
                if pend_pv is not None:
                    done = emit_pv(*pend_pv)
                    if done:
                        pend_norm = pend_pv[0]
                    pend_pv = None
                if pend_norm is not None and u >= 1:
                    fo, fs = flush_norm(pend_norm)
                    free_pO[pend_norm % 2] = fo
                    free_otile[pend_norm % 2] = fs
                    pend_norm = None
                p.w(nc.scalar, sctok)
                if last_exp is not None:
                    p.w(nc.scalar, last_exp)
                t1 = p.inc(nc.scalar.activation(out=tmp_t[:], in_=pp[:], func=AF.Tanh,
                                                scale=ATTN_MULT / CAP), "act")
                free_psc[sbi] = ("act", t1[1])
                p.w(nc.scalar, t1)
                pbi = u % 2
                p.w(nc.scalar, free_probs[pbi])
                p.w(nc.scalar, free_probs2[pbi])
                t2 = p.inc(nc.scalar.activation(out=probs[pbi][:], in_=tmp_t[:], func=AF.Exp,
                                                scale=CAP, bias=b30[:, 0:1]), "act")
                last_exp = t2
                ptok = t2
                if u >= 4 * g:
                    mp = u - 4 * g
                    p.w(nc.vector, t2)
                    if u == 0 and g == 0 and h == 0:
                        p.w(nc.vector, msk)
                    ptok = p.inc(nc.vector.tensor_tensor(out=probs[pbi][:], in0=probs[pbi][:],
                                                         in1=mask_sb[:, mp, :],
                                                         op=ALU.mult), "dve")
                p.w(nc.vector, ptok)
                if u == 0:
                    p.w(nc.vector, acc_free[0])
                    at = p.inc(nc.vector.tensor_copy(out=acc[:], in_=probs[pbi][:]), "dve")
                else:
                    at = p.inc(nc.vector.tensor_tensor(out=acc[:], in0=acc[:],
                                                       in1=probs[pbi][:], op=ALU.add), "dve")
                free_probs2[pbi] = ("dve", at[1])
                if u == 4 * g + 3:
                    acc_done[g] = ("dve", at[1])
                pend_pv = (g, u, pbi, ptok)
            done = emit_pv(*pend_pv)
            pend_pv = None
            if pend_norm is not None:
                fo, fs = flush_norm(pend_norm)
                free_pO[pend_norm % 2] = fo
                free_otile[pend_norm % 2] = fs
            fo, fs = flush_norm(3)
            free_pO[3 % 2] = fo
            free_otile[3 % 2] = fs
            free_q = ("pe", p.cnt["pe"])
            if h % 4 == 3:
                free_k = ("pe", p.cnt["pe"])
                free_v = ("pe", p.cnt["pe"])
        bar2 = p.snap()
        st.close()
        return bar2

    def phase_gateup(bar):
        st = ExitStack()
        scg = p.sb("p4scg", [128, OC], F32, st)
        scu = p.sb("p4scu", [128, OC], F32, st)
        srow = [p.sb(f"p4srow{i}", [1, 512], F32, st) for i in range(2)]
        xnm_h = p.sb("p4xn", [128, HC, TM], BF16, st)

        p.wall_all(bar)
        xnm3 = xnm_s.rearrange("(c p) t -> p c t", p=128)
        xtok = p.dma(nc.gpsimd, "gp", out=xnm_h[:], in_=xnm3)
        t = p.dma(nc.sync, "sy", out=scg[:], in_=gscT[:])
        t2_ = p.dma(nc.sync, "sy", out=scu[:], in_=uscT[:])
        p.w(nc.vector, t2_)
        t1 = p.inc(nc.vector.tensor_scalar(out=scg[:], in0=scg[:], scalar1=QSTEP, scalar2=1e-30,
                                           op0=ALU.mult, op1=ALU.max), "dve")
        p.w(nc.vector, t1)
        t1 = p.inc(nc.vector.tensor_scalar(out=scu[:], in0=scu[:], scalar1=QSTEP, scalar2=1e-30,
                                           op0=ALU.mult, op1=ALU.max), "dve")
        scready = t1

        with ExitStack() as st2:
            gb_tmp = p.sb("p4gbt", [32, IM], F32, st2)
            bs_tmp = [p.sb(f"p4bst{i}", [33, 512], BF16, st2) for i in range(2)]
            pbc = p.ps("p4bc", [32, 512], F32, st2)
            pdum = p.ps("p4dum", [32, 16], F32, st2)
            for mi, (scsrc, Bsrc, Bdst_s) in enumerate(((gsc, gB, bsg_s), (usc, uB, bsu_s))):
                if mi > 0:
                    p.wall_all()
                tb_ = p.dma(nc.sync, "sy", out=gb_tmp[:], in_=Bsrc[:])
                p.inc(nc.vector.memset(bs_tmp[0][32:33, :], 1.0), "dve")
                p.inc(nc.vector.memset(bs_tmp[1][32:33, :], 1.0), "dve")
                free_bc = None
                free_sr = [None, None]
                free_bt = [None, None]
                for ch in range((IM + 511) // 512):
                    c0, c1 = ch * 512, min((ch + 1) * 512, IM)
                    n = c1 - c0
                    cbi = ch % 2
                    p.w(nc.sync, free_sr[cbi])
                    t = p.dma(nc.sync, "sy", out=srow[cbi][:, 0:n],
                                                in_=scsrc[None, c0:c1])
                    p.w(nc.vector, t)
                    t1 = p.inc(nc.vector.tensor_scalar(out=srow[cbi][:, 0:n],
                                                       in0=srow[cbi][:, 0:n],
                                                       scalar1=QSTEP, scalar2=1e-30,
                                                       op0=ALU.mult, op1=ALU.max), "dve")
                    p.w(nc.vector, t1)
                    t2 = p.inc(nc.vector.reciprocal(out=srow[cbi][:, 0:n],
                                                    in_=srow[cbi][:, 0:n]), "dve")
                    p.w(nc.tensor, t2)
                    p.w(nc.tensor, free_bc)
                    nc.tensor.matmul(out=pbc[:, 0:n], lhsT=ones_r32[:],
                                     rhs=srow[cbi][:, 0:n], start=True, stop=True)
                    dmm = nc.tensor.matmul(out=pdum[:], lhsT=ones_r32[:],
                                           rhs=srow[cbi][:, 0:16], start=True, stop=True)
                    btok = p.inc(dmm, "pe")
                    free_sr[cbi] = ("pe", btok[1])
                    p.w(nc.vector, btok)
                    p.w(nc.vector, tb_)
                    p.w(nc.vector, free_bt[cbi])
                    ev = p.inc(nc.vector.scalar_tensor_tensor(
                        out=bs_tmp[cbi][0:32, 0:n], in0=pbc[:, 0:n], scalar=LRS,
                        in1=gb_tmp[:, c0:c1], op0=ALU.mult, op1=ALU.mult), "dve")
                    free_bc = ("dve", ev[1])
                    p.w(nc.sync, ev)
                    bt = p.dma(nc.sync, "sy", out=Bdst_s[:, c0:c1], in_=bs_tmp[cbi][:, 0:n])
                    free_bt[cbi] = bt
            p.wall_all()

        wgT = [p.sb(f"p4wgT{i}", [128, HC, 128], BF16, st) for i in range(2)]
        wuT = [p.sb(f"p4wuT{i}", [128, HC, 128], BF16, st) for i in range(2)]
        bsg_t = [p.sb(f"p4bsgt{i}", [33, 128], BF16, st) for i in range(2)]
        bsu_t = [p.sb(f"p4bsut{i}", [33, 128], BF16, st) for i in range(2)]
        gs_t = [p.sb(f"p4g{i}", [128, 512], BF16, st) for i in range(4)]
        us_t = [p.sb(f"p4u{i}", [128, 512], BF16, st) for i in range(4)]
        hstage = [p.sb(f"p4h{i}", [128, 4, 512], BF16, st) for i in range(2)]
        pg = p.ps("p4pg", [128, 2048], F32, st)
        pu = p.ps("p4pu", [128, 2048], F32, st)

        wg_toks = [None, None]
        wu_toks = [None, None]
        bg_toks = [None, None]
        bu_toks = [None, None]
        free_wg = [None, None]
        free_wu = [None, None]
        free_bg = [None, None]
        free_bu = [None, None]
        free_pg = None
        free_pu = None
        free_gs = [None] * 4
        free_us = [None] * 4
        free_h = [None, None]

        def tload(oc):
            b = oc % 2
            p.w(nc.gpsimd, free_wg[b])
            wg_toks[b] = p.dma(nc.gpsimd, "gp", out=wgT[b][:], in_=gqL[:, oc, :, :])
            p.w(nc.gpsimd, free_wu[b])
            wu_toks[b] = p.dma(nc.gpsimd, "gp", out=wuT[b][:], in_=uqL[:, oc, :, :])
            p.w(nc.sync, free_bg[b])
            bg_toks[b] = p.dma(nc.sync, "sy", out=bsg_t[b][:],
                               in_=bsg_s[:, oc * 128:(oc + 1) * 128])
            p.w(nc.sync, free_bu[b])
            bu_toks[b] = p.dma(nc.sync, "sy", out=bsu_t[b][:],
                               in_=bsu_s[:, oc * 128:(oc + 1) * 128])

        tload(0)
        tload(1)
        for oc in range(OC):
            b = oc % 2
            p.w(nc.tensor, wg_toks[b])
            p.w(nc.tensor, bg_toks[b])
            p.w(nc.tensor, free_pg)
            if oc == 0:
                p.w(nc.tensor, xtok)
            for tb in range(4):
                nc.tensor.matmul(out=pg[:, tb * 512:(tb + 1) * 512], lhsT=bsg_t[b][:],
                                 rhs=rhs33_g[:, tb * 512:(tb + 1) * 512],
                                 start=True, stop=False)
            for c in range(HC):
                for tb in range(4):
                    mmg = nc.tensor.matmul(out=pg[:, tb * 512:(tb + 1) * 512],
                                           lhsT=wgT[b][:, c, :],
                                           rhs=xnm_h[:, c, tb * 512:(tb + 1) * 512],
                                           start=False, stop=(c == HC - 1))
            gtok = p.inc(mmg, "pe")
            free_wg[b] = ("pe", gtok[1])
            free_bg[b] = ("pe", gtok[1])
            p.w(nc.scalar, gtok)
            if oc == 0:
                p.w(nc.scalar, scready)
            for tb in range(4):
                p.w(nc.scalar, free_gs[tb])
                ge = p.inc(nc.scalar.activation(out=gs_t[tb][:],
                                                in_=pg[:, tb * 512:(tb + 1) * 512],
                                                func=AF.Silu, scale=scg[:, oc:oc + 1]), "act")
            free_pg = ("act", ge[1])
            p.w(nc.tensor, wu_toks[b])
            p.w(nc.tensor, bu_toks[b])
            p.w(nc.tensor, free_pu)
            for tb in range(4):
                nc.tensor.matmul(out=pu[:, tb * 512:(tb + 1) * 512], lhsT=bsu_t[b][:],
                                 rhs=rhs33_u[:, tb * 512:(tb + 1) * 512],
                                 start=True, stop=False)
            for c in range(HC):
                for tb in range(4):
                    mmu = nc.tensor.matmul(out=pu[:, tb * 512:(tb + 1) * 512],
                                           lhsT=wuT[b][:, c, :],
                                           rhs=xnm_h[:, c, tb * 512:(tb + 1) * 512],
                                           start=False, stop=(c == HC - 1))
            utok = p.inc(mmu, "pe")
            free_wu[b] = ("pe", utok[1])
            free_bu[b] = ("pe", utok[1])
            if oc + 2 < OC:
                tload(oc + 2)
            p.w(nc.scalar, utok)
            hb = oc % 2
            htok = None
            for tb in range(4):
                p.w(nc.scalar, free_us[tb])
                ue = p.inc(nc.scalar.mul(us_t[tb][:], pu[:, tb * 512:(tb + 1) * 512],
                                         scu[:, oc:oc + 1]), "act")
                p.w(nc.vector, ue)
                p.w(nc.vector, ("act", ge[1]))
                if tb == 0:
                    p.w(nc.vector, free_h[hb])
                htok = p.inc(nc.vector.tensor_tensor(out=hstage[hb][:, tb, :],
                                                     in0=gs_t[tb][:], in1=us_t[tb][:],
                                                     op=ALU.mult), "dve")
                free_gs[tb] = ("dve", htok[1])
                free_us[tb] = ("dve", htok[1])
            free_pu = ("act", p.cnt["act"])
            p.w(nc.sync, htok)
            sd = p.dma(nc.sync, "sy", out=h_s[oc * 128:(oc + 1) * 128, :],
                       in_=hstage[hb][:].rearrange("p c f -> p (c f)"))
            free_h[hb] = sd
        bar2 = p.snap()
        st.close()
        return bar2

    def phase_down(bar):
        st = ExitStack()
        scd = p.sb("p5scd", [128, HC], F32, st)

        p.wall_all(bar)
        t = p.dma(nc.sync, "sy", out=scd[:], in_=dscT[:])
        p.w(nc.vector, t)
        sctok = p.inc(nc.vector.tensor_scalar(out=scd[:], in0=scd[:], scalar1=QSTEP,
                                              scalar2=1e-30, op0=ALU.mult, op1=ALU.max), "dve")
        with ExitStack() as st2:
            db_tmp = p.sb("p5dbt", [32, HID], F32, st2)
            srow = [p.sb(f"p5srow{i}", [1, 512], F32, st2) for i in range(2)]
            bs_tmp = [p.sb(f"p5bst{i}", [33, 512], BF16, st2) for i in range(2)]
            pbc = p.ps("p5bc", [32, 512], F32, st2)
            pdum = p.ps("p5dum", [32, 16], F32, st2)
            tb_ = p.dma(nc.sync, "sy", out=db_tmp[:], in_=dB[:])
            p.inc(nc.vector.memset(bs_tmp[0][32:33, :], 1.0), "dve")
            p.inc(nc.vector.memset(bs_tmp[1][32:33, :], 1.0), "dve")
            free_bc = None
            free_sr = [None, None]
            free_bt = [None, None]
            for ch in range(HID // 512):
                cbi = ch % 2
                p.w(nc.sync, free_sr[cbi])
                t = p.dma(nc.sync, "sy", out=srow[cbi][:],
                                            in_=dsc[None, ch * 512:(ch + 1) * 512])
                p.w(nc.vector, t)
                t1 = p.inc(nc.vector.tensor_scalar(out=srow[cbi][:], in0=srow[cbi][:],
                                                   scalar1=QSTEP, scalar2=1e-30,
                                                   op0=ALU.mult, op1=ALU.max), "dve")
                p.w(nc.vector, t1)
                t2 = p.inc(nc.vector.reciprocal(out=srow[cbi][:], in_=srow[cbi][:]), "dve")
                p.w(nc.tensor, t2)
                p.w(nc.tensor, free_bc)
                nc.tensor.matmul(out=pbc[:], lhsT=ones_r32[:],
                                 rhs=srow[cbi][:], start=True, stop=True)
                dmm = nc.tensor.matmul(out=pdum[:], lhsT=ones_r32[:],
                                       rhs=srow[cbi][:, 0:16], start=True, stop=True)
                btok = p.inc(dmm, "pe")
                free_sr[cbi] = ("pe", btok[1])
                p.w(nc.vector, btok)
                p.w(nc.vector, tb_)
                p.w(nc.vector, free_bt[cbi])
                ev = p.inc(nc.vector.scalar_tensor_tensor(
                    out=bs_tmp[cbi][0:32, :], in0=pbc[:], scalar=LRS,
                    in1=db_tmp[:, ch * 512:(ch + 1) * 512], op0=ALU.mult, op1=ALU.mult), "dve")
                free_bc = ("dve", ev[1])
                p.w(nc.sync, ev)
                bt = p.dma(nc.sync, "sy", out=bsd_s[:, ch * 512:(ch + 1) * 512],
                           in_=bs_tmp[cbi][:])
                free_bt[cbi] = bt
            p.wall_all()
        bsready = ("pe", p.cnt["pe"])

        h_sb = p.sb("p5h", [128, OC, 1024], BF16, st)
        wdT = [p.sb(f"p5wdT{i}", [128, 22, 128], BF16, st) for i in range(2)]
        bs_t = [p.sb(f"p5bt{i}", [33, 128], BF16, st) for i in range(2)]
        fsum = [p.sb("p5f0", [128, 512], F32, st) for i in range(1)]
        dnA_sb = p.sb("p5dnA", [128, OC, 33], BF16, st)
        pd = [p.ps(f"p5d{i}", [128, 1024], F32, st) for i in range(2)]
        pP33 = p.ps("p5P33", [33, 1024], F32, st)

        t = p.dma(nc.gpsimd, "gp", out=dnA_sb[:, :, 0:32],
                                      in_=dA.rearrange("(c p) r -> p c r", p=128))
        p.w(nc.vector, t)
        dnatok = p.inc(nc.vector.memset(dnA_sb[:, :, 32:33], 1.0), "dve")

        qlist = [(0, 22), (22, 22), (44, 22), (66, 20)]
        h3 = h_s.rearrange("(c p) t -> p c t", p=128)
        r33evac = None
        for half in range(2):
            p.wall_all()
            htoks = []
            for cq in range(4):
                c0, c1 = cq * 22, min((cq + 1) * 22, OC)
                t = p.dma(nc.sync, "sy",
                    out=h_sb[:, c0:c1, :],
                    in_=h3[:, c0:c1, half * 1024:(half + 1) * 1024])
                htoks.append(t)
            for tt in htoks:
                p.w(nc.tensor, tt)
            p.w(nc.tensor, dnatok)
            if r33evac is not None:
                p.w(nc.tensor, r33evac)
            for poc in range(OC):
                for nb in range(2):
                    mmp = nc.tensor.matmul(out=pP33[0:33, nb * 512:(nb + 1) * 512],
                                           lhsT=dnA_sb[:, poc, :],
                                           rhs=h_sb[:, poc, nb * 512:(nb + 1) * 512],
                                           start=(poc == 0), stop=(poc == OC - 1))
            ptok = p.inc(mmp, "pe")
            p.w(nc.scalar, ptok)
            r33evac = p.inc(nc.scalar.mul(rhs33_d[:, half * 1024:(half + 1) * 1024],
                                          pP33[0:33, :], vec33[:, 0:1]), "act")
            free_wdT = [None, None]
            free_bs = [None, None]
            free_pd = [None, None]
            free_fs = [None, None]
            wq_toks = [None, None]
            bs_toks = [None, None]

            def wload(ho, qi, buf):
                so, sn = qlist[qi]
                p.w(nc.gpsimd, free_wdT[buf])
                wq_toks[buf] = p.dma(nc.gpsimd, "gp", out=wdT[buf][:, 0:sn, :],
                                     in_=dqL[:, ho, so:so + sn, :])

            def bsload(ho, buf):
                p.w(nc.sync, free_bs[buf])
                bs_toks[buf] = p.dma(nc.sync, "sy", out=bs_t[buf][:],
                                     in_=bsd_s[:, ho * 128:(ho + 1) * 128])

            wload(0, 0, 0)
            bsload(0, 0)
            for ho in range(HC):
                pbi = ho % 2
                p.w(nc.tensor, free_pd[pbi])
                if ho == 0:
                    for tt in htoks:
                        p.w(nc.tensor, tt)
                    p.w(nc.tensor, bsready)
                for qi in range(4):
                    buf = qi % 2
                    so, sn = qlist[qi]
                    if qi < 3:
                        wload(ho, qi + 1, 1 - buf)
                    elif ho + 1 < HC:
                        wload(ho + 1, 0, 1 - buf)
                        bsload(ho + 1, 1 - (ho % 2))
                    p.w(nc.tensor, wq_toks[buf])
                    for k in range(sn):
                        oc = so + k
                        for nb in range(2):
                            mmd = nc.tensor.matmul(
                                out=pd[pbi][:, nb * 512:(nb + 1) * 512],
                                lhsT=wdT[buf][:, k, :],
                                rhs=h_sb[:, oc, nb * 512:(nb + 1) * 512],
                                start=(oc == 0), stop=False)
                    dtok = p.inc(mmd, "pe")
                    free_wdT[buf] = ("pe", dtok[1])
                bbuf = ho % 2
                p.w(nc.tensor, bs_toks[bbuf])
                p.w(nc.tensor, r33evac)
                for nb in range(2):
                    mmd = nc.tensor.matmul(out=pd[pbi][:, nb * 512:(nb + 1) * 512],
                                           lhsT=bs_t[bbuf][:],
                                           rhs=rhs33_d[:, half * 1024 + nb * 512:
                                                       half * 1024 + (nb + 1) * 512],
                                           start=False, stop=True)
                dtok = p.inc(mmd, "pe")
                free_bs[bbuf] = ("pe", dtok[1])
                p.w(nc.scalar, dtok)
                if ho == 0 and half == 0:
                    p.w(nc.scalar, sctok)
                for nb in range(2):
                    p.w(nc.scalar, free_fs[0])
                    ev = p.inc(nc.scalar.mul(fsum[0][:], pd[pbi][:, nb * 512:(nb + 1) * 512],
                                             scd[:, ho:ho + 1]), "act")
                    p.w(nc.sync, ev)
                    sd = p.dma(nc.sync, "sy",
                        out=ffn_s[ho * 128:(ho + 1) * 128,
                                  half * 1024 + nb * 512:half * 1024 + (nb + 1) * 512],
                        in_=fsum[0][:])
                    free_fs[0] = sd
                free_pd[pbi] = ("act", p.cnt["act"])
        bar2 = p.snap()
        st.close()
        return bar2

    def phase_wo(bar):
        st = ExitStack()
        identf_sb = p.sb("identf_sb", [128, 128], F32, st)
        o_sb = p.sb("p6o", [128, HC, TM], BF16, st)
        wob = [p.sb(f"p6w{i}", [128, HC, 128], BF16, st) for i in range(2)]
        ffn_t = [p.sb(f"p6f{i}", [128, 512], F32, st) for i in range(2)]
        x_t = [p.sb(f"p6x{i}", [128, 4, 128], F32, st) for i in range(2)]
        s1 = [p.sb(f"p6s{i}", [128, 512], F32, st) for i in range(2)]
        outb = [p.sb(f"p6ob{i}", [128, 4, 128], F32, st) for i in range(2)]
        pa = [p.ps(f"p6a{i}", [128, 512], F32, st) for i in range(2)]
        pt = [p.ps(f"p6t{i}", [128, 512], F32, st) for i in range(2)]

        p.wall_all(bar)
        iftok = p.dma(nc.sync, "sy", out=identf_sb[:], in_=identf[:])
        o4 = o_s.rearrange("(c p) t -> p c t", p=128)
        wo4 = wo.rearrange("(c p) o -> p c o", p=128)
        xm4 = xm.rearrange("(tt p) (c f) -> p tt c f", p=128, f=128)
        out4 = out.rearrange("(tt p) (c f) -> p tt c f", p=128, f=128)
        otoks = []
        for cq in range(4):
            t = p.dma(nc.sync, "sy", out=o_sb[:, cq * 8:(cq + 1) * 8, :],
                                        in_=o4[:, cq * 8:(cq + 1) * 8, :])
            otoks.append(t)
        free_wo = [None, None]
        free_ffn = [None, None]
        free_x = [None, None]
        free_s1 = [None, None]
        free_out = [None, None]
        free_pa = [None, None]
        free_pt = [None, None]
        for oc in range(HC):
            wbi = oc % 2
            p.w(nc.gpsimd, free_wo[wbi])
            wt = p.dma(nc.gpsimd, "gp", out=wob[wbi][:],
                                           in_=wo4[:, :, oc * 128:(oc + 1) * 128])
            for tb in range(4):
                pbi = tb % 2
                fbi = (oc * 4 + tb) % 2
                p.w(nc.sync, free_ffn[fbi])
                ftok = p.dma(nc.sync, "sy", 
                    out=ffn_t[fbi][:],
                    in_=ffn_s[oc * 128:(oc + 1) * 128, tb * 512:(tb + 1) * 512])
                p.w(nc.sync, free_x[fbi])
                xtok = p.dma(nc.sync, "sy", out=x_t[fbi][:],
                                               in_=xm4[:, tb * 4:(tb + 1) * 4, oc, :])
                p.w(nc.tensor, wt)
                if oc == 0 and tb == 0:
                    for tt in otoks:
                        p.w(nc.tensor, tt)
                p.w(nc.tensor, free_pa[pbi])
                for c in range(HC):
                    mm = nc.tensor.matmul(out=pa[pbi][:], lhsT=wob[wbi][:, c, :],
                                          rhs=o_sb[:, c, tb * 512:(tb + 1) * 512],
                                          start=(c == 0), stop=(c == HC - 1))
                atok = p.inc(mm, "pe")
                sbi = (oc * 4 + tb) % 2
                p.w(nc.vector, atok)
                p.w(nc.vector, ftok)
                p.w(nc.vector, free_s1[sbi])
                stok = p.inc(nc.vector.tensor_tensor(out=s1[sbi][:], in0=pa[pbi][:],
                                                     in1=ffn_t[fbi][:], op=ALU.add), "dve")
                free_pa[pbi] = ("dve", stok[1])
                free_ffn[fbi] = ("dve", stok[1])
                p.w(nc.tensor, stok)
                p.w(nc.tensor, iftok)
                p.w(nc.tensor, free_pt[pbi])
                for k in range(4):
                    mm = nc.tensor.matmul(out=pt[pbi][:, k * 128:(k + 1) * 128],
                                          lhsT=s1[sbi][:, k * 128:(k + 1) * 128],
                                          rhs=identf_sb[:], start=True, stop=True)
                ttok = p.inc(mm, "pe")
                free_s1[sbi] = ("pe", ttok[1])
                p.w(nc.vector, ttok)
                p.w(nc.vector, xtok)
                p.w(nc.vector, free_out[fbi])
                ot = p.inc(nc.vector.tensor_tensor(
                    out=outb[fbi][:].rearrange("p c f -> p (c f)"), in0=pt[pbi][:],
                    in1=x_t[fbi][:].rearrange("p c f -> p (c f)"), op=ALU.add), "dve")
                free_pt[pbi] = ("dve", ot[1])
                free_x[fbi] = ("dve", ot[1])
                p.w(nc.sync, ot)
                sd = p.dma(nc.sync, "sy", out=out4[:, tb * 4:(tb + 1) * 4, oc, :],
                                             in_=outb[fbi][:])
                free_out[fbi] = sd
            free_wo[wbi] = ("pe", p.cnt["pe"])
        p.wall(nc.sync, {s: v for s, v in p.snap().items() if s.startswith("sy")})
        st.close()

    bar = consts_ready
    if "ln" in phases or "kv" in phases:
        bar = phase_lnkv(bar)
    if "flash" in phases:
        bar = phase_flash(bar)
    if "gateup" in phases:
        bar = phase_gateup(bar)
    if "down" in phases:
        bar = phase_down(bar)
    if "wo" in phases:
        phase_wo(bar)
    else:
        p.wall(nc.sync, {s: v for s, v in p.snap().items() if s.startswith("sy")})
        p.wall(nc.gpsimd, {s: v for s, v in p.snap().items() if s.startswith("gp")})

    p.es.close()
    return nc


def _host_inputs(inputs):
    import ml_dtypes
    x = np.asarray(inputs["L0_attn_hidden_states"], dtype=np.float32)
    f32 = lambda k: np.ascontiguousarray(np.asarray(inputs[k], dtype=np.float32))
    i32 = lambda k: np.ascontiguousarray(np.asarray(inputs[k], dtype=np.int32))
    wq, wk, wv, wo = f32("L0_attn_Wq"), f32("L0_attn_Wk"), f32("L0_attn_Wv"), f32("L0_attn_Wo")
    gq, gsc = i32("L0_ffn_gate_q"), f32("L0_ffn_gate_scale")
    gA, gB = f32("L0_ffn_gate_A"), f32("L0_ffn_gate_B")
    uq, usc = i32("L0_ffn_up_q"), f32("L0_ffn_up_scale")
    uA, uB = f32("L0_ffn_up_A"), f32("L0_ffn_up_B")
    dq, dsc = i32("L0_ffn_down_q"), f32("L0_ffn_down_scale")
    dA, dB = f32("L0_ffn_down_A"), f32("L0_ffn_down_B")
    gscT = np.ascontiguousarray(gsc.reshape(OC, 128).T)
    uscT = np.ascontiguousarray(usc.reshape(OC, 128).T)
    dscT = np.ascontiguousarray(dsc.reshape(HC, 128).T)
    gqL = np.ascontiguousarray(gq.reshape(OC, 128, HC, 128).transpose(3, 0, 2, 1))
    uqL = np.ascontiguousarray(uq.reshape(OC, 128, HC, 128).transpose(3, 0, 2, 1))
    dqL = np.ascontiguousarray(dq.reshape(HC, 128, OC, 128).transpose(3, 0, 2, 1))
    bf = ml_dtypes.bfloat16
    identb = np.eye(128, dtype=np.float32).astype(bf)
    identf = np.eye(128, dtype=np.float32)
    tri = (np.arange(128)[:, None] <= np.arange(128)[None, :])
    in_maps = []
    for c in range(8):
        b, j = c // 2, c % 2
        perm = [2 * i + j for i in range(NTM)]
        xb = np.ascontiguousarray(x[b])
        xmv = np.ascontiguousarray(xb.reshape(NTF, 128, HID)[perm].reshape(TM, HID))
        masks = []
        for m in range(8):
            blocks = []
            for k in range(4):
                rel = 2 * k + j
                if rel > m:
                    blocks.append(np.ones((128, 128), dtype=np.float32))
                elif rel == m:
                    blocks.append(tri.astype(np.float32))
                else:
                    blocks.append(np.zeros((128, 128), dtype=np.float32))
            masks.append(np.concatenate(blocks, axis=1))
        fm = np.ascontiguousarray(
            np.stack([np.concatenate([masks[2 * mp], masks[2 * mp + 1]], axis=1)
                      for mp in range(4)]).astype(bf))
        in_maps.append({
            "xf": xb, "xm": xmv,
            "wq": wq, "wk": wk, "wv": wv, "wo": wo,
            "gqL": gqL, "gsc": gsc, "gscT": gscT, "gA": gA, "gB": gB,
            "uqL": uqL, "usc": usc, "uscT": uscT, "uA": uA, "uB": uB,
            "dqL": dqL, "dsc": dsc, "dscT": dscT, "dA": dA, "dB": dB,
            "identb": identb, "identf": identf, "fmask": fm,
        })
    return in_maps


def kernel(**inputs):
    in_maps = _host_inputs(inputs)
    nc = build_program(debug_taps=_DEBUG_TAPS, phases=_PHASES)
    res = run_bass_kernel_spmd(nc, in_maps, core_ids=list(range(8)))
    out = np.empty((BATCH, S, HID), dtype=np.float32)
    for c in range(8):
        b, j = c // 2, c % 2
        perm = [2 * i + j for i in range(NTM)]
        out[b].reshape(NTF, 128, HID)[perm] = res.results[c]["out"].reshape(NTM, 128, HID)
    if _DEBUG_TAPS:
        kernel._taps = res.results
    return out


# version 09
# speedup vs baseline: 1.1015x; 1.0096x over previous
import numpy as np
from contextlib import ExitStack
import concourse.bass as bass
import concourse.mybir as mybir
from concourse.bass_utils import run_bass_kernel_spmd

F32 = mybir.dt.float32
BF16 = mybir.dt.bfloat16
I32 = mybir.dt.int32
AF = mybir.ActivationFunctionType
ALU = mybir.AluOpType
AX = mybir.AxisListType

HID = 4096
NH = 32
S = 4096
BATCH = 4
IM = 11008
ATTN_MULT = 0.08838834764831845
CAP = 30.0
QZ = 7.5
QSTEP = 0.005
LRS = 0.02
EPS = 1e-5
TM = 2048
NTM = TM // 128
NTF = S // 128
OC = IM // 128
HC = HID // 128

_PHASES = ("ln", "kv", "flash", "gateup", "down", "wo")
_DEBUG_TAPS = ()


NLANES = 8


class Prog:

    def __init__(self):
        nc = bass.Bass("TRN2", target_bir_lowering=False, debug=False)
        self.nc = nc
        self.es = ExitStack()
        self.sem = {}
        self.cnt = {}
        for name in ("pe", "dve", "act"):
            self.sem[name] = self.es.enter_context(nc.semaphore(name))
            self.cnt[name] = 0
        self.lanes = {}
        for q in ("sy", "gp"):
            for i in range(NLANES):
                nm = f"{q}{i}"
                self.sem[nm] = self.es.enter_context(nc.semaphore(nm))
                self.cnt[nm] = 0
        self.rr = {"sy": 0, "gp": 0}
        self.lastw = {}

    def inc(self, instr, s, k=1):
        self.cnt[s] += k
        instr.then_inc(self.sem[s], k)
        return (s, self.cnt[s])

    def dma(self, eng, q, **kw):
        lane = f"{q}{self.rr[q] % NLANES}"
        self.rr[q] += 1
        if self.cnt[lane] > 0:
            eng.wait_ge(self.sem[lane], self.cnt[lane])
        self.cnt[lane] += 16
        eng.dma_start(**kw).then_inc(self.sem[lane], 16)
        return (lane, self.cnt[lane])

    def dmat(self, eng, q, out, in_):
        lane = f"{q}{self.rr[q] % NLANES}"
        self.rr[q] += 1
        if self.cnt[lane] > 0:
            eng.wait_ge(self.sem[lane], self.cnt[lane])
        self.cnt[lane] += 16
        eng.dma_start_transpose(out, in_).then_inc(self.sem[lane], 16)
        return (lane, self.cnt[lane])

    def w(self, eng, tok):
        if tok is not None:
            s, v = tok
            if v > 0 and self.lastw.get((id(eng), s), 0) < v:
                eng.wait_ge(self.sem[s], v)
                self.lastw[(id(eng), s)] = v

    def snap(self):
        return dict(self.cnt)

    def wall(self, eng, snap):
        for s, v in snap.items():
            self.w(eng, (s, v))

    def wall_all(self, snap=None):
        snap = snap or self.snap()
        nc = self.nc
        for eng in (nc.sync, nc.gpsimd, nc.tensor, nc.vector, nc.scalar):
            self.wall(eng, snap)

    def sb(self, name, shape, dtype, stack=None):
        return (stack or self.es).enter_context(self.nc.sbuf_tensor(name, shape, dtype))

    def ps(self, name, shape, dtype=F32, stack=None):
        return (stack or self.es).enter_context(self.nc.psum_tensor(name, shape, dtype))


def build_program(debug_taps=(), phases=_PHASES):
    p = Prog()
    nc = p.nc

    def dram(name, shape, dtype, kind="ExternalInput"):
        if name in debug_taps:
            kind = "ExternalOutput"
        return nc.dram_tensor(name, shape, dtype, kind=kind).ap()

    xf = dram("xf", [S, HID], F32)
    xm = dram("xm", [TM, HID], F32)
    wq = dram("wq", [HID, HID], F32)
    wk = dram("wk", [HID, 1024], F32)
    wv = dram("wv", [HID, 1024], F32)
    wo = dram("wo", [HID, HID], F32)
    gqL = dram("gqL", [128, OC, HC, 128], I32)
    gsc = dram("gsc", [IM], F32)
    gscT = dram("gscT", [128, OC], F32)
    gA = dram("gA", [HID, 32], F32)
    gB = dram("gB", [32, IM], F32)
    uqL = dram("uqL", [128, OC, HC, 128], I32)
    usc = dram("usc", [IM], F32)
    uscT = dram("uscT", [128, OC], F32)
    uA = dram("uA", [HID, 32], F32)
    uB = dram("uB", [32, IM], F32)
    dqL = dram("dqL", [128, HC, OC, 128], I32)
    dsc = dram("dsc", [HID], F32)
    dscT = dram("dscT", [128, HC], F32)
    dA = dram("dA", [IM, 32], F32)
    dB = dram("dB", [32, HID], F32)
    identb = dram("identb", [128, 128], BF16)
    identf = dram("identf", [128, 128], F32)
    fmask = dram("fmask", [4, 128, 1024], BF16)
    out = dram("out", [TM, HID], F32, kind="ExternalOutput")

    def scratch(name, shape, dtype):
        kind = "ExternalOutput" if name in debug_taps else "Internal"
        return nc.dram_tensor(name, shape, dtype, kind=kind).ap()

    g_dbg = scratch("g_dbg", [IM, TM], BF16) if "g_dbg" in debug_taps else None
    u_dbg = scratch("u_dbg", [IM, TM], BF16) if "u_dbg" in debug_taps else None
    xnf_s = scratch("xnf_s", [HID, S], BF16)
    xnm_s = scratch("xnm_s", [HID, TM], BF16)
    k_s = scratch("k_s", [1024, S], BF16)
    v_s = scratch("v_s", [S, 1024], BF16)
    o_s = scratch("o_s", [HID, TM], BF16)
    h_s = scratch("h_s", [IM, TM], BF16)
    ffn_s = scratch("ffn_s", [HID, TM], F32)
    bsg_s = scratch("bsg_s", [33, IM], BF16)
    bsu_s = scratch("bsu_s", [33, IM], BF16)
    bsd_s = scratch("bsd_s", [33, HID], BF16)
    q_s = scratch("q_s", [HID, TM], BF16)

    identb_sb = p.sb("identb_sb", [128, 128], BF16)
    ones_col = p.sb("ones_col", [128, 1], BF16)
    ones_r32 = p.sb("ones_r32", [1, 32], F32)
    epsb = p.sb("epsb", [128, 1], F32)
    b30 = p.sb("b30", [128, 1], F32)
    vec33 = p.sb("vec33", [33, 1], F32)
    rhs33_g = p.sb("rhs33_g", [33, TM], BF16)
    rhs33_u = p.sb("rhs33_u", [33, TM], BF16)

    p.dma(nc.sync, "sy", out=identb_sb[:], in_=identb[:])
    nc.vector.memset(ones_col[:], 1.0)
    nc.vector.memset(ones_r32[:], 1.0)
    nc.vector.memset(epsb[:], EPS)
    nc.vector.memset(b30[:], -CAP)
    nc.vector.memset(vec33[0:32, :], 1.0)
    p.inc(nc.vector.memset(vec33[32:33, :], -QZ), "dve")
    consts_ready = p.snap()

    def phase_lnkv(bar):
        st = ExitStack()
        ND = 2
        xbuf = [p.sb(f"p0x{i}", [128, HID], F32, st) for i in range(ND)]
        bst = [p.sb(f"p0b{i}", [128, 8, 6], F32, st) for i in range(ND)]
        bat = [p.sb(f"p0a{i}", [128, 2], F32, st) for i in range(ND)]
        lnv = [p.sb(f"p0l{i}", [128, 1], F32, st) for i in range(ND)]
        rstd = [p.sb(f"p0r{i}", [128, 1], F32, st) for i in range(ND)]
        xn_tm = [p.sb(f"p0n{i}", [128, HID], BF16, st) for i in range(ND)]
        stage = [p.sb(f"p0s{i}", [128, HC, 128], BF16, st) for i in range(ND)]
        pst = [p.ps(f"p0p{i}", [128, 512], F32, st) for i in range(2)]
        wbuf = p.sb("p1w", [128, HC, 1024], BF16, st)
        xnb = [p.sb(f"p1x{i}", [128, HC, 512], BF16, st) for i in range(2)]
        kst = [p.sb(f"p1k{i}", [128, 512], BF16, st) for i in range(2)]
        pk = [p.ps(f"p1p{i}", [128, 512], F32, st) for i in range(2)]

        p.wall_all(bar)
        wtok = p.dma(nc.gpsimd, "gp", out=wbuf[:],
                     in_=wk.rearrange("(c p) d -> p c d", p=128))

        ln_jobs = [("f", i) for i in range(NTF)] + [("m", i) for i in range(NTM)]
        store_tok = {}
        lnstate = {"jj": 0, "free_x": [None] * ND, "free_stage": [None] * ND,
                   "free_pst": [None, None], "free_stats": [None] * ND,
                   "free_xn": [None] * ND}

        def ln_emit_one():
            if not ln_jobs:
                return
            kind, i = ln_jobs.pop(0)
            src_, dst = (xf, xnf_s) if kind == "f" else (xm, xnm_s)
            dst3 = dst.rearrange("(c p) t -> p c t", p=128)
            s = lnstate
            bi = s["jj"] % ND
            p.w(nc.sync, s["free_x"][bi])
            xa = xbuf[bi]
            xrdy = p.dma(nc.sync, "sy", out=xa[:], in_=src_[i * 128:(i + 1) * 128, :])
            p.w(nc.vector, xrdy)
            p.w(nc.vector, s["free_stats"][bi])
            for ch in range(8):
                t1 = p.inc(nc.vector.bn_stats(out=bst[bi][:, ch, :],
                                              in_=xa[:, ch * 512:(ch + 1) * 512]), "dve")
            p.w(nc.vector, t1)
            t2 = p.inc(nc.vector.bn_aggr(out=bat[bi][:],
                                         in_=bst[bi][:].rearrange("p c f -> p (c f)")), "dve")
            p.w(nc.scalar, t2)
            a1 = p.inc(nc.scalar.activation(out=lnv[bi][:], in_=bat[bi][:, 1:2], func=AF.Ln,
                                            bias=epsb[:, 0:1]), "act")
            p.w(nc.scalar, a1)
            a2 = p.inc(nc.scalar.activation(out=rstd[bi][:], in_=lnv[bi][:], func=AF.Exp,
                                            scale=-0.5), "act")
            p.w(nc.vector, a2)
            p.w(nc.vector, s["free_xn"][bi])
            t3 = p.inc(nc.vector.tensor_scalar(out=xn_tm[bi][:], in0=xa[:],
                                               scalar1=bat[bi][:, 0:1],
                                               scalar2=rstd[bi][:, 0:1],
                                               op0=ALU.subtract, op1=ALU.mult), "dve")
            s["free_x"][bi] = ("dve", t3[1])
            s["free_stats"][bi] = ("dve", t3[1])
            p.w(nc.tensor, t3)
            first_ev = True
            for qb in range(8):
                pb = pst[qb % 2]
                p.w(nc.tensor, s["free_pst"][qb % 2])
                for k in range(4):
                    c = qb * 4 + k
                    mm = nc.tensor.matmul(out=pb[:, k * 128:(k + 1) * 128],
                                          lhsT=xn_tm[bi][:, c * 128:(c + 1) * 128],
                                          rhs=identb_sb[:], start=True, stop=True)
                mtok = p.inc(mm, "pe")
                p.w(nc.vector, mtok)
                if first_ev:
                    p.w(nc.vector, s["free_stage"][bi])
                    first_ev = False
                ev = p.inc(nc.vector.tensor_copy(
                    out=stage[bi][:, qb * 4:(qb + 1) * 4, :].rearrange("p c f -> p (c f)"),
                    in_=pb[:]), "dve")
                s["free_pst"][qb % 2] = ("dve", ev[1])
            s["free_xn"][bi] = ("pe", mtok[1])
            p.w(nc.sync, ev)
            sdone = p.dma(nc.sync, "sy", out=dst3[:, :, i * 128:(i + 1) * 128],
                          in_=stage[bi][:])
            s["free_stage"][bi] = sdone
            store_tok[(kind, i)] = sdone
            s["jj"] += 1

        xnf3 = xnf_s.rearrange("(c p) t -> p c t", p=128)
        kvstate = {"free_xn": [None, None], "free_kst": [None] * 2,
                   "free_pk": [None, None], "ks": 0}

        def kv_emit(kind, tb):
            s = kvstate
            bi = tb % 2
            p.w(nc.sync, s["free_xn"][bi])
            for i in range(4 * tb, 4 * tb + 4):
                p.w(nc.sync, store_tok[("f", i)])
            xtok = p.dma(nc.sync, "sy", out=xnb[bi][:],
                         in_=xnf3[:, :, tb * 512:(tb + 1) * 512])
            for dch in range(8):
                pbi = dch % 2
                pb = pk[pbi]
                p.w(nc.tensor, xtok)
                if tb == 0 and dch == 0:
                    p.w(nc.tensor, wtok)
                p.w(nc.tensor, s["free_pk"][pbi])
                if kind == 0:
                    for c in range(HC):
                        mm = nc.tensor.matmul(out=pb[:],
                                              lhsT=wbuf[:, c, dch * 128:(dch + 1) * 128],
                                              rhs=xnb[bi][:, c, :],
                                              start=(c == 0), stop=(c == HC - 1))
                else:
                    tksub, dvb = dch // 2, dch % 2
                    for c in range(HC):
                        mm = nc.tensor.matmul(out=pb[:],
                                              lhsT=xnb[bi][:, c, tksub * 128:(tksub + 1) * 128],
                                              rhs=wbuf[:, c, dvb * 512:(dvb + 1) * 512],
                                              start=(c == 0), stop=(c == HC - 1))
                mtok = p.inc(mm, "pe")
                if dch == 7:
                    s["free_xn"][bi] = ("pe", mtok[1])
                sbi = s["ks"] % 2
                p.w(nc.vector, mtok)
                p.w(nc.vector, s["free_kst"][sbi])
                ev = p.inc(nc.vector.tensor_copy(out=kst[sbi][:], in_=pb[:]), "dve")
                s["free_pk"][pbi] = ("dve", ev[1])
                p.w(nc.sync, ev)
                if kind == 0:
                    dst = k_s[dch * 128:(dch + 1) * 128, tb * 512:(tb + 1) * 512]
                else:
                    row0 = tb * 512 + tksub * 128
                    dst = v_s[row0:row0 + 128, dvb * 512:(dvb + 1) * 512]
                sd = p.dma(nc.sync, "sy", out=dst, in_=kst[sbi][:])
                s["free_kst"][sbi] = sd
                s["ks"] += 1

        for _ in range(8):
            ln_emit_one()
        for tb in range(8):
            for _ in range(5):
                ln_emit_one()
            kv_emit(0, tb)
        p.w(nc.gpsimd, ("pe", p.cnt["pe"]))
        wtok = p.dma(nc.gpsimd, "gp", out=wbuf[:],
                     in_=wv.rearrange("(c p) d -> p c d", p=128))
        kvstate["free_xn"] = [("pe", p.cnt["pe"])] * 2
        for tb in range(8):
            while ln_jobs:
                ln_emit_one()
            p.w(nc.tensor, wtok)
            kv_emit(1, tb)
        bar2 = p.snap()
        st.close()
        return bar2

    def phase_flash(bar):
        st = ExitStack()
        k_sb = p.sb("p3k", [128, S], BF16, st)
        v_sb = p.sb("p3v", [128, NTF, 128], BF16, st)
        q_h = [p.sb(f"p3q{i}", [128, TM], BF16, st) for i in range(2)]
        mask_sb = p.sb("p3m", [128, 4, 1024], BF16, st)
        probs = [p.sb(f"p3pr{i}", [128, 1024], BF16, st) for i in range(2)]
        tmp_t = p.sb("p3tt", [128, 1024], F32, st)
        binv = p.sb("p3bi", [128, 512], F32, st)
        acc = p.sb("p3acc", [128, 1024], F32, st)
        ones_mat = p.sb("ones_mat", [128, 128], F32, st)
        o_tile = [p.sb(f"p3o{i}", [128, 512], BF16, st) for i in range(2)]

        psc = [p.ps(f"p3c{i}", [128, 1024], F32, st) for i in range(2)]
        pO = [p.ps(f"p3O{i}", [128, 512], F32, st) for i in range(2)]
        pD = p.ps("p3pd", [128, 512], F32, st)

        p.wall_all(bar)
        p.inc(nc.vector.memset(ones_mat[:], 1.0), "dve")
        msk = p.dma(nc.sync, "sy", out=mask_sb[:],
                                      in_=fmask.rearrange("g p f -> p g f"))

        free_k = None
        free_v = None
        free_qh = [None, None]
        qh_toks = [None, None]
        free_probs = [None, None]
        free_probs2 = [None, None]
        free_psc = [None, None]
        free_pO = [None, None]
        free_otile = [None, None]
        free_pD = [None]
        acc_free = [None]
        last_exp = None
        v4 = v_s.rearrange("(c p) d -> p c d", p=128)
        ktok = vtok = None

        def qh_load(h):
            b = h % 2
            p.w(nc.sync, free_qh[b])
            qh_toks[b] = p.dma(nc.sync, "sy", out=q_h[b][:],
                               in_=q_s[h * 128:(h + 1) * 128, :])

        qh_load(0)
        qh_load(1)
        for h in range(NH):
            kvh = h // 4
            if h % 4 == 0:
                p.w(nc.sync, free_k)
                ktok = p.dma(nc.sync, "sy", out=k_sb[:],
                                               in_=k_s[kvh * 128:(kvh + 1) * 128, :])
                p.w(nc.sync, free_v)
                vtok = p.dma(nc.sync, "sy", out=v_sb[:],
                                               in_=v4[:, :, kvh * 128:(kvh + 1) * 128])
            qbi = h % 2
            q_sb = q_h[qbi]
            qready = qh_toks[qbi]

            units = [(g, u) for g in range(4) for u in range(4 * g + 4)]
            pend_pv = None
            pend_norm = None
            rtoks = {}
            acc_done = {}

            def emit_pv(g, u, pbi, ptok):
                npair = 4 * g + 4
                ob = g % 2
                p.w(nc.tensor, ptok)
                if u == 0:
                    p.w(nc.tensor, free_pO[ob])
                    if h % 4 == 0 and g == 0:
                        p.w(nc.tensor, vtok)
                nc.tensor.matmul(out=pO[ob][:], lhsT=v_sb[:, 2 * u, :],
                                 rhs=probs[pbi][:, 0:512], start=(u == 0), stop=False)
                pvm = nc.tensor.matmul(out=pO[ob][:], lhsT=v_sb[:, 2 * u + 1, :],
                                       rhs=probs[pbi][:, 512:1024],
                                       start=False, stop=(u == npair - 1))
                mtok2 = p.inc(pvm, "pe")
                free_probs[pbi] = ("pe", mtok2[1])
                if u == npair - 1:
                    p.w(nc.tensor, acc_done[g])
                    p.w(nc.tensor, free_pD[0])
                    nc.tensor.matmul(out=pD[:], lhsT=ones_mat[:], rhs=acc[:, 0:512],
                                     start=True, stop=False)
                    rmm = nc.tensor.matmul(out=pD[:], lhsT=ones_mat[:],
                                           rhs=acc[:, 512:1024], start=False, stop=True)
                    rtok_pe = p.inc(rmm, "pe")
                    acc_free[0] = ("pe", rtok_pe[1])
                    p.w(nc.vector, rtok_pe)
                    rtoks[g] = p.inc(nc.vector.reciprocal(out=binv[:], in_=pD[:]), "dve")
                    free_pD[0] = ("dve", rtoks[g][1])
                    return True
                return False

            def flush_norm(g):
                ob = g % 2
                rtok = rtoks[g]
                obi = g % 2
                p.w(nc.vector, rtok)
                p.w(nc.vector, free_otile[obi])
                otok = p.inc(nc.vector.tensor_tensor(out=o_tile[obi][:], in0=pO[ob][:],
                                                     in1=binv[:], op=ALU.mult), "dve")
                p.w(nc.sync, otok)
                sd = p.dma(nc.sync, "sy",
                           out=o_s[h * 128:(h + 1) * 128, g * 512:(g + 1) * 512],
                           in_=o_tile[obi][:])
                return ("dve", otok[1]), sd

            for g, u in units:
                sbi = u % 2
                pp = psc[sbi]
                p.w(nc.tensor, free_psc[sbi])
                if u == 0 and g == 0:
                    p.w(nc.tensor, qready)
                    if h % 4 == 0:
                        p.w(nc.tensor, ktok)
                mm0 = nc.tensor.matmul(out=pp[:, 0:512],
                                       lhsT=k_sb[:, (2 * u) * 128:(2 * u + 1) * 128],
                                       rhs=q_sb[:, g * 512:(g + 1) * 512],
                                       start=True, stop=True)
                mm1 = nc.tensor.matmul(out=pp[:, 512:1024],
                                       lhsT=k_sb[:, (2 * u + 1) * 128:(2 * u + 2) * 128],
                                       rhs=q_sb[:, g * 512:(g + 1) * 512],
                                       start=True, stop=True)
                sctok = p.inc(mm1, "pe")
                if pend_pv is not None:
                    done = emit_pv(*pend_pv)
                    if done:
                        pend_norm = pend_pv[0]
                    pend_pv = None
                if pend_norm is not None and u >= 1:
                    fo, fs = flush_norm(pend_norm)
                    free_pO[pend_norm % 2] = fo
                    free_otile[pend_norm % 2] = fs
                    pend_norm = None
                p.w(nc.scalar, sctok)
                if last_exp is not None:
                    p.w(nc.scalar, last_exp)
                t1 = p.inc(nc.scalar.activation(out=tmp_t[:], in_=pp[:], func=AF.Tanh,
                                                scale=ATTN_MULT / CAP), "act")
                free_psc[sbi] = ("act", t1[1])
                p.w(nc.scalar, t1)
                pbi = u % 2
                p.w(nc.scalar, free_probs[pbi])
                p.w(nc.scalar, free_probs2[pbi])
                t2 = p.inc(nc.scalar.activation(out=probs[pbi][:], in_=tmp_t[:], func=AF.Exp,
                                                scale=CAP, bias=b30[:, 0:1]), "act")
                last_exp = t2
                ptok = t2
                if u >= 4 * g:
                    mp = u - 4 * g
                    p.w(nc.vector, t2)
                    if u == 0 and g == 0 and h == 0:
                        p.w(nc.vector, msk)
                    ptok = p.inc(nc.vector.tensor_tensor(out=probs[pbi][:], in0=probs[pbi][:],
                                                         in1=mask_sb[:, mp, :],
                                                         op=ALU.mult), "dve")
                p.w(nc.vector, ptok)
                if u == 0:
                    p.w(nc.vector, acc_free[0])
                    at = p.inc(nc.vector.tensor_copy(out=acc[:], in_=probs[pbi][:]), "dve")
                else:
                    at = p.inc(nc.vector.tensor_tensor(out=acc[:], in0=acc[:],
                                                       in1=probs[pbi][:], op=ALU.add), "dve")
                free_probs2[pbi] = ("dve", at[1])
                if u == 4 * g + 3:
                    acc_done[g] = ("dve", at[1])
                pend_pv = (g, u, pbi, ptok)
            done = emit_pv(*pend_pv)
            pend_pv = None
            if pend_norm is not None:
                fo, fs = flush_norm(pend_norm)
                free_pO[pend_norm % 2] = fo
                free_otile[pend_norm % 2] = fs
            fo, fs = flush_norm(3)
            free_pO[3 % 2] = fo
            free_otile[3 % 2] = fs
            free_qh[qbi] = ("pe", p.cnt["pe"])
            if h + 2 < NH:
                qh_load(h + 2)
            if h % 4 == 3:
                free_k = ("pe", p.cnt["pe"])
                free_v = ("pe", p.cnt["pe"])
        bar2 = p.snap()
        st.close()
        return bar2

    def phase_gateup(bar):
        st = ExitStack()
        scg = p.sb("p4scg", [128, OC], F32, st)
        scu = p.sb("p4scu", [128, OC], F32, st)
        xnm_h = p.sb("p4xn", [128, HC, TM], BF16, st)

        p.wall_all(bar)
        xnm3 = xnm_s.rearrange("(c p) t -> p c t", p=128)
        xtok = p.dma(nc.gpsimd, "gp", out=xnm_h[:], in_=xnm3)
        t = p.dma(nc.sync, "sy", out=scg[:], in_=gscT[:])
        t2_ = p.dma(nc.sync, "sy", out=scu[:], in_=uscT[:])
        p.w(nc.vector, t2_)
        t1 = p.inc(nc.vector.tensor_scalar(out=scg[:], in0=scg[:], scalar1=QSTEP, scalar2=1e-30,
                                           op0=ALU.mult, op1=ALU.max), "dve")
        p.w(nc.vector, t1)
        t1 = p.inc(nc.vector.tensor_scalar(out=scu[:], in0=scu[:], scalar1=QSTEP, scalar2=1e-30,
                                           op0=ALU.mult, op1=ALU.max), "dve")
        scready = t1

        with ExitStack() as st2:
            gb_tmp = p.sb("p4gbt", [32, IM], F32, st2)
            srow = [p.sb(f"p4srow{i}", [1, 512], F32, st2) for i in range(2)]
            bs_tmp = [p.sb(f"p4bst{i}", [33, 512], BF16, st2) for i in range(2)]
            pbc = p.ps("p4bc", [32, 512], F32, st2)
            pdum = p.ps("p4dum", [32, 16], F32, st2)
            for mi, (scsrc, Bsrc, Bdst_s) in enumerate(((gsc, gB, bsg_s), (usc, uB, bsu_s))):
                if mi > 0:
                    p.wall_all()
                tb_ = p.dma(nc.sync, "sy", out=gb_tmp[:], in_=Bsrc[:])
                p.inc(nc.vector.memset(bs_tmp[0][32:33, :], 1.0), "dve")
                p.inc(nc.vector.memset(bs_tmp[1][32:33, :], 1.0), "dve")
                free_bc = None
                free_sr = [None, None]
                free_bt = [None, None]
                for ch in range((IM + 511) // 512):
                    c0, c1 = ch * 512, min((ch + 1) * 512, IM)
                    n = c1 - c0
                    cbi = ch % 2
                    p.w(nc.sync, free_sr[cbi])
                    t = p.dma(nc.sync, "sy", out=srow[cbi][:, 0:n],
                                                in_=scsrc[None, c0:c1])
                    p.w(nc.vector, t)
                    t1 = p.inc(nc.vector.tensor_scalar(out=srow[cbi][:, 0:n],
                                                       in0=srow[cbi][:, 0:n],
                                                       scalar1=QSTEP, scalar2=1e-30,
                                                       op0=ALU.mult, op1=ALU.max), "dve")
                    p.w(nc.vector, t1)
                    t2 = p.inc(nc.vector.reciprocal(out=srow[cbi][:, 0:n],
                                                    in_=srow[cbi][:, 0:n]), "dve")
                    p.w(nc.tensor, t2)
                    p.w(nc.tensor, free_bc)
                    nc.tensor.matmul(out=pbc[:, 0:n], lhsT=ones_r32[:],
                                     rhs=srow[cbi][:, 0:n], start=True, stop=True)
                    dmm = nc.tensor.matmul(out=pdum[:], lhsT=ones_r32[:],
                                           rhs=srow[cbi][:, 0:16], start=True, stop=True)
                    btok = p.inc(dmm, "pe")
                    free_sr[cbi] = ("pe", btok[1])
                    p.w(nc.vector, btok)
                    p.w(nc.vector, tb_)
                    p.w(nc.vector, free_bt[cbi])
                    ev = p.inc(nc.vector.scalar_tensor_tensor(
                        out=bs_tmp[cbi][0:32, 0:n], in0=pbc[:, 0:n], scalar=LRS,
                        in1=gb_tmp[:, c0:c1], op0=ALU.mult, op1=ALU.mult), "dve")
                    free_bc = ("dve", ev[1])
                    p.w(nc.sync, ev)
                    bt = p.dma(nc.sync, "sy", out=Bdst_s[:, c0:c1], in_=bs_tmp[cbi][:, 0:n])
                    free_bt[cbi] = bt
            p.wall_all()

        wgT = [p.sb(f"p4wgT{i}", [128, HC, 128], BF16, st) for i in range(2)]
        wuT = [p.sb(f"p4wuT{i}", [128, HC, 128], BF16, st) for i in range(2)]
        bsg_t = [p.sb(f"p4bsgt{i}", [33, 128], BF16, st) for i in range(2)]
        bsu_t = [p.sb(f"p4bsut{i}", [33, 128], BF16, st) for i in range(2)]
        gs_t = [p.sb(f"p4g{i}", [128, 512], BF16, st) for i in range(4)]
        us_t = [p.sb(f"p4u{i}", [128, 512], BF16, st) for i in range(4)]
        hstage = [p.sb(f"p4h{i}", [128, 4, 512], BF16, st) for i in range(2)]
        wqb = [p.sb(f"p4wq{i}", [128, HC, 128], BF16, st) for i in range(2)]
        q_stage = [p.sb(f"p4qs{i}", [128, 512], BF16, st) for i in range(2)]
        pg = p.ps("p4pg", [128, 2048], F32, st)
        pu = p.ps("p4pu", [128, 2048], F32, st)

        last_ev = None
        with ExitStack() as st3:
            agb = p.sb("p4ag", [128, HC, 33], BF16, st3)
            for mi, (Asrc, r33) in enumerate(((gA, rhs33_g), (uA, rhs33_u))):
                if mi > 0:
                    p.w(nc.gpsimd, ("pe", p.cnt["pe"]))
                atok = p.dma(nc.gpsimd, "gp", out=agb[:, :, 0:32],
                                             in_=Asrc.rearrange("(c p) r -> p c r", p=128))
                p.w(nc.vector, atok)
                otok = p.inc(nc.vector.memset(agb[:, :, 32:33], 1.0), "dve")
                for tb in range(4):
                    bo = (tb % 2) * 512
                    p.w(nc.tensor, xtok)
                    p.w(nc.tensor, otok)
                    if last_ev is not None:
                        p.w(nc.tensor, last_ev)
                    for c in range(HC):
                        mm = nc.tensor.matmul(out=pg[0:33, bo:bo + 512],
                                              lhsT=agb[:, c, :],
                                              rhs=xnm_h[:, c, tb * 512:(tb + 1) * 512],
                                              start=(c == 0), stop=(c == HC - 1))
                    mtok = p.inc(mm, "pe")
                    p.w(nc.scalar, mtok)
                    last_ev = p.inc(nc.scalar.mul(r33[:, tb * 512:(tb + 1) * 512],
                                                  pg[0:33, bo:bo + 512], vec33[:, 0:1]), "act")
            p.wall_all()
        pgu_done = last_ev

        wq4 = wq.rearrange("(c p) o -> p c o", p=128)
        q_jobs = [(h, tb) for h in range(NH) for tb in range(4)]
        qst = {"free_wqb": [None, None], "wq_toks": [None, None],
               "free_qs": [None, None], "qdone": 0, "q_pe": [None] * 4}

        def qload(h):
            b = h % 2
            p.w(nc.gpsimd, qst["free_wqb"][b])
            qst["wq_toks"][b] = p.dma(nc.gpsimd, "gp", out=wqb[b][:],
                                      in_=wq4[:, :, h * 128:(h + 1) * 128])

        qload(0)
        qload(1)

        def q_emit_one(gate_pend):
            if not q_jobs:
                return
            h, tb = q_jobs.pop(0)
            b = h % 2
            sbq = qst["qdone"] % 2
            bo = tb * 512
            p.w(nc.tensor, qst["wq_toks"][b])
            p.w(nc.tensor, gate_pend)
            p.w(nc.tensor, qst["q_pe"][tb])
            for c in range(HC):
                mm = nc.tensor.matmul(out=pg[:, bo:bo + 512], lhsT=wqb[b][:, c, :],
                                      rhs=xnm_h[:, c, bo:bo + 512],
                                      start=(c == 0), stop=(c == HC - 1))
            mtok = p.inc(mm, "pe")
            if tb == 3:
                qst["free_wqb"][b] = ("pe", mtok[1])
                if h + 2 < NH:
                    qload(h + 2)
            p.w(nc.vector, mtok)
            p.w(nc.vector, qst["free_qs"][sbq])
            ev = p.inc(nc.vector.tensor_copy(out=q_stage[sbq][:], in_=pg[:, bo:bo + 512]),
                       "dve")
            qst["q_pe"][tb] = ("dve", ev[1])
            p.w(nc.sync, ev)
            sd = p.dma(nc.sync, "sy", out=q_s[h * 128:(h + 1) * 128, bo:bo + 512],
                       in_=q_stage[sbq][:])
            qst["free_qs"][sbq] = sd
            qst["qdone"] += 1

        wg_toks = [None, None]
        wu_toks = [None, None]
        bg_toks = [None, None]
        bu_toks = [None, None]
        free_wg = [None, None]
        free_wu = [None, None]
        free_bg = [None, None]
        free_bu = [None, None]
        free_pg = None
        free_pu = None
        free_gs = [None] * 4
        free_us = [None] * 4
        free_h = [None, None]

        def tload(oc):
            b = oc % 2
            p.w(nc.gpsimd, free_wg[b])
            wg_toks[b] = p.dma(nc.gpsimd, "gp", out=wgT[b][:], in_=gqL[:, oc, :, :])
            p.w(nc.gpsimd, free_wu[b])
            wu_toks[b] = p.dma(nc.gpsimd, "gp", out=wuT[b][:], in_=uqL[:, oc, :, :])
            p.w(nc.sync, free_bg[b])
            bg_toks[b] = p.dma(nc.sync, "sy", out=bsg_t[b][:],
                               in_=bsg_s[:, oc * 128:(oc + 1) * 128])
            p.w(nc.sync, free_bu[b])
            bu_toks[b] = p.dma(nc.sync, "sy", out=bsu_t[b][:],
                               in_=bsu_s[:, oc * 128:(oc + 1) * 128])

        tload(0)
        tload(1)
        for oc in range(OC):
            b = oc % 2
            p.w(nc.tensor, wg_toks[b])
            p.w(nc.tensor, bg_toks[b])
            p.w(nc.tensor, free_pg)
            for qtok in qst["q_pe"]:
                p.w(nc.tensor, qtok)
            if oc == 0:
                p.w(nc.tensor, pgu_done)
            for tb in range(4):
                nc.tensor.matmul(out=pg[:, tb * 512:(tb + 1) * 512], lhsT=bsg_t[b][:],
                                 rhs=rhs33_g[:, tb * 512:(tb + 1) * 512],
                                 start=True, stop=False)
            for c in range(HC):
                for tb in range(4):
                    mmg = nc.tensor.matmul(out=pg[:, tb * 512:(tb + 1) * 512],
                                           lhsT=wgT[b][:, c, :],
                                           rhs=xnm_h[:, c, tb * 512:(tb + 1) * 512],
                                           start=False, stop=(c == HC - 1))
            gtok = p.inc(mmg, "pe")
            free_wg[b] = ("pe", gtok[1])
            free_bg[b] = ("pe", gtok[1])
            p.w(nc.scalar, gtok)
            if oc == 0:
                p.w(nc.scalar, scready)
            for tb in range(4):
                p.w(nc.scalar, free_gs[tb])
                ge = p.inc(nc.scalar.activation(out=gs_t[tb][:],
                                                in_=pg[:, tb * 512:(tb + 1) * 512],
                                                func=AF.Silu, scale=scg[:, oc:oc + 1]), "act")
            free_pg = ("act", ge[1])
            p.w(nc.tensor, wu_toks[b])
            p.w(nc.tensor, bu_toks[b])
            p.w(nc.tensor, free_pu)
            for tb in range(4):
                nc.tensor.matmul(out=pu[:, tb * 512:(tb + 1) * 512], lhsT=bsu_t[b][:],
                                 rhs=rhs33_u[:, tb * 512:(tb + 1) * 512],
                                 start=True, stop=False)
            for c in range(HC):
                for tb in range(4):
                    mmu = nc.tensor.matmul(out=pu[:, tb * 512:(tb + 1) * 512],
                                           lhsT=wuT[b][:, c, :],
                                           rhs=xnm_h[:, c, tb * 512:(tb + 1) * 512],
                                           start=False, stop=(c == HC - 1))
            utok = p.inc(mmu, "pe")
            free_wu[b] = ("pe", utok[1])
            free_bu[b] = ("pe", utok[1])
            if oc + 2 < OC:
                tload(oc + 2)
            p.w(nc.scalar, utok)
            hb = oc % 2
            htok = None
            for tb in range(4):
                p.w(nc.scalar, free_us[tb])
                ue = p.inc(nc.scalar.mul(us_t[tb][:], pu[:, tb * 512:(tb + 1) * 512],
                                         scu[:, oc:oc + 1]), "act")
                p.w(nc.vector, ue)
                p.w(nc.vector, ("act", ge[1]))
                if tb == 0:
                    p.w(nc.vector, free_h[hb])
                htok = p.inc(nc.vector.tensor_tensor(out=hstage[hb][:, tb, :],
                                                     in0=gs_t[tb][:], in1=us_t[tb][:],
                                                     op=ALU.mult), "dve")
                free_gs[tb] = ("dve", htok[1])
                free_us[tb] = ("dve", htok[1])
            free_pu = ("act", p.cnt["act"])
            p.w(nc.sync, htok)
            sd = p.dma(nc.sync, "sy", out=h_s[oc * 128:(oc + 1) * 128, :],
                       in_=hstage[hb][:].rearrange("p c f -> p (c f)"))
            free_h[hb] = sd
            while qst["qdone"] < (oc + 1) * (4 * NH) // OC and q_jobs:
                q_emit_one(free_pg)
        while q_jobs:
            q_emit_one(free_pg)
        bar2 = p.snap()
        st.close()
        return bar2

    def phase_down(bar):
        st = ExitStack()
        scd = p.sb("p5scd", [128, HC], F32, st)
        rhs33_d = p.sb("rhs33_d", [33, TM], BF16, st)

        p.wall_all(bar)
        t = p.dma(nc.sync, "sy", out=scd[:], in_=dscT[:])
        p.w(nc.vector, t)
        sctok = p.inc(nc.vector.tensor_scalar(out=scd[:], in0=scd[:], scalar1=QSTEP,
                                              scalar2=1e-30, op0=ALU.mult, op1=ALU.max), "dve")
        with ExitStack() as st2:
            db_tmp = p.sb("p5dbt", [32, HID], F32, st2)
            srow = [p.sb(f"p5srow{i}", [1, 512], F32, st2) for i in range(2)]
            bs_tmp = [p.sb(f"p5bst{i}", [33, 512], BF16, st2) for i in range(2)]
            pbc = p.ps("p5bc", [32, 512], F32, st2)
            pdum = p.ps("p5dum", [32, 16], F32, st2)
            tb_ = p.dma(nc.sync, "sy", out=db_tmp[:], in_=dB[:])
            p.inc(nc.vector.memset(bs_tmp[0][32:33, :], 1.0), "dve")
            p.inc(nc.vector.memset(bs_tmp[1][32:33, :], 1.0), "dve")
            free_bc = None
            free_sr = [None, None]
            free_bt = [None, None]
            for ch in range(HID // 512):
                cbi = ch % 2
                p.w(nc.sync, free_sr[cbi])
                t = p.dma(nc.sync, "sy", out=srow[cbi][:],
                                            in_=dsc[None, ch * 512:(ch + 1) * 512])
                p.w(nc.vector, t)
                t1 = p.inc(nc.vector.tensor_scalar(out=srow[cbi][:], in0=srow[cbi][:],
                                                   scalar1=QSTEP, scalar2=1e-30,
                                                   op0=ALU.mult, op1=ALU.max), "dve")
                p.w(nc.vector, t1)
                t2 = p.inc(nc.vector.reciprocal(out=srow[cbi][:], in_=srow[cbi][:]), "dve")
                p.w(nc.tensor, t2)
                p.w(nc.tensor, free_bc)
                nc.tensor.matmul(out=pbc[:], lhsT=ones_r32[:],
                                 rhs=srow[cbi][:], start=True, stop=True)
                dmm = nc.tensor.matmul(out=pdum[:], lhsT=ones_r32[:],
                                       rhs=srow[cbi][:, 0:16], start=True, stop=True)
                btok = p.inc(dmm, "pe")
                free_sr[cbi] = ("pe", btok[1])
                p.w(nc.vector, btok)
                p.w(nc.vector, tb_)
                p.w(nc.vector, free_bt[cbi])
                ev = p.inc(nc.vector.scalar_tensor_tensor(
                    out=bs_tmp[cbi][0:32, :], in0=pbc[:], scalar=LRS,
                    in1=db_tmp[:, ch * 512:(ch + 1) * 512], op0=ALU.mult, op1=ALU.mult), "dve")
                free_bc = ("dve", ev[1])
                p.w(nc.sync, ev)
                bt = p.dma(nc.sync, "sy", out=bsd_s[:, ch * 512:(ch + 1) * 512],
                           in_=bs_tmp[cbi][:])
                free_bt[cbi] = bt
            p.wall_all()
        bsready = ("pe", p.cnt["pe"])

        h_sb = p.sb("p5h", [128, OC, 1024], BF16, st)
        wdT = [p.sb(f"p5wdT{i}", [128, 22, 128], BF16, st) for i in range(2)]
        bs_t = [p.sb(f"p5bt{i}", [33, 128], BF16, st) for i in range(2)]
        fsum = [p.sb("p5f0", [128, 512], F32, st) for i in range(1)]
        dnA_sb = p.sb("p5dnA", [128, OC, 33], BF16, st)
        pd = [p.ps(f"p5d{i}", [128, 1024], F32, st) for i in range(2)]
        pP33 = p.ps("p5P33", [33, 1024], F32, st)

        t = p.dma(nc.gpsimd, "gp", out=dnA_sb[:, :, 0:32],
                                      in_=dA.rearrange("(c p) r -> p c r", p=128))
        p.w(nc.vector, t)
        dnatok = p.inc(nc.vector.memset(dnA_sb[:, :, 32:33], 1.0), "dve")

        qlist = [(0, 22), (22, 22), (44, 22), (66, 20)]
        h3 = h_s.rearrange("(c p) t -> p c t", p=128)
        r33evac = None
        for half in range(2):
            p.wall_all()
            htoks = []
            for cq in range(4):
                c0, c1 = cq * 22, min((cq + 1) * 22, OC)
                t = p.dma(nc.sync, "sy",
                    out=h_sb[:, c0:c1, :],
                    in_=h3[:, c0:c1, half * 1024:(half + 1) * 1024])
                htoks.append(t)
            for tt in htoks:
                p.w(nc.tensor, tt)
            p.w(nc.tensor, dnatok)
            if r33evac is not None:
                p.w(nc.tensor, r33evac)
            for poc in range(OC):
                for nb in range(2):
                    mmp = nc.tensor.matmul(out=pP33[0:33, nb * 512:(nb + 1) * 512],
                                           lhsT=dnA_sb[:, poc, :],
                                           rhs=h_sb[:, poc, nb * 512:(nb + 1) * 512],
                                           start=(poc == 0), stop=(poc == OC - 1))
            ptok = p.inc(mmp, "pe")
            p.w(nc.scalar, ptok)
            r33evac = p.inc(nc.scalar.mul(rhs33_d[:, half * 1024:(half + 1) * 1024],
                                          pP33[0:33, :], vec33[:, 0:1]), "act")
            free_wdT = [None, None]
            free_bs = [None, None]
            free_pd = [None, None]
            free_fs = [None, None]
            wq_toks = [None, None]
            bs_toks = [None, None]

            def wload(ho, qi, buf):
                so, sn = qlist[qi]
                p.w(nc.gpsimd, free_wdT[buf])
                wq_toks[buf] = p.dma(nc.gpsimd, "gp", out=wdT[buf][:, 0:sn, :],
                                     in_=dqL[:, ho, so:so + sn, :])

            def bsload(ho, buf):
                p.w(nc.sync, free_bs[buf])
                bs_toks[buf] = p.dma(nc.sync, "sy", out=bs_t[buf][:],
                                     in_=bsd_s[:, ho * 128:(ho + 1) * 128])

            wload(0, 0, 0)
            bsload(0, 0)
            for ho in range(HC):
                pbi = ho % 2
                p.w(nc.tensor, free_pd[pbi])
                if ho == 0:
                    for tt in htoks:
                        p.w(nc.tensor, tt)
                    p.w(nc.tensor, bsready)
                for qi in range(4):
                    buf = qi % 2
                    so, sn = qlist[qi]
                    if qi < 3:
                        wload(ho, qi + 1, 1 - buf)
                    elif ho + 1 < HC:
                        wload(ho + 1, 0, 1 - buf)
                        bsload(ho + 1, 1 - (ho % 2))
                    p.w(nc.tensor, wq_toks[buf])
                    for k in range(sn):
                        oc = so + k
                        for nb in range(2):
                            mmd = nc.tensor.matmul(
                                out=pd[pbi][:, nb * 512:(nb + 1) * 512],
                                lhsT=wdT[buf][:, k, :],
                                rhs=h_sb[:, oc, nb * 512:(nb + 1) * 512],
                                start=(oc == 0), stop=False)
                    dtok = p.inc(mmd, "pe")
                    free_wdT[buf] = ("pe", dtok[1])
                bbuf = ho % 2
                p.w(nc.tensor, bs_toks[bbuf])
                p.w(nc.tensor, r33evac)
                for nb in range(2):
                    mmd = nc.tensor.matmul(out=pd[pbi][:, nb * 512:(nb + 1) * 512],
                                           lhsT=bs_t[bbuf][:],
                                           rhs=rhs33_d[:, half * 1024 + nb * 512:
                                                       half * 1024 + (nb + 1) * 512],
                                           start=False, stop=True)
                dtok = p.inc(mmd, "pe")
                free_bs[bbuf] = ("pe", dtok[1])
                p.w(nc.scalar, dtok)
                if ho == 0 and half == 0:
                    p.w(nc.scalar, sctok)
                for nb in range(2):
                    p.w(nc.scalar, free_fs[0])
                    ev = p.inc(nc.scalar.mul(fsum[0][:], pd[pbi][:, nb * 512:(nb + 1) * 512],
                                             scd[:, ho:ho + 1]), "act")
                    p.w(nc.sync, ev)
                    sd = p.dma(nc.sync, "sy",
                        out=ffn_s[ho * 128:(ho + 1) * 128,
                                  half * 1024 + nb * 512:half * 1024 + (nb + 1) * 512],
                        in_=fsum[0][:])
                    free_fs[0] = sd
                free_pd[pbi] = ("act", p.cnt["act"])
        bar2 = p.snap()
        st.close()
        return bar2

    def phase_wo(bar):
        st = ExitStack()
        identf_sb = p.sb("identf_sb", [128, 128], F32, st)
        o_sb = p.sb("p6o", [128, HC, TM], BF16, st)
        wob = [p.sb(f"p6w{i}", [128, HC, 128], BF16, st) for i in range(2)]
        ffn_t = [p.sb(f"p6f{i}", [128, 512], F32, st) for i in range(2)]
        x_t = [p.sb(f"p6x{i}", [128, 4, 128], F32, st) for i in range(2)]
        s1 = [p.sb(f"p6s{i}", [128, 512], F32, st) for i in range(2)]
        outb = [p.sb(f"p6ob{i}", [128, 4, 128], F32, st) for i in range(2)]
        pa = [p.ps(f"p6a{i}", [128, 512], F32, st) for i in range(2)]
        pt = [p.ps(f"p6t{i}", [128, 512], F32, st) for i in range(2)]

        p.wall_all(bar)
        iftok = p.dma(nc.sync, "sy", out=identf_sb[:], in_=identf[:])
        o4 = o_s.rearrange("(c p) t -> p c t", p=128)
        wo4 = wo.rearrange("(c p) o -> p c o", p=128)
        xm4 = xm.rearrange("(tt p) (c f) -> p tt c f", p=128, f=128)
        out4 = out.rearrange("(tt p) (c f) -> p tt c f", p=128, f=128)
        otoks = []
        for cq in range(4):
            t = p.dma(nc.sync, "sy", out=o_sb[:, cq * 8:(cq + 1) * 8, :],
                                        in_=o4[:, cq * 8:(cq + 1) * 8, :])
            otoks.append(t)
        free_wo = [None, None]
        free_ffn = [None, None]
        free_x = [None, None]
        free_s1 = [None, None]
        free_out = [None, None]
        free_pa = [None, None]
        free_pt = [None, None]
        for oc in range(HC):
            wbi = oc % 2
            p.w(nc.gpsimd, free_wo[wbi])
            wt = p.dma(nc.gpsimd, "gp", out=wob[wbi][:],
                                           in_=wo4[:, :, oc * 128:(oc + 1) * 128])
            for tb in range(4):
                pbi = tb % 2
                fbi = (oc * 4 + tb) % 2
                p.w(nc.sync, free_ffn[fbi])
                ftok = p.dma(nc.sync, "sy", 
                    out=ffn_t[fbi][:],
                    in_=ffn_s[oc * 128:(oc + 1) * 128, tb * 512:(tb + 1) * 512])
                p.w(nc.sync, free_x[fbi])
                xtok = p.dma(nc.sync, "sy", out=x_t[fbi][:],
                                               in_=xm4[:, tb * 4:(tb + 1) * 4, oc, :])
                p.w(nc.tensor, wt)
                if oc == 0 and tb == 0:
                    for tt in otoks:
                        p.w(nc.tensor, tt)
                p.w(nc.tensor, free_pa[pbi])
                for c in range(HC):
                    mm = nc.tensor.matmul(out=pa[pbi][:], lhsT=wob[wbi][:, c, :],
                                          rhs=o_sb[:, c, tb * 512:(tb + 1) * 512],
                                          start=(c == 0), stop=(c == HC - 1))
                atok = p.inc(mm, "pe")
                sbi = (oc * 4 + tb) % 2
                p.w(nc.vector, atok)
                p.w(nc.vector, ftok)
                p.w(nc.vector, free_s1[sbi])
                stok = p.inc(nc.vector.tensor_tensor(out=s1[sbi][:], in0=pa[pbi][:],
                                                     in1=ffn_t[fbi][:], op=ALU.add), "dve")
                free_pa[pbi] = ("dve", stok[1])
                free_ffn[fbi] = ("dve", stok[1])
                p.w(nc.tensor, stok)
                p.w(nc.tensor, iftok)
                p.w(nc.tensor, free_pt[pbi])
                for k in range(4):
                    mm = nc.tensor.matmul(out=pt[pbi][:, k * 128:(k + 1) * 128],
                                          lhsT=s1[sbi][:, k * 128:(k + 1) * 128],
                                          rhs=identf_sb[:], start=True, stop=True)
                ttok = p.inc(mm, "pe")
                free_s1[sbi] = ("pe", ttok[1])
                p.w(nc.vector, ttok)
                p.w(nc.vector, xtok)
                p.w(nc.vector, free_out[fbi])
                ot = p.inc(nc.vector.tensor_tensor(
                    out=outb[fbi][:].rearrange("p c f -> p (c f)"), in0=pt[pbi][:],
                    in1=x_t[fbi][:].rearrange("p c f -> p (c f)"), op=ALU.add), "dve")
                free_pt[pbi] = ("dve", ot[1])
                free_x[fbi] = ("dve", ot[1])
                p.w(nc.sync, ot)
                sd = p.dma(nc.sync, "sy", out=out4[:, tb * 4:(tb + 1) * 4, oc, :],
                                             in_=outb[fbi][:])
                free_out[fbi] = sd
            free_wo[wbi] = ("pe", p.cnt["pe"])
        p.wall(nc.sync, {s: v for s, v in p.snap().items() if s.startswith("sy")})
        st.close()

    bar = consts_ready
    if "ln" in phases or "kv" in phases:
        bar = phase_lnkv(bar)
    if "gateup" in phases:
        bar = phase_gateup(bar)
    if "flash" in phases:
        bar = phase_flash(bar)
    if "down" in phases:
        bar = phase_down(bar)
    if "wo" in phases:
        phase_wo(bar)
    else:
        p.wall(nc.sync, {s: v for s, v in p.snap().items() if s.startswith("sy")})
        p.wall(nc.gpsimd, {s: v for s, v in p.snap().items() if s.startswith("gp")})

    p.es.close()
    return nc


def _host_inputs(inputs):
    import ml_dtypes
    x = np.asarray(inputs["L0_attn_hidden_states"], dtype=np.float32)
    f32 = lambda k: np.ascontiguousarray(np.asarray(inputs[k], dtype=np.float32))
    i32 = lambda k: np.ascontiguousarray(np.asarray(inputs[k], dtype=np.int32))
    wq, wk, wv, wo = f32("L0_attn_Wq"), f32("L0_attn_Wk"), f32("L0_attn_Wv"), f32("L0_attn_Wo")
    gq, gsc = i32("L0_ffn_gate_q"), f32("L0_ffn_gate_scale")
    gA, gB = f32("L0_ffn_gate_A"), f32("L0_ffn_gate_B")
    uq, usc = i32("L0_ffn_up_q"), f32("L0_ffn_up_scale")
    uA, uB = f32("L0_ffn_up_A"), f32("L0_ffn_up_B")
    dq, dsc = i32("L0_ffn_down_q"), f32("L0_ffn_down_scale")
    dA, dB = f32("L0_ffn_down_A"), f32("L0_ffn_down_B")
    gscT = np.ascontiguousarray(gsc.reshape(OC, 128).T)
    uscT = np.ascontiguousarray(usc.reshape(OC, 128).T)
    dscT = np.ascontiguousarray(dsc.reshape(HC, 128).T)
    gqL = np.ascontiguousarray(gq.reshape(OC, 128, HC, 128).transpose(3, 0, 2, 1))
    uqL = np.ascontiguousarray(uq.reshape(OC, 128, HC, 128).transpose(3, 0, 2, 1))
    dqL = np.ascontiguousarray(dq.reshape(HC, 128, OC, 128).transpose(3, 0, 2, 1))
    bf = ml_dtypes.bfloat16
    identb = np.eye(128, dtype=np.float32).astype(bf)
    identf = np.eye(128, dtype=np.float32)
    tri = (np.arange(128)[:, None] <= np.arange(128)[None, :])
    in_maps = []
    for c in range(8):
        b, j = c // 2, c % 2
        perm = [2 * i + j for i in range(NTM)]
        xb = np.ascontiguousarray(x[b])
        xmv = np.ascontiguousarray(xb.reshape(NTF, 128, HID)[perm].reshape(TM, HID))
        masks = []
        for m in range(8):
            blocks = []
            for k in range(4):
                rel = 2 * k + j
                if rel > m:
                    blocks.append(np.ones((128, 128), dtype=np.float32))
                elif rel == m:
                    blocks.append(tri.astype(np.float32))
                else:
                    blocks.append(np.zeros((128, 128), dtype=np.float32))
            masks.append(np.concatenate(blocks, axis=1))
        fm = np.ascontiguousarray(
            np.stack([np.concatenate([masks[2 * mp], masks[2 * mp + 1]], axis=1)
                      for mp in range(4)]).astype(bf))
        in_maps.append({
            "xf": xb, "xm": xmv,
            "wq": wq, "wk": wk, "wv": wv, "wo": wo,
            "gqL": gqL, "gsc": gsc, "gscT": gscT, "gA": gA, "gB": gB,
            "uqL": uqL, "usc": usc, "uscT": uscT, "uA": uA, "uB": uB,
            "dqL": dqL, "dsc": dsc, "dscT": dscT, "dA": dA, "dB": dB,
            "identb": identb, "identf": identf, "fmask": fm,
        })
    return in_maps


def kernel(**inputs):
    in_maps = _host_inputs(inputs)
    nc = build_program(debug_taps=_DEBUG_TAPS, phases=_PHASES)
    res = run_bass_kernel_spmd(nc, in_maps, core_ids=list(range(8)))
    out = np.empty((BATCH, S, HID), dtype=np.float32)
    for c in range(8):
        b, j = c // 2, c % 2
        perm = [2 * i + j for i in range(NTM)]
        out[b].reshape(NTF, 128, HID)[perm] = res.results[c]["out"].reshape(NTM, 128, HID)
    if _DEBUG_TAPS:
        kernel._taps = res.results
    return out


# version 10 (final)
# speedup vs baseline: 1.1631x; 1.0559x over previous
import numpy as np
from contextlib import ExitStack
import concourse.bass as bass
import concourse.mybir as mybir
from concourse.bass_utils import run_bass_kernel_spmd

F32 = mybir.dt.float32
BF16 = mybir.dt.bfloat16
I32 = mybir.dt.int32
AF = mybir.ActivationFunctionType
ALU = mybir.AluOpType
AX = mybir.AxisListType

HID = 4096
NH = 32
S = 4096
BATCH = 4
IM = 11008
ATTN_MULT = 0.08838834764831845
CAP = 30.0
QZ = 7.5
QSTEP = 0.005
LRS = 0.02
EPS = 1e-5
TM = 2048
NTM = TM // 128
NTF = S // 128
OC = IM // 128
HC = HID // 128

_PHASES = ("ln", "kv", "flash", "gateup", "down", "wo")
_DEBUG_TAPS = ()


NLANES = 8


class Prog:

    def __init__(self):
        nc = bass.Bass("TRN2", target_bir_lowering=False, debug=False)
        self.nc = nc
        self.es = ExitStack()
        self.sem = {}
        self.cnt = {}
        for name in ("pe", "dve", "act"):
            self.sem[name] = self.es.enter_context(nc.semaphore(name))
            self.cnt[name] = 0
        self.lanes = {}
        for q in ("sy", "gp"):
            for i in range(NLANES):
                nm = f"{q}{i}"
                self.sem[nm] = self.es.enter_context(nc.semaphore(nm))
                self.cnt[nm] = 0
        self.rr = {"sy": 0, "gp": 0}
        self.lastw = {}

    def inc(self, instr, s, k=1):
        self.cnt[s] += k
        instr.then_inc(self.sem[s], k)
        return (s, self.cnt[s])

    def dma(self, eng, q, **kw):
        lane = f"{q}{self.rr[q] % NLANES}"
        self.rr[q] += 1
        if self.cnt[lane] > 0:
            eng.wait_ge(self.sem[lane], self.cnt[lane])
        self.cnt[lane] += 16
        eng.dma_start(**kw).then_inc(self.sem[lane], 16)
        return (lane, self.cnt[lane])

    def dmat(self, eng, q, out, in_):
        lane = f"{q}{self.rr[q] % NLANES}"
        self.rr[q] += 1
        if self.cnt[lane] > 0:
            eng.wait_ge(self.sem[lane], self.cnt[lane])
        self.cnt[lane] += 16
        eng.dma_start_transpose(out, in_).then_inc(self.sem[lane], 16)
        return (lane, self.cnt[lane])

    def w(self, eng, tok):
        if tok is not None:
            s, v = tok
            if v > 0 and self.lastw.get((id(eng), s), 0) < v:
                eng.wait_ge(self.sem[s], v)
                self.lastw[(id(eng), s)] = v

    def snap(self):
        return dict(self.cnt)

    def wall(self, eng, snap):
        for s, v in snap.items():
            self.w(eng, (s, v))

    def wall_all(self, snap=None):
        snap = snap or self.snap()
        nc = self.nc
        for eng in (nc.sync, nc.gpsimd, nc.tensor, nc.vector, nc.scalar):
            self.wall(eng, snap)

    def sb(self, name, shape, dtype, stack=None):
        return (stack or self.es).enter_context(self.nc.sbuf_tensor(name, shape, dtype))

    def ps(self, name, shape, dtype=F32, stack=None):
        return (stack or self.es).enter_context(self.nc.psum_tensor(name, shape, dtype))


def build_program(debug_taps=(), phases=_PHASES):
    p = Prog()
    nc = p.nc

    def dram(name, shape, dtype, kind="ExternalInput"):
        if name in debug_taps:
            kind = "ExternalOutput"
        return nc.dram_tensor(name, shape, dtype, kind=kind).ap()

    xf = dram("xf", [S, HID], F32)
    xm = dram("xm", [TM, HID], F32)
    wq = dram("wq", [HID, HID], F32)
    wk = dram("wk", [HID, 1024], F32)
    wv = dram("wv", [HID, 1024], F32)
    wo = dram("wo", [HID, HID], F32)
    gqL = dram("gqL", [128, OC, HC, 128], I32)
    gsc = dram("gsc", [IM], F32)
    gscT = dram("gscT", [128, OC], F32)
    gA = dram("gA", [HID, 32], F32)
    gB = dram("gB", [32, IM], F32)
    uqL = dram("uqL", [128, OC, HC, 128], I32)
    usc = dram("usc", [IM], F32)
    uscT = dram("uscT", [128, OC], F32)
    uA = dram("uA", [HID, 32], F32)
    uB = dram("uB", [32, IM], F32)
    dqL = dram("dqL", [128, HC, OC, 128], I32)
    dsc = dram("dsc", [HID], F32)
    dscT = dram("dscT", [128, HC], F32)
    dA = dram("dA", [IM, 32], F32)
    dB = dram("dB", [32, HID], F32)
    identb = dram("identb", [128, 128], BF16)
    identf = dram("identf", [128, 128], F32)
    fmask = dram("fmask", [4, 128, 1024], BF16)
    out = dram("out", [TM, HID], F32, kind="ExternalOutput")

    def scratch(name, shape, dtype):
        kind = "ExternalOutput" if name in debug_taps else "Internal"
        return nc.dram_tensor(name, shape, dtype, kind=kind).ap()

    g_dbg = scratch("g_dbg", [IM, TM], BF16) if "g_dbg" in debug_taps else None
    u_dbg = scratch("u_dbg", [IM, TM], BF16) if "u_dbg" in debug_taps else None
    xnf_s = scratch("xnf_s", [HID, S], BF16)
    xnm_s = scratch("xnm_s", [HID, TM], BF16)
    k_s = scratch("k_s", [1024, S], BF16)
    v_s = scratch("v_s", [S, 1024], BF16)
    o_s = scratch("o_s", [HID, TM], BF16)
    h_s = scratch("h_s", [IM, TM], BF16)
    ffn_s = scratch("ffn_s", [HID, TM], F32)
    bsg_s = scratch("bsg_s", [33, IM], BF16)
    bsu_s = scratch("bsu_s", [33, IM], BF16)
    bsd_s = scratch("bsd_s", [33, HID], BF16)
    q_s = scratch("q_s", [HID, TM], BF16)

    identb_sb = p.sb("identb_sb", [128, 128], BF16)
    ones_col = p.sb("ones_col", [128, 1], BF16)
    ones_r32 = p.sb("ones_r32", [1, 32], F32)
    epsb = p.sb("epsb", [128, 1], F32)
    b30 = p.sb("b30", [128, 1], F32)
    vec33 = p.sb("vec33", [33, 1], F32)
    rhs33_g = p.sb("rhs33_g", [33, TM], BF16)
    rhs33_u = p.sb("rhs33_u", [33, TM], BF16)

    p.dma(nc.sync, "sy", out=identb_sb[:], in_=identb[:])
    nc.vector.memset(ones_col[:], 1.0)
    nc.vector.memset(ones_r32[:], 1.0)
    nc.vector.memset(epsb[:], EPS)
    nc.vector.memset(b30[:], -CAP)
    nc.vector.memset(vec33[0:32, :], 1.0)
    p.inc(nc.vector.memset(vec33[32:33, :], -QZ), "dve")
    consts_ready = p.snap()

    def phase_lnkv(bar):
        st = ExitStack()
        ND = 2
        xbuf = [p.sb(f"p0x{i}", [128, HID], F32, st) for i in range(ND)]
        bst = [p.sb(f"p0b{i}", [128, 8, 6], F32, st) for i in range(ND)]
        bat = [p.sb(f"p0a{i}", [128, 2], F32, st) for i in range(ND)]
        lnv = [p.sb(f"p0l{i}", [128, 1], F32, st) for i in range(ND)]
        rstd = [p.sb(f"p0r{i}", [128, 1], F32, st) for i in range(ND)]
        xn_tm = [p.sb(f"p0n{i}", [128, HID], BF16, st) for i in range(ND)]
        stage = [p.sb(f"p0s{i}", [128, HC, 128], BF16, st) for i in range(ND)]
        pst = [p.ps(f"p0p{i}", [128, 512], F32, st) for i in range(2)]
        wbuf = p.sb("p1w", [128, HC, 1024], BF16, st)
        xnb = [p.sb(f"p1x{i}", [128, HC, 512], BF16, st) for i in range(2)]
        kst = [p.sb(f"p1k{i}", [128, 512], BF16, st) for i in range(2)]
        pk = [p.ps(f"p1p{i}", [128, 512], F32, st) for i in range(2)]

        p.wall_all(bar)
        wtok = p.dma(nc.gpsimd, "gp", out=wbuf[:],
                     in_=wk.rearrange("(c p) d -> p c d", p=128))

        ln_jobs = [("f", i) for i in range(NTF)] + [("m", i) for i in range(NTM)]
        store_tok = {}
        lnstate = {"jj": 0, "free_x": [None] * ND, "free_stage": [None] * ND,
                   "free_pst": [None, None], "free_stats": [None] * ND,
                   "free_xn": [None] * ND}

        def ln_emit_one():
            if not ln_jobs:
                return
            kind, i = ln_jobs.pop(0)
            src_, dst = (xf, xnf_s) if kind == "f" else (xm, xnm_s)
            dst3 = dst.rearrange("(c p) t -> p c t", p=128)
            s = lnstate
            bi = s["jj"] % ND
            p.w(nc.sync, s["free_x"][bi])
            xa = xbuf[bi]
            xrdy = p.dma(nc.sync, "sy", out=xa[:], in_=src_[i * 128:(i + 1) * 128, :])
            p.w(nc.vector, xrdy)
            p.w(nc.vector, s["free_stats"][bi])
            for ch in range(8):
                t1 = p.inc(nc.vector.bn_stats(out=bst[bi][:, ch, :],
                                              in_=xa[:, ch * 512:(ch + 1) * 512]), "dve")
            p.w(nc.vector, t1)
            t2 = p.inc(nc.vector.bn_aggr(out=bat[bi][:],
                                         in_=bst[bi][:].rearrange("p c f -> p (c f)")), "dve")
            p.w(nc.scalar, t2)
            a1 = p.inc(nc.scalar.activation(out=lnv[bi][:], in_=bat[bi][:, 1:2], func=AF.Ln,
                                            bias=epsb[:, 0:1]), "act")
            p.w(nc.scalar, a1)
            a2 = p.inc(nc.scalar.activation(out=rstd[bi][:], in_=lnv[bi][:], func=AF.Exp,
                                            scale=-0.5), "act")
            p.w(nc.vector, a2)
            p.w(nc.vector, s["free_xn"][bi])
            t3 = p.inc(nc.vector.tensor_scalar(out=xn_tm[bi][:], in0=xa[:],
                                               scalar1=bat[bi][:, 0:1],
                                               scalar2=rstd[bi][:, 0:1],
                                               op0=ALU.subtract, op1=ALU.mult), "dve")
            s["free_x"][bi] = ("dve", t3[1])
            s["free_stats"][bi] = ("dve", t3[1])
            p.w(nc.tensor, t3)
            first_ev = True
            for qb in range(8):
                pb = pst[qb % 2]
                p.w(nc.tensor, s["free_pst"][qb % 2])
                for k in range(4):
                    c = qb * 4 + k
                    mm = nc.tensor.matmul(out=pb[:, k * 128:(k + 1) * 128],
                                          lhsT=xn_tm[bi][:, c * 128:(c + 1) * 128],
                                          rhs=identb_sb[:], start=True, stop=True)
                mtok = p.inc(mm, "pe")
                p.w(nc.vector, mtok)
                if first_ev:
                    p.w(nc.vector, s["free_stage"][bi])
                    first_ev = False
                ev = p.inc(nc.vector.tensor_copy(
                    out=stage[bi][:, qb * 4:(qb + 1) * 4, :].rearrange("p c f -> p (c f)"),
                    in_=pb[:]), "dve")
                s["free_pst"][qb % 2] = ("dve", ev[1])
            s["free_xn"][bi] = ("pe", mtok[1])
            p.w(nc.sync, ev)
            sdone = p.dma(nc.sync, "sy", out=dst3[:, :, i * 128:(i + 1) * 128],
                          in_=stage[bi][:])
            s["free_stage"][bi] = sdone
            store_tok[(kind, i)] = sdone
            s["jj"] += 1

        xnf3 = xnf_s.rearrange("(c p) t -> p c t", p=128)
        kvstate = {"free_xn": [None, None], "free_kst": [None] * 2,
                   "free_pk": [None, None], "ks": 0}

        def kv_emit(kind, tb):
            s = kvstate
            bi = tb % 2
            p.w(nc.sync, s["free_xn"][bi])
            for i in range(4 * tb, 4 * tb + 4):
                p.w(nc.sync, store_tok[("f", i)])
            xtok = p.dma(nc.sync, "sy", out=xnb[bi][:],
                         in_=xnf3[:, :, tb * 512:(tb + 1) * 512])
            for dch in range(8):
                pbi = dch % 2
                pb = pk[pbi]
                p.w(nc.tensor, xtok)
                if tb == 0 and dch == 0:
                    p.w(nc.tensor, wtok)
                p.w(nc.tensor, s["free_pk"][pbi])
                if kind == 0:
                    for c in range(HC):
                        mm = nc.tensor.matmul(out=pb[:],
                                              lhsT=wbuf[:, c, dch * 128:(dch + 1) * 128],
                                              rhs=xnb[bi][:, c, :],
                                              start=(c == 0), stop=(c == HC - 1))
                else:
                    tksub, dvb = dch // 2, dch % 2
                    for c in range(HC):
                        mm = nc.tensor.matmul(out=pb[:],
                                              lhsT=xnb[bi][:, c, tksub * 128:(tksub + 1) * 128],
                                              rhs=wbuf[:, c, dvb * 512:(dvb + 1) * 512],
                                              start=(c == 0), stop=(c == HC - 1))
                mtok = p.inc(mm, "pe")
                if dch == 7:
                    s["free_xn"][bi] = ("pe", mtok[1])
                sbi = s["ks"] % 2
                p.w(nc.vector, mtok)
                p.w(nc.vector, s["free_kst"][sbi])
                ev = p.inc(nc.vector.tensor_copy(out=kst[sbi][:], in_=pb[:]), "dve")
                s["free_pk"][pbi] = ("dve", ev[1])
                p.w(nc.sync, ev)
                if kind == 0:
                    dst = k_s[dch * 128:(dch + 1) * 128, tb * 512:(tb + 1) * 512]
                else:
                    row0 = tb * 512 + tksub * 128
                    dst = v_s[row0:row0 + 128, dvb * 512:(dvb + 1) * 512]
                sd = p.dma(nc.sync, "sy", out=dst, in_=kst[sbi][:])
                s["free_kst"][sbi] = sd
                s["ks"] += 1

        for _ in range(8):
            ln_emit_one()
        for tb in range(8):
            for _ in range(5):
                ln_emit_one()
            kv_emit(0, tb)
        p.w(nc.gpsimd, ("pe", p.cnt["pe"]))
        wtok = p.dma(nc.gpsimd, "gp", out=wbuf[:],
                     in_=wv.rearrange("(c p) d -> p c d", p=128))
        kvstate["free_xn"] = [("pe", p.cnt["pe"])] * 2
        for tb in range(8):
            while ln_jobs:
                ln_emit_one()
            p.w(nc.tensor, wtok)
            kv_emit(1, tb)
        bar2 = p.snap()
        st.close()
        return bar2

    def phase_flash(bar):
        st = ExitStack()
        k_sb = p.sb("p3k", [128, S], BF16, st)
        v_sb = p.sb("p3v", [128, NTF, 128], BF16, st)
        q_h = [p.sb(f"p3q{i}", [128, TM], BF16, st) for i in range(2)]
        mask_sb = p.sb("p3m", [128, 4, 1024], BF16, st)
        probs = [p.sb(f"p3pr{i}", [128, 1024], BF16, st) for i in range(2)]
        tmp_t = p.sb("p3tt", [128, 1024], F32, st)
        binv = p.sb("p3bi", [128, 512], F32, st)
        acc = p.sb("p3acc", [128, 1024], F32, st)
        ones_mat = p.sb("ones_mat", [128, 128], F32, st)
        o_tile = [p.sb(f"p3o{i}", [128, 512], BF16, st) for i in range(2)]

        psc = p.ps("p3c", [128, 1024], F32, st)
        pO = p.ps("p3O", [128, 512], F32, st)
        pD = p.ps("p3pd", [128, 512], F32, st)

        scd = p.sb("p5scd", [128, HC], F32, st)
        rhs33_d = p.sb("rhs33_d", [33, TM], BF16, st)
        h_sb = p.sb("p5h", [128, OC, 512], BF16, st)
        wdT = [p.sb(f"p5wdT{i}", [128, 22, 128], BF16, st) for i in range(2)]
        bs_t = [p.sb(f"p5bt{i}", [33, 128], BF16, st) for i in range(2)]
        fsum = p.sb("p5f0", [128, 512], F32, st)
        dnA_sb = p.sb("p5dnA", [128, OC, 33], BF16, st)
        db_tmp = p.sb("p5dbt", [32, HID], F32, st)
        dsrow = [p.sb(f"p5srow{i}", [1, 512], F32, st) for i in range(2)]
        dbs_tmp = [p.sb(f"p5bst{i}", [33, 512], BF16, st) for i in range(2)]
        pP33 = p.ps("p5P33", [33, 512], F32, st)
        pdn = p.ps("p5dn", [128, 512], F32, st)

        p.wall_all(bar)
        p.inc(nc.vector.memset(ones_mat[:], 1.0), "dve")
        msk = p.dma(nc.sync, "sy", out=mask_sb[:],
                                      in_=fmask.rearrange("g p f -> p g f"))

        qlist = [(0, 22), (22, 22), (44, 22), (66, 20)]
        h3 = h_s.rearrange("(c p) t -> p c t", p=128)

        def down_gen():
            t = p.dma(nc.gpsimd, "gp", out=scd[:], in_=dscT[:])
            p.w(nc.vector, t)
            sctok = p.inc(nc.vector.tensor_scalar(out=scd[:], in0=scd[:], scalar1=QSTEP,
                                                  scalar2=1e-30, op0=ALU.mult,
                                                  op1=ALU.max), "dve")
            t = p.dma(nc.gpsimd, "gp", out=dnA_sb[:, :, 0:32],
                      in_=dA.rearrange("(c p) r -> p c r", p=128))
            p.w(nc.vector, t)
            dnatok = p.inc(nc.vector.memset(dnA_sb[:, :, 32:33], 1.0), "dve")
            yield
            tb_ = p.dma(nc.gpsimd, "gp", out=db_tmp[:], in_=dB[:])
            p.inc(nc.vector.memset(dbs_tmp[0][32:33, :], 1.0), "dve")
            p.inc(nc.vector.memset(dbs_tmp[1][32:33, :], 1.0), "dve")
            free_bc = None
            free_sr = [None, None]
            free_bt = [None, None]
            for ch in range(HID // 512):
                cbi = ch % 2
                p.w(nc.gpsimd, free_sr[cbi])
                t = p.dma(nc.gpsimd, "gp", out=dsrow[cbi][:],
                          in_=dsc[None, ch * 512:(ch + 1) * 512])
                p.w(nc.vector, t)
                t1 = p.inc(nc.vector.tensor_scalar(out=dsrow[cbi][:], in0=dsrow[cbi][:],
                                                   scalar1=QSTEP, scalar2=1e-30,
                                                   op0=ALU.mult, op1=ALU.max), "dve")
                p.w(nc.vector, t1)
                t2 = p.inc(nc.vector.reciprocal(out=dsrow[cbi][:], in_=dsrow[cbi][:]), "dve")
                p.w(nc.tensor, t2)
                p.w(nc.tensor, free_bc)
                nc.tensor.matmul(out=pdn[0:32, 0:512], lhsT=ones_r32[:],
                                 rhs=dsrow[cbi][:], start=True, stop=True)
                dmm = nc.tensor.matmul(out=pP33[0:32, 0:16], lhsT=ones_r32[:],
                                       rhs=dsrow[cbi][:, 0:16], start=True, stop=True)
                btok = p.inc(dmm, "pe")
                free_sr[cbi] = ("pe", btok[1])
                p.w(nc.vector, btok)
                p.w(nc.vector, tb_)
                p.w(nc.vector, free_bt[cbi])
                ev = p.inc(nc.vector.scalar_tensor_tensor(
                    out=dbs_tmp[cbi][0:32, :], in0=pdn[0:32, 0:512], scalar=LRS,
                    in1=db_tmp[:, ch * 512:(ch + 1) * 512], op0=ALU.mult, op1=ALU.mult),
                    "dve")
                free_bc = ("dve", ev[1])
                p.w(nc.gpsimd, ev)
                bt = p.dma(nc.gpsimd, "gp", out=bsd_s[:, ch * 512:(ch + 1) * 512],
                           in_=dbs_tmp[cbi][:])
                free_bt[cbi] = bt
                yield
            pdn_free = ("dve", p.cnt["dve"])
            r33evac = None
            prev_mains_done = None
            for qt in range(4):
                if prev_mains_done is not None:
                    p.w(nc.gpsimd, prev_mains_done)
                htoks = []
                for cq in range(4):
                    c0, c1 = cq * 22, min((cq + 1) * 22, OC)
                    t = p.dma(nc.gpsimd, "gp", out=h_sb[:, c0:c1, :],
                              in_=h3[:, c0:c1, qt * 512:(qt + 1) * 512])
                    htoks.append(t)
                yield
                for tt in htoks:
                    p.w(nc.tensor, tt)
                p.w(nc.tensor, dnatok)
                if r33evac is not None:
                    p.w(nc.tensor, r33evac)
                if qt == 0:
                    p.w(nc.tensor, ("dve", p.cnt["dve"]))
                for poc in range(OC):
                    mmp = nc.tensor.matmul(out=pP33[0:33, :],
                                           lhsT=dnA_sb[:, poc, :],
                                           rhs=h_sb[:, poc, :],
                                           start=(poc == 0), stop=(poc == OC - 1))
                    if poc % 8 == 7:
                        yield
                ptok_ = p.inc(mmp, "pe")
                p.w(nc.scalar, ptok_)
                r33evac = p.inc(nc.scalar.mul(rhs33_d[:, qt * 512:(qt + 1) * 512],
                                              pP33[0:33, :], vec33[:, 0:1]), "act")
                yield
                free_wdT = [None, None]
                free_bs = [None, None]
                free_fs = None
                wq_toks = [None, None]
                bs_toks = [None, None]

                def wload(ho, qi, buf):
                    so, sn = qlist[qi]
                    p.w(nc.gpsimd, free_wdT[buf])
                    wq_toks[buf] = p.dma(nc.gpsimd, "gp", out=wdT[buf][:, 0:sn, :],
                                         in_=dqL[:, ho, so:so + sn, :])

                def bsload(ho, buf):
                    p.w(nc.gpsimd, free_bs[buf])
                    bs_toks[buf] = p.dma(nc.gpsimd, "gp", out=bs_t[buf][:],
                                         in_=bsd_s[:, ho * 128:(ho + 1) * 128])

                wload(0, 0, 0)
                bsload(0, 0)
                for ho in range(HC):
                    p.w(nc.tensor, pdn_free)
                    for qi in range(4):
                        buf = qi % 2
                        so, sn = qlist[qi]
                        if qi < 3:
                            wload(ho, qi + 1, 1 - buf)
                        elif ho + 1 < HC:
                            wload(ho + 1, 0, 1 - buf)
                            bsload(ho + 1, 1 - (ho % 2))
                        p.w(nc.tensor, wq_toks[buf])
                        for k in range(sn):
                            oc = so + k
                            mmd = nc.tensor.matmul(out=pdn[:],
                                                   lhsT=wdT[buf][:, k, :],
                                                   rhs=h_sb[:, oc, :],
                                                   start=(oc == 0), stop=False)
                            if k % 8 == 7:
                                yield
                        dtok = p.inc(mmd, "pe")
                        free_wdT[buf] = ("pe", dtok[1])
                        yield
                    bbuf = ho % 2
                    p.w(nc.tensor, bs_toks[bbuf])
                    p.w(nc.tensor, r33evac)
                    mmd = nc.tensor.matmul(out=pdn[:], lhsT=bs_t[bbuf][:],
                                           rhs=rhs33_d[:, qt * 512:(qt + 1) * 512],
                                           start=False, stop=True)
                    dtok = p.inc(mmd, "pe")
                    free_bs[bbuf] = ("pe", dtok[1])
                    p.w(nc.scalar, dtok)
                    if qt == 0 and ho == 0:
                        p.w(nc.scalar, sctok)
                    p.w(nc.scalar, free_fs)
                    ev = p.inc(nc.scalar.mul(fsum[:], pdn[:], scd[:, ho:ho + 1]), "act")
                    p.w(nc.gpsimd, ev)
                    sd = p.dma(nc.gpsimd, "gp",
                               out=ffn_s[ho * 128:(ho + 1) * 128,
                                         qt * 512:(qt + 1) * 512],
                               in_=fsum[:])
                    free_fs = sd
                    pdn_free = ("act", p.cnt["act"])
                    yield
                prev_mains_done = ("pe", p.cnt["pe"])

        down_pump = down_gen()

        def pump(n):
            for _ in range(n):
                if next(down_pump, "END") == "END":
                    break

        free_k = None
        free_v = None
        free_qh = [None, None]
        qh_toks = [None, None]
        free_probs = [None, None]
        free_probs2 = [None, None]
        free_psc = [None]
        free_pO = [None]
        free_otile = [None, None]
        free_pD = [None]
        acc_free = [None]
        last_exp = None
        v4 = v_s.rearrange("(c p) d -> p c d", p=128)
        ktok = vtok = None

        def qh_load(h):
            b = h % 2
            p.w(nc.sync, free_qh[b])
            qh_toks[b] = p.dma(nc.sync, "sy", out=q_h[b][:],
                               in_=q_s[h * 128:(h + 1) * 128, :])

        qh_load(0)
        qh_load(1)
        for h in range(NH):
            kvh = h // 4
            if h % 4 == 0:
                p.w(nc.sync, free_k)
                ktok = p.dma(nc.sync, "sy", out=k_sb[:],
                                               in_=k_s[kvh * 128:(kvh + 1) * 128, :])
                p.w(nc.sync, free_v)
                vtok = p.dma(nc.sync, "sy", out=v_sb[:],
                                               in_=v4[:, :, kvh * 128:(kvh + 1) * 128])
            qbi = h % 2
            q_sb = q_h[qbi]
            qready = qh_toks[qbi]

            units = [(g, u) for g in range(4) for u in range(4 * g + 4)]
            pend_pv = None
            pend_norm = None
            rtoks = {}
            acc_done = {}

            def emit_pv(g, u, pbi, ptok):
                npair = 4 * g + 4
                p.w(nc.tensor, ptok)
                if u == 0:
                    p.w(nc.tensor, free_pO[0])
                    if h % 4 == 0 and g == 0:
                        p.w(nc.tensor, vtok)
                nc.tensor.matmul(out=pO[:], lhsT=v_sb[:, 2 * u, :],
                                 rhs=probs[pbi][:, 0:512], start=(u == 0), stop=False)
                pvm = nc.tensor.matmul(out=pO[:], lhsT=v_sb[:, 2 * u + 1, :],
                                       rhs=probs[pbi][:, 512:1024],
                                       start=False, stop=(u == npair - 1))
                mtok2 = p.inc(pvm, "pe")
                free_probs[pbi] = ("pe", mtok2[1])
                if u == npair - 1:
                    p.w(nc.tensor, acc_done[g])
                    p.w(nc.tensor, free_pD[0])
                    nc.tensor.matmul(out=pD[:], lhsT=ones_mat[:], rhs=acc[:, 0:512],
                                     start=True, stop=False)
                    rmm = nc.tensor.matmul(out=pD[:], lhsT=ones_mat[:],
                                           rhs=acc[:, 512:1024], start=False, stop=True)
                    rtok_pe = p.inc(rmm, "pe")
                    acc_free[0] = ("pe", rtok_pe[1])
                    p.w(nc.vector, rtok_pe)
                    rtoks[g] = p.inc(nc.vector.reciprocal(out=binv[:], in_=pD[:]), "dve")
                    free_pD[0] = ("dve", rtoks[g][1])
                    return True
                return False

            def flush_norm(g):
                rtok = rtoks[g]
                obi = g % 2
                p.w(nc.vector, rtok)
                p.w(nc.vector, free_otile[obi])
                otok = p.inc(nc.vector.tensor_tensor(out=o_tile[obi][:], in0=pO[:],
                                                     in1=binv[:], op=ALU.mult), "dve")
                p.w(nc.sync, otok)
                sd = p.dma(nc.sync, "sy",
                           out=o_s[h * 128:(h + 1) * 128, g * 512:(g + 1) * 512],
                           in_=o_tile[obi][:])
                return ("dve", otok[1]), sd

            for g, u in units:
                pp = psc
                p.w(nc.tensor, free_psc[0])
                if u == 0 and g == 0:
                    p.w(nc.tensor, qready)
                    if h % 4 == 0:
                        p.w(nc.tensor, ktok)
                mm0 = nc.tensor.matmul(out=pp[:, 0:512],
                                       lhsT=k_sb[:, (2 * u) * 128:(2 * u + 1) * 128],
                                       rhs=q_sb[:, g * 512:(g + 1) * 512],
                                       start=True, stop=True)
                mm1 = nc.tensor.matmul(out=pp[:, 512:1024],
                                       lhsT=k_sb[:, (2 * u + 1) * 128:(2 * u + 2) * 128],
                                       rhs=q_sb[:, g * 512:(g + 1) * 512],
                                       start=True, stop=True)
                sctok = p.inc(mm1, "pe")
                if pend_norm is not None and u >= 1:
                    fo, fs = flush_norm(pend_norm)
                    free_pO[0] = fo
                    free_otile[pend_norm % 2] = fs
                    pend_norm = None
                if pend_pv is not None:
                    done = emit_pv(*pend_pv)
                    if done:
                        pend_norm = pend_pv[0]
                    pend_pv = None
                p.w(nc.scalar, sctok)
                if last_exp is not None:
                    p.w(nc.scalar, last_exp)
                t1 = p.inc(nc.scalar.activation(out=tmp_t[:], in_=pp[:], func=AF.Tanh,
                                                scale=ATTN_MULT / CAP), "act")
                free_psc[0] = ("act", t1[1])
                p.w(nc.scalar, t1)
                pbi = u % 2
                p.w(nc.scalar, free_probs[pbi])
                p.w(nc.scalar, free_probs2[pbi])
                t2 = p.inc(nc.scalar.activation(out=probs[pbi][:], in_=tmp_t[:], func=AF.Exp,
                                                scale=CAP, bias=b30[:, 0:1]), "act")
                last_exp = t2
                ptok = t2
                if u >= 4 * g:
                    mp = u - 4 * g
                    p.w(nc.vector, t2)
                    if u == 0 and g == 0 and h == 0:
                        p.w(nc.vector, msk)
                    ptok = p.inc(nc.vector.tensor_tensor(out=probs[pbi][:], in0=probs[pbi][:],
                                                         in1=mask_sb[:, mp, :],
                                                         op=ALU.mult), "dve")
                p.w(nc.vector, ptok)
                if u == 0:
                    p.w(nc.vector, acc_free[0])
                    at = p.inc(nc.vector.tensor_copy(out=acc[:], in_=probs[pbi][:]), "dve")
                else:
                    at = p.inc(nc.vector.tensor_tensor(out=acc[:], in0=acc[:],
                                                       in1=probs[pbi][:], op=ALU.add), "dve")
                free_probs2[pbi] = ("dve", at[1])
                if u == 4 * g + 3:
                    acc_done[g] = ("dve", at[1])
                pend_pv = (g, u, pbi, ptok)
                pump(2)
            done = emit_pv(*pend_pv)
            pend_pv = None
            if pend_norm is not None:
                fo, fs = flush_norm(pend_norm)
                free_pO[0] = fo
                free_otile[pend_norm % 2] = fs
            fo, fs = flush_norm(3)
            free_pO[0] = fo
            free_otile[3 % 2] = fs
            free_qh[qbi] = ("pe", p.cnt["pe"])
            if h + 2 < NH:
                qh_load(h + 2)
            if h % 4 == 3:
                free_k = ("pe", p.cnt["pe"])
                free_v = ("pe", p.cnt["pe"])
        pump(1 << 30)
        bar2 = p.snap()
        st.close()
        return bar2

    def phase_gateup(bar):
        st = ExitStack()
        scg = p.sb("p4scg", [128, OC], F32, st)
        scu = p.sb("p4scu", [128, OC], F32, st)
        xnm_h = p.sb("p4xn", [128, HC, TM], BF16, st)

        p.wall_all(bar)
        xnm3 = xnm_s.rearrange("(c p) t -> p c t", p=128)
        xtok = p.dma(nc.gpsimd, "gp", out=xnm_h[:], in_=xnm3)
        t = p.dma(nc.sync, "sy", out=scg[:], in_=gscT[:])
        t2_ = p.dma(nc.sync, "sy", out=scu[:], in_=uscT[:])
        p.w(nc.vector, t2_)
        t1 = p.inc(nc.vector.tensor_scalar(out=scg[:], in0=scg[:], scalar1=QSTEP, scalar2=1e-30,
                                           op0=ALU.mult, op1=ALU.max), "dve")
        p.w(nc.vector, t1)
        t1 = p.inc(nc.vector.tensor_scalar(out=scu[:], in0=scu[:], scalar1=QSTEP, scalar2=1e-30,
                                           op0=ALU.mult, op1=ALU.max), "dve")
        scready = t1

        with ExitStack() as st2:
            gb_tmp = p.sb("p4gbt", [32, IM], F32, st2)
            srow = [p.sb(f"p4srow{i}", [1, 512], F32, st2) for i in range(2)]
            bs_tmp = [p.sb(f"p4bst{i}", [33, 512], BF16, st2) for i in range(2)]
            pbc = p.ps("p4bc", [32, 512], F32, st2)
            pdum = p.ps("p4dum", [32, 16], F32, st2)
            for mi, (scsrc, Bsrc, Bdst_s) in enumerate(((gsc, gB, bsg_s), (usc, uB, bsu_s))):
                if mi > 0:
                    p.wall_all()
                tb_ = p.dma(nc.sync, "sy", out=gb_tmp[:], in_=Bsrc[:])
                p.inc(nc.vector.memset(bs_tmp[0][32:33, :], 1.0), "dve")
                p.inc(nc.vector.memset(bs_tmp[1][32:33, :], 1.0), "dve")
                free_bc = None
                free_sr = [None, None]
                free_bt = [None, None]
                for ch in range((IM + 511) // 512):
                    c0, c1 = ch * 512, min((ch + 1) * 512, IM)
                    n = c1 - c0
                    cbi = ch % 2
                    p.w(nc.sync, free_sr[cbi])
                    t = p.dma(nc.sync, "sy", out=srow[cbi][:, 0:n],
                                                in_=scsrc[None, c0:c1])
                    p.w(nc.vector, t)
                    t1 = p.inc(nc.vector.tensor_scalar(out=srow[cbi][:, 0:n],
                                                       in0=srow[cbi][:, 0:n],
                                                       scalar1=QSTEP, scalar2=1e-30,
                                                       op0=ALU.mult, op1=ALU.max), "dve")
                    p.w(nc.vector, t1)
                    t2 = p.inc(nc.vector.reciprocal(out=srow[cbi][:, 0:n],
                                                    in_=srow[cbi][:, 0:n]), "dve")
                    p.w(nc.tensor, t2)
                    p.w(nc.tensor, free_bc)
                    nc.tensor.matmul(out=pbc[:, 0:n], lhsT=ones_r32[:],
                                     rhs=srow[cbi][:, 0:n], start=True, stop=True)
                    dmm = nc.tensor.matmul(out=pdum[:], lhsT=ones_r32[:],
                                           rhs=srow[cbi][:, 0:16], start=True, stop=True)
                    btok = p.inc(dmm, "pe")
                    free_sr[cbi] = ("pe", btok[1])
                    p.w(nc.vector, btok)
                    p.w(nc.vector, tb_)
                    p.w(nc.vector, free_bt[cbi])
                    ev = p.inc(nc.vector.scalar_tensor_tensor(
                        out=bs_tmp[cbi][0:32, 0:n], in0=pbc[:, 0:n], scalar=LRS,
                        in1=gb_tmp[:, c0:c1], op0=ALU.mult, op1=ALU.mult), "dve")
                    free_bc = ("dve", ev[1])
                    p.w(nc.sync, ev)
                    bt = p.dma(nc.sync, "sy", out=Bdst_s[:, c0:c1], in_=bs_tmp[cbi][:, 0:n])
                    free_bt[cbi] = bt
            p.wall_all()

        wgT = [p.sb(f"p4wgT{i}", [128, HC, 128], BF16, st) for i in range(2)]
        wuT = [p.sb(f"p4wuT{i}", [128, HC, 128], BF16, st) for i in range(2)]
        bsg_t = [p.sb(f"p4bsgt{i}", [33, 128], BF16, st) for i in range(2)]
        bsu_t = [p.sb(f"p4bsut{i}", [33, 128], BF16, st) for i in range(2)]
        gs_t = [p.sb(f"p4g{i}", [128, 512], BF16, st) for i in range(4)]
        us_t = [p.sb(f"p4u{i}", [128, 512], BF16, st) for i in range(4)]
        hstage = [p.sb(f"p4h{i}", [128, 4, 512], BF16, st) for i in range(2)]
        wqb = [p.sb(f"p4wq{i}", [128, HC, 128], BF16, st) for i in range(2)]
        q_stage = [p.sb(f"p4qs{i}", [128, 512], BF16, st) for i in range(2)]
        pg = p.ps("p4pg", [128, 2048], F32, st)
        pu = p.ps("p4pu", [128, 2048], F32, st)

        last_ev = None
        with ExitStack() as st3:
            agb = p.sb("p4ag", [128, HC, 33], BF16, st3)
            for mi, (Asrc, r33) in enumerate(((gA, rhs33_g), (uA, rhs33_u))):
                if mi > 0:
                    p.w(nc.gpsimd, ("pe", p.cnt["pe"]))
                atok = p.dma(nc.gpsimd, "gp", out=agb[:, :, 0:32],
                                             in_=Asrc.rearrange("(c p) r -> p c r", p=128))
                p.w(nc.vector, atok)
                otok = p.inc(nc.vector.memset(agb[:, :, 32:33], 1.0), "dve")
                for tb in range(4):
                    bo = (tb % 2) * 512
                    p.w(nc.tensor, xtok)
                    p.w(nc.tensor, otok)
                    if last_ev is not None:
                        p.w(nc.tensor, last_ev)
                    for c in range(HC):
                        mm = nc.tensor.matmul(out=pg[0:33, bo:bo + 512],
                                              lhsT=agb[:, c, :],
                                              rhs=xnm_h[:, c, tb * 512:(tb + 1) * 512],
                                              start=(c == 0), stop=(c == HC - 1))
                    mtok = p.inc(mm, "pe")
                    p.w(nc.scalar, mtok)
                    last_ev = p.inc(nc.scalar.mul(r33[:, tb * 512:(tb + 1) * 512],
                                                  pg[0:33, bo:bo + 512], vec33[:, 0:1]), "act")
            p.wall_all()
        pgu_done = last_ev

        wq4 = wq.rearrange("(c p) o -> p c o", p=128)
        q_jobs = [(h, tb) for h in range(NH) for tb in range(4)]
        qst = {"free_wqb": [None, None], "wq_toks": [None, None],
               "free_qs": [None, None], "qdone": 0, "q_pe": [None] * 4}

        def qload(h):
            b = h % 2
            p.w(nc.gpsimd, qst["free_wqb"][b])
            qst["wq_toks"][b] = p.dma(nc.gpsimd, "gp", out=wqb[b][:],
                                      in_=wq4[:, :, h * 128:(h + 1) * 128])

        qload(0)
        qload(1)

        def q_emit_one(gate_pend):
            if not q_jobs:
                return
            h, tb = q_jobs.pop(0)
            b = h % 2
            sbq = qst["qdone"] % 2
            bo = tb * 512
            p.w(nc.tensor, qst["wq_toks"][b])
            p.w(nc.tensor, gate_pend)
            p.w(nc.tensor, qst["q_pe"][tb])
            for c in range(HC):
                mm = nc.tensor.matmul(out=pg[:, bo:bo + 512], lhsT=wqb[b][:, c, :],
                                      rhs=xnm_h[:, c, bo:bo + 512],
                                      start=(c == 0), stop=(c == HC - 1))
            mtok = p.inc(mm, "pe")
            if tb == 3:
                qst["free_wqb"][b] = ("pe", mtok[1])
                if h + 2 < NH:
                    qload(h + 2)
            p.w(nc.vector, mtok)
            p.w(nc.vector, qst["free_qs"][sbq])
            ev = p.inc(nc.vector.tensor_copy(out=q_stage[sbq][:], in_=pg[:, bo:bo + 512]),
                       "dve")
            qst["q_pe"][tb] = ("dve", ev[1])
            p.w(nc.sync, ev)
            sd = p.dma(nc.sync, "sy", out=q_s[h * 128:(h + 1) * 128, bo:bo + 512],
                       in_=q_stage[sbq][:])
            qst["free_qs"][sbq] = sd
            qst["qdone"] += 1

        wg_toks = [None, None]
        wu_toks = [None, None]
        bg_toks = [None, None]
        bu_toks = [None, None]
        free_wg = [None, None]
        free_wu = [None, None]
        free_bg = [None, None]
        free_bu = [None, None]
        free_pg = None
        free_pu = None
        free_gs = [None] * 4
        free_us = [None] * 4
        free_h = [None, None]

        def tload(oc):
            b = oc % 2
            p.w(nc.gpsimd, free_wg[b])
            wg_toks[b] = p.dma(nc.gpsimd, "gp", out=wgT[b][:], in_=gqL[:, oc, :, :])
            p.w(nc.gpsimd, free_wu[b])
            wu_toks[b] = p.dma(nc.gpsimd, "gp", out=wuT[b][:], in_=uqL[:, oc, :, :])
            p.w(nc.sync, free_bg[b])
            bg_toks[b] = p.dma(nc.sync, "sy", out=bsg_t[b][:],
                               in_=bsg_s[:, oc * 128:(oc + 1) * 128])
            p.w(nc.sync, free_bu[b])
            bu_toks[b] = p.dma(nc.sync, "sy", out=bsu_t[b][:],
                               in_=bsu_s[:, oc * 128:(oc + 1) * 128])

        tload(0)
        tload(1)
        for oc in range(OC):
            b = oc % 2
            p.w(nc.tensor, wg_toks[b])
            p.w(nc.tensor, bg_toks[b])
            p.w(nc.tensor, free_pg)
            for qtok in qst["q_pe"]:
                p.w(nc.tensor, qtok)
            if oc == 0:
                p.w(nc.tensor, pgu_done)
            for tb in range(4):
                nc.tensor.matmul(out=pg[:, tb * 512:(tb + 1) * 512], lhsT=bsg_t[b][:],
                                 rhs=rhs33_g[:, tb * 512:(tb + 1) * 512],
                                 start=True, stop=False)
            for c in range(HC):
                for tb in range(4):
                    mmg = nc.tensor.matmul(out=pg[:, tb * 512:(tb + 1) * 512],
                                           lhsT=wgT[b][:, c, :],
                                           rhs=xnm_h[:, c, tb * 512:(tb + 1) * 512],
                                           start=False, stop=(c == HC - 1))
            gtok = p.inc(mmg, "pe")
            free_wg[b] = ("pe", gtok[1])
            free_bg[b] = ("pe", gtok[1])
            p.w(nc.scalar, gtok)
            if oc == 0:
                p.w(nc.scalar, scready)
            for tb in range(4):
                p.w(nc.scalar, free_gs[tb])
                ge = p.inc(nc.scalar.activation(out=gs_t[tb][:],
                                                in_=pg[:, tb * 512:(tb + 1) * 512],
                                                func=AF.Silu, scale=scg[:, oc:oc + 1]), "act")
            free_pg = ("act", ge[1])
            p.w(nc.tensor, wu_toks[b])
            p.w(nc.tensor, bu_toks[b])
            p.w(nc.tensor, free_pu)
            for tb in range(4):
                nc.tensor.matmul(out=pu[:, tb * 512:(tb + 1) * 512], lhsT=bsu_t[b][:],
                                 rhs=rhs33_u[:, tb * 512:(tb + 1) * 512],
                                 start=True, stop=False)
            for c in range(HC):
                for tb in range(4):
                    mmu = nc.tensor.matmul(out=pu[:, tb * 512:(tb + 1) * 512],
                                           lhsT=wuT[b][:, c, :],
                                           rhs=xnm_h[:, c, tb * 512:(tb + 1) * 512],
                                           start=False, stop=(c == HC - 1))
            utok = p.inc(mmu, "pe")
            free_wu[b] = ("pe", utok[1])
            free_bu[b] = ("pe", utok[1])
            if oc + 2 < OC:
                tload(oc + 2)
            p.w(nc.scalar, utok)
            hb = oc % 2
            htok = None
            for tb in range(4):
                p.w(nc.scalar, free_us[tb])
                ue = p.inc(nc.scalar.mul(us_t[tb][:], pu[:, tb * 512:(tb + 1) * 512],
                                         scu[:, oc:oc + 1]), "act")
                p.w(nc.vector, ue)
                p.w(nc.vector, ("act", ge[1]))
                if tb == 0:
                    p.w(nc.vector, free_h[hb])
                htok = p.inc(nc.vector.tensor_tensor(out=hstage[hb][:, tb, :],
                                                     in0=gs_t[tb][:], in1=us_t[tb][:],
                                                     op=ALU.mult), "dve")
                free_gs[tb] = ("dve", htok[1])
                free_us[tb] = ("dve", htok[1])
            free_pu = ("act", p.cnt["act"])
            p.w(nc.sync, htok)
            sd = p.dma(nc.sync, "sy", out=h_s[oc * 128:(oc + 1) * 128, :],
                       in_=hstage[hb][:].rearrange("p c f -> p (c f)"))
            free_h[hb] = sd
            while qst["qdone"] < (oc + 1) * (4 * NH) // OC and q_jobs:
                q_emit_one(free_pg)
        while q_jobs:
            q_emit_one(free_pg)
        bar2 = p.snap()
        st.close()
        return bar2

    def phase_wo(bar):
        st = ExitStack()
        identf_sb = p.sb("identf_sb", [128, 128], F32, st)
        o_sb = p.sb("p6o", [128, HC, TM], BF16, st)
        wob = [p.sb(f"p6w{i}", [128, HC, 128], BF16, st) for i in range(2)]
        ffn_t = [p.sb(f"p6f{i}", [128, 512], F32, st) for i in range(2)]
        x_t = [p.sb(f"p6x{i}", [128, 4, 128], F32, st) for i in range(2)]
        s1 = [p.sb(f"p6s{i}", [128, 512], F32, st) for i in range(2)]
        outb = [p.sb(f"p6ob{i}", [128, 4, 128], F32, st) for i in range(2)]
        pa = [p.ps(f"p6a{i}", [128, 512], F32, st) for i in range(2)]
        pt = [p.ps(f"p6t{i}", [128, 512], F32, st) for i in range(2)]

        p.wall_all(bar)
        iftok = p.dma(nc.sync, "sy", out=identf_sb[:], in_=identf[:])
        o4 = o_s.rearrange("(c p) t -> p c t", p=128)
        wo4 = wo.rearrange("(c p) o -> p c o", p=128)
        xm4 = xm.rearrange("(tt p) (c f) -> p tt c f", p=128, f=128)
        out4 = out.rearrange("(tt p) (c f) -> p tt c f", p=128, f=128)
        otoks = []
        for cq in range(4):
            t = p.dma(nc.sync, "sy", out=o_sb[:, cq * 8:(cq + 1) * 8, :],
                                        in_=o4[:, cq * 8:(cq + 1) * 8, :])
            otoks.append(t)
        free_wo = [None, None]
        free_ffn = [None, None]
        free_x = [None, None]
        free_s1 = [None, None]
        free_out = [None, None]
        free_pa = [None, None]
        free_pt = [None, None]
        for oc in range(HC):
            wbi = oc % 2
            p.w(nc.gpsimd, free_wo[wbi])
            wt = p.dma(nc.gpsimd, "gp", out=wob[wbi][:],
                                           in_=wo4[:, :, oc * 128:(oc + 1) * 128])
            for tb in range(4):
                pbi = tb % 2
                fbi = (oc * 4 + tb) % 2
                p.w(nc.sync, free_ffn[fbi])
                ftok = p.dma(nc.sync, "sy", 
                    out=ffn_t[fbi][:],
                    in_=ffn_s[oc * 128:(oc + 1) * 128, tb * 512:(tb + 1) * 512])
                p.w(nc.sync, free_x[fbi])
                xtok = p.dma(nc.sync, "sy", out=x_t[fbi][:],
                                               in_=xm4[:, tb * 4:(tb + 1) * 4, oc, :])
                p.w(nc.tensor, wt)
                if oc == 0 and tb == 0:
                    for tt in otoks:
                        p.w(nc.tensor, tt)
                p.w(nc.tensor, free_pa[pbi])
                for c in range(HC):
                    mm = nc.tensor.matmul(out=pa[pbi][:], lhsT=wob[wbi][:, c, :],
                                          rhs=o_sb[:, c, tb * 512:(tb + 1) * 512],
                                          start=(c == 0), stop=(c == HC - 1))
                atok = p.inc(mm, "pe")
                sbi = (oc * 4 + tb) % 2
                p.w(nc.vector, atok)
                p.w(nc.vector, ftok)
                p.w(nc.vector, free_s1[sbi])
                stok = p.inc(nc.vector.tensor_tensor(out=s1[sbi][:], in0=pa[pbi][:],
                                                     in1=ffn_t[fbi][:], op=ALU.add), "dve")
                free_pa[pbi] = ("dve", stok[1])
                free_ffn[fbi] = ("dve", stok[1])
                p.w(nc.tensor, stok)
                p.w(nc.tensor, iftok)
                p.w(nc.tensor, free_pt[pbi])
                for k in range(4):
                    mm = nc.tensor.matmul(out=pt[pbi][:, k * 128:(k + 1) * 128],
                                          lhsT=s1[sbi][:, k * 128:(k + 1) * 128],
                                          rhs=identf_sb[:], start=True, stop=True)
                ttok = p.inc(mm, "pe")
                free_s1[sbi] = ("pe", ttok[1])
                p.w(nc.vector, ttok)
                p.w(nc.vector, xtok)
                p.w(nc.vector, free_out[fbi])
                ot = p.inc(nc.vector.tensor_tensor(
                    out=outb[fbi][:].rearrange("p c f -> p (c f)"), in0=pt[pbi][:],
                    in1=x_t[fbi][:].rearrange("p c f -> p (c f)"), op=ALU.add), "dve")
                free_pt[pbi] = ("dve", ot[1])
                free_x[fbi] = ("dve", ot[1])
                p.w(nc.sync, ot)
                sd = p.dma(nc.sync, "sy", out=out4[:, tb * 4:(tb + 1) * 4, oc, :],
                                             in_=outb[fbi][:])
                free_out[fbi] = sd
            free_wo[wbi] = ("pe", p.cnt["pe"])
        p.wall(nc.sync, {s: v for s, v in p.snap().items() if s.startswith("sy")})
        st.close()

    bar = consts_ready
    if "ln" in phases or "kv" in phases:
        bar = phase_lnkv(bar)
    if "gateup" in phases:
        bar = phase_gateup(bar)
    if "flash" in phases:
        bar = phase_flash(bar)
    if "wo" in phases:
        phase_wo(bar)
    else:
        p.wall(nc.sync, {s: v for s, v in p.snap().items() if s.startswith("sy")})
        p.wall(nc.gpsimd, {s: v for s, v in p.snap().items() if s.startswith("gp")})

    p.es.close()
    return nc


def _host_inputs(inputs):
    import ml_dtypes
    x = np.asarray(inputs["L0_attn_hidden_states"], dtype=np.float32)
    f32 = lambda k: np.ascontiguousarray(np.asarray(inputs[k], dtype=np.float32))
    i32 = lambda k: np.ascontiguousarray(np.asarray(inputs[k], dtype=np.int32))
    wq, wk, wv, wo = f32("L0_attn_Wq"), f32("L0_attn_Wk"), f32("L0_attn_Wv"), f32("L0_attn_Wo")
    gq, gsc = i32("L0_ffn_gate_q"), f32("L0_ffn_gate_scale")
    gA, gB = f32("L0_ffn_gate_A"), f32("L0_ffn_gate_B")
    uq, usc = i32("L0_ffn_up_q"), f32("L0_ffn_up_scale")
    uA, uB = f32("L0_ffn_up_A"), f32("L0_ffn_up_B")
    dq, dsc = i32("L0_ffn_down_q"), f32("L0_ffn_down_scale")
    dA, dB = f32("L0_ffn_down_A"), f32("L0_ffn_down_B")
    gscT = np.ascontiguousarray(gsc.reshape(OC, 128).T)
    uscT = np.ascontiguousarray(usc.reshape(OC, 128).T)
    dscT = np.ascontiguousarray(dsc.reshape(HC, 128).T)
    gqL = np.ascontiguousarray(gq.reshape(OC, 128, HC, 128).transpose(3, 0, 2, 1))
    uqL = np.ascontiguousarray(uq.reshape(OC, 128, HC, 128).transpose(3, 0, 2, 1))
    dqL = np.ascontiguousarray(dq.reshape(HC, 128, OC, 128).transpose(3, 0, 2, 1))
    bf = ml_dtypes.bfloat16
    identb = np.eye(128, dtype=np.float32).astype(bf)
    identf = np.eye(128, dtype=np.float32)
    tri = (np.arange(128)[:, None] <= np.arange(128)[None, :])
    in_maps = []
    for c in range(8):
        b, j = c // 2, c % 2
        perm = [2 * i + j for i in range(NTM)]
        xb = np.ascontiguousarray(x[b])
        xmv = np.ascontiguousarray(xb.reshape(NTF, 128, HID)[perm].reshape(TM, HID))
        masks = []
        for m in range(8):
            blocks = []
            for k in range(4):
                rel = 2 * k + j
                if rel > m:
                    blocks.append(np.ones((128, 128), dtype=np.float32))
                elif rel == m:
                    blocks.append(tri.astype(np.float32))
                else:
                    blocks.append(np.zeros((128, 128), dtype=np.float32))
            masks.append(np.concatenate(blocks, axis=1))
        fm = np.ascontiguousarray(
            np.stack([np.concatenate([masks[2 * mp], masks[2 * mp + 1]], axis=1)
                      for mp in range(4)]).astype(bf))
        in_maps.append({
            "xf": xb, "xm": xmv,
            "wq": wq, "wk": wk, "wv": wv, "wo": wo,
            "gqL": gqL, "gsc": gsc, "gscT": gscT, "gA": gA, "gB": gB,
            "uqL": uqL, "usc": usc, "uscT": uscT, "uA": uA, "uB": uB,
            "dqL": dqL, "dsc": dsc, "dscT": dscT, "dA": dA, "dB": dB,
            "identb": identb, "identf": identf, "fmask": fm,
        })
    return in_maps


def kernel(**inputs):
    in_maps = _host_inputs(inputs)
    nc = build_program(debug_taps=_DEBUG_TAPS, phases=_PHASES)
    res = run_bass_kernel_spmd(nc, in_maps, core_ids=list(range(8)))
    out = np.empty((BATCH, S, HID), dtype=np.float32)
    for c in range(8):
        b, j = c // 2, c % 2
        perm = [2 * i + j for i in range(NTM)]
        out[b].reshape(NTF, 128, HID)[perm] = res.results[c]["out"].reshape(NTM, 128, HID)
    if _DEBUG_TAPS:
        kernel._taps = res.results
    return out
